# version 29
# baseline (speedup 1.0000x reference)
import numpy as np
import ml_dtypes
import concourse.bass as bass
import concourse.mybir as mybir
from concourse.bass_utils import run_bass_kernel_spmd

F32 = mybir.dt.float32
BF16 = mybir.dt.bfloat16
AX = mybir.AxisListType.X
ALU = mybir.AluOpType
AF = mybir.ActivationFunctionType

NT = 32
GRP = 4
D = 1024
DK = 128
INNER = 4096
HC = 8
OC = 32
SM_SCALE = 128.0 ** -0.5
C32 = 12582912.0
C52MC32 = 6755399428472832.0
CLIP_B = 1.49
EPS = 1e-5
INV_D = 1.0 / 1024.0
INV_WN = 1.0 / float(INNER * D)
DEBUG = False


class Sched:
    def __init__(self):
        self.st = {k: [] for k in "tvsgq"}
        self.cnt = dict(T=0, V=0, S=0, G=0, DQC=0, DGA=0, DGX0=0, DGX1=0,
                        DQW0=0, DQW1=0, DSW=0, DSX0=0, DSX1=0, DSN=0,
                        DSY0=0, DSY1=0, DQN0=0, DQN1=0, DQX0=0, DQX1=0, DQWD=0,
                        DSO=0, DQO0=0, DQO1=0, DGX2=0, DGX3=0, DSX2=0, DSX3=0, DQW2=0, DQW3=0,
                        DGW0=0, DGW1=0, DPX0=0, DPX1=0, DPX2=0)

    def op(self, e, fn, waits=(), inc=None, n=1):
        w = [(s, int(v)) for (s, v) in waits if v and v > 0]
        self.st[e].append((fn, w, inc, n))
        if inc is None:
            return None
        self.cnt[inc] += n
        return self.cnt[inc]


def _build():
    nc = bass.Bass("TRN2", target_bir_lowering=False)
    scratch_kind = "ExternalOutput" if DEBUG else "Internal"

    x_d = nc.dram_tensor("x", [4096, D], F32, kind="ExternalInput").ap()
    wq_d = nc.dram_tensor("wq", [D, DK], F32, kind="ExternalInput").ap()
    wk_d = nc.dram_tensor("wk", [D, DK], F32, kind="ExternalInput").ap()
    wv_d = nc.dram_tensor("wv", [D, DK], F32, kind="ExternalInput").ap()
    wo_d = nc.dram_tensor("wo", [DK, D], F32, kind="ExternalInput").ap()
    wgt_d = nc.dram_tensor("wgt", [D, INNER], F32, kind="ExternalInput").ap()
    wut_d = nc.dram_tensor("wut", [D, INNER], F32, kind="ExternalInput").ap()
    wdt_d = nc.dram_tensor("wdt", [INNER, D], F32, kind="ExternalInput").ap()
    ident_d = nc.dram_tensor("ident", [128, 128], BF16, kind="ExternalInput").ap()
    mask_d = nc.dram_tensor("mask", [128, 128], BF16, kind="ExternalInput").ap()
    y_d = nc.dram_tensor("y", [4096, D], F32, kind="ExternalOutput").ap()
    x1h_d = nc.dram_tensor("x1h", [4096, D], F32, kind=scratch_kind).ap()
    nqh_d = nc.dram_tensor("nqh", [NT, 128, INNER], BF16, kind=scratch_kind).ap()
    wdqh_d = nc.dram_tensor("wdqh", [OC, 128, D], BF16, kind="Internal").ap()
    oh_d = nc.dram_tensor("oh", [NT // GRP, 128, GRP * 128], BF16, kind="Internal").ap()

    sc = Sched()
    import contextlib
    stack = contextlib.ExitStack()

    def sb(name, shape, dt):
        return stack.enter_context(nc.sbuf_tensor(name, shape, dt)).ap()

    qres = sb("qres", [128, NT * 128], BF16)
    kres = sb("kres", [128, NT * 128], BF16)
    vres = sb("vres", [128, NT * 129], BF16)
    wo_sb = sb("wo_sb", [128, D], BF16)
    ident = sb("ident_sb", [128, 128], BF16)
    mask = sb("mask_sb", [128, 128], BF16)
    ones_c = sb("ones_c", [128, 1], F32)
    ones_r = sb("ones_r", [1, 128], F32)
    x2_st = sb("x2_st", [128, 2 * D], F32)
    x_st = [x2_st[:, 0:D], x2_st[:, D:2 * D]]
    x3 = None
    ln_scr = sb("ln_scr", [128, D], F32)
    nqh_sb = sb("nqh_sb", [128, INNER], BF16)
    nqt_sb = sb("nqt_sb", [128, INNER], BF16)
    st = sb("st_cols", [128, 32], F32)
    c2_all = sb("c2_all", [128, NT], F32)
    wsums = sb("wsums", [128, 96], F32)
    kn = sb("konst", [128, 8], F32)
    m_all = sb("m_all", [128, NT], F32)
    var_all = sb("var_all", [128, NT], F32)
    rstd_all = sb("rstd_all", [128, NT], F32)
    bm_all = sb("bm_all", [128, NT], F32)
    wsc = sb("wscales", [128, 8], F32)
    tot_sb = sb("tot_sb", [1, 4], F32)

    pi_c = kn[:, 0:1]; cb_c = kn[:, 1:2]; ng_c = kn[:, 2:3]
    eps_c = kn[:, 3:4]; smsc_c = kn[:, 4:5]; c127_c = kn[:, 5:6]
    s1_c = st[:, 0:1]; sq_c = st[:, 1:2]; m_c = st[:, 2:3]; m2_c = st[:, 3:4]
    var_c = st[:, 4:5]; sd_c = st[:, 5:6]; rstd_c = st[:, 6:7]; bm_c = st[:, 7:8]
    mx_c = st[:, 8:9]; mxc_c = st[:, 9:10]; r_c = st[:, 10:11]; s1q_c = st[:, 11:12]
    c1g_p = [st[:, 12:13], st[:, 20:21]]
    c1u_p = [st[:, 13:14], st[:, 21:22]]
    mh4 = st[:, 24:28]
    mh_c = st[:, 15:16]; mhc_c = st[:, 16:17]; rh_c = st[:, 17:18]; s2q_c = st[:, 18:19]

    ln1_sb = nqh_sb[:, 0:D]
    ln1T_sb = nqt_sb[:, 0:D]

    s1ctx = contextlib.ExitStack()
    wgq = s1ctx.enter_context(nc.sbuf_tensor("wgq", [128, HC * INNER], BF16)).ap()
    wuq = s1ctx.enter_context(nc.sbuf_tensor("wuq", [128, HC * INNER], BF16)).ap()

    s2ctx = contextlib.ExitStack()
    wst2 = s2ctx.enter_context(nc.sbuf_tensor("wstage0", [128, 2048], F32)).ap()
    x3 = [x2_st[:, 0:D], x2_st[:, D:2 * D], wst2[:, 0:D]]
    wdq_st = s2ctx.enter_context(nc.sbuf_tensor("wdq_st", [128, 1024], BF16)).ap()
    wq_sb = s2ctx.enter_context(nc.sbuf_tensor("wq_sb", [128, D], BF16)).ap()
    wk_sb = s2ctx.enter_context(nc.sbuf_tensor("wk_sb", [128, D], BF16)).ap()
    wv_sb = s2ctx.enter_context(nc.sbuf_tensor("wv_sb", [128, D], BF16)).ap()
    pt_row = [s2ctx.enter_context(nc.sbuf_tensor(f"pt{i}", [128, 1024], BF16)).ap() for i in range(3)]
    o_sb = pt_row[2][:, 0:GRP * 128]
    ot_sb = s2ctx.enter_context(nc.sbuf_tensor("ot_sb", [128, 512], BF16)).ap()
    rq_sb = s2ctx.enter_context(nc.sbuf_tensor("rq_sb", [128, GRP], F32)).ap()

    pb = [stack.enter_context(nc.psum_tensor(f"pb{i}", [128, 1024], F32)).ap() for i in range(4)]

    sc.op('q', lambda e: e.dma_start(out=ident, in_=ident_d), inc='DQC', n=16)
    sc.op('q', lambda e: e.dma_start(out=mask, in_=mask_d), inc='DQC', n=16)
    dq_const = ('DQC', sc.cnt['DQC'])
    sc.op('v', lambda e: e.memset(ones_c, 1.0))
    sc.op('v', lambda e: e.memset(ones_r, 1.0))
    sc.op('v', lambda e: e.memset(eps_c, EPS))
    sc.op('v', lambda e: e.memset(smsc_c, SM_SCALE))
    sc.op('v', lambda e: e.memset(c127_c, 127.0))
    sc.op('v', lambda e: e.memset(pi_c, 2.0 ** -24))
    sc.op('v', lambda e: e.drain())
    sc.op('v', lambda e: e.tensor_scalar(pi_c, pi_c, 1.0, None, ALU.add))
    sc.op('v', lambda e: e.drain())
    sc.op('v', lambda e: e.tensor_scalar(pi_c, pi_c, 1.0, None, ALU.subtract))
    sc.op('v', lambda e: e.drain())
    sc.op('v', lambda e: e.tensor_scalar(pi_c, pi_c, 2.0 ** 24, None, ALU.mult))
    sc.op('v', lambda e: e.drain())
    sc.op('v', lambda e: e.tensor_scalar(cb_c, pi_c, C52MC32, C32, ALU.mult, ALU.add))
    sc.op('v', lambda e: e.drain())
    v_konst = sc.op('v', lambda e: e.tensor_scalar(ng_c, cb_c, -1.0, None, ALU.mult), inc='V')

    sc.op('g', lambda e: e.dma_start(
        out=wq_sb.rearrange("p (c m) -> p c m", c=HC),
        in_=wq_d.rearrange("(c p) m -> p c m", p=128)), inc='DGA', n=16)
    sc.op('g', lambda e: e.dma_start(
        out=wk_sb.rearrange("p (c m) -> p c m", c=HC),
        in_=wk_d.rearrange("(c p) m -> p c m", p=128)), inc='DGA', n=16)
    sc.op('g', lambda e: e.dma_start(
        out=wv_sb.rearrange("p (c m) -> p c m", c=HC),
        in_=wv_d.rearrange("(c p) m -> p c m", p=128)), inc='DGA', n=16)
    sc.op('g', lambda e: e.dma_start(out=wo_sb, in_=wo_d), inc='DGA', n=16)
    dg_attnw = ('DGA', sc.cnt['DGA'])

    dg_x = [None] * NT
    s_lastx = [0] * NT

    def emit_xload_p1(i):
        par = i % 3
        w = []
        if i >= 3:
            w += [('S', s_lastx[i - 3]), ('V', v_s1[i - 3])]
        v = sc.op('g', (lambda i=i, par=par: (lambda e: e.dma_start(
            out=x3[par], in_=x_d[i * 128:(i + 1) * 128, :])))(), waits=w, inc='DGX%d' % par, n=16)
        dg_x[i] = ('DGX%d' % par, v)

    wpieces = []
    for wi, wd_ap in ((0, wgt_d), (1, wut_d)):
        for c in range(HC):
            for hh in range(4):
                wpieces.append((wi, wd_ap[c * 128:(c + 1) * 128, hh * 1024:(hh + 1) * 1024], 1024))
    for p in range(OC):
        wpieces.append((2, wdt_d[p * 128:(p + 1) * 128, :], 1024))

    dq_p1w = [None] * len(wpieces)
    s_abs = [None] * len(wpieces)
    w1stages = [wst2[:, 1024:2048],
                nqh_sb.bitcast(F32)[:, 512:1536], nqt_sb.bitcast(F32)[:, 512:1536]]
    w2stages = [x2_st, wst2,
                nqh_sb.bitcast(F32)[:, 0:2048], nqt_sb.bitcast(F32)[:, 0:2048]]
    w2sems = ['DQW0', 'DQW1', 'DQW2', 'DQW3']

    def emit_w1_piece(idx):
        wi, src, wid = wpieces[idx]
        par = idx % 3
        dstap = w1stages[par][:, 0:wid]
        w = [s_abs[idx - 3]] if idx >= 3 else []
        v = sc.op('q', (lambda src=src, dstap=dstap: (lambda e: e.dma_start(out=dstap, in_=src)))(),
                  waits=w, inc='DQW%d' % par, n=16)
        dq_p1w[idx] = ('DQW%d' % par, v)
        col = wsums[:, idx:idx + 1]
        if idx % 2 == 0:
            s_abs[idx] = ('S', sc.op('s', (lambda dstap=dstap, col=col: (lambda e: e.activation(
                dstap, dstap, AF.Abs, accum_out=col)))(),
                waits=[dq_p1w[idx]], inc='S'))
        else:
            s_abs[idx] = ('V', sc.op('v', (lambda dstap=dstap, col=col: (lambda e: e.reduce_sum(
                col, dstap, axis=AX, apply_absolute_value=True)))(),
                waits=[dq_p1w[idx]], inc='V'))

    psA = [pb[0].bitcast(BF16)[:, 0:1024], pb[0].bitcast(BF16)[:, 1024:2048]]
    Qps = [pb[1][:, 0:128], pb[2][:, 512:640]]
    Kps = [pb[1][:, 512:640], pb[3][:, 0:128]]
    Vps = [pb[2][:, 0:128], pb[3][:, 512:640]]
    v_evA = [0, 0]
    v_evQ = [0, 0]; s_evK = [0, 0]; v_evV = [0, 0]

    p1c = [dict(s1=st[:, 8 * k + 0:8 * k + 1], sq=st[:, 8 * k + 1:8 * k + 2],
                m=st[:, 8 * k + 2:8 * k + 3], m2=st[:, 8 * k + 3:8 * k + 4],
                var=st[:, 8 * k + 4:8 * k + 5], sd=st[:, 8 * k + 5:8 * k + 6],
                rstd=st[:, 8 * k + 6:8 * k + 7], bm=st[:, 8 * k + 7:8 * k + 8])
           for k in range(3)]
    s_sqa = [0] * NT
    v_s1 = [0] * NT
    g_m = [0] * NT
    v_m2 = [0] * NT
    g_var = [0] * NT
    s_sd = [0] * NT
    v_rstd = [0] * NT
    g_bm = [0] * NT

    def emit_p1_A(i):
        par = i % 3
        xs = x3[par]
        c = p1c[par]
        vw = [dg_x[i]]
        if i >= 3:
            vw.append(('G', g_m[i - 3]))
        v_s1[i] = sc.op('v', (lambda xs=xs, c=c: (lambda e: e.reduce_sum(c['s1'], xs, axis=AX)))(),
                        waits=vw, inc='V')
        sw = [dg_x[i]]
        if i >= 1:
            sw.append(('S', s_sqa[i - 1]))
        if i >= 3:
            sw.append(('G', g_var[i - 3]))
        s_sqa[i] = sc.op('s', (lambda xs=xs, c=c: (lambda e: e.activation(
            ln_scr, xs, AF.Square, accum_out=c['sq'])))(), waits=sw, inc='S')

    def emit_p1_B1(i):
        c = p1c[i % 3]
        gw = [('V', v_s1[i])]
        if i >= 3:
            gw.append(('V', v_m2[i - 3]))
        g_m[i] = sc.op('g', (lambda c=c: (lambda e: e.tensor_scalar(
            c['m'], c['s1'], INV_D, None, ALU.mult)))(), waits=gw, inc='G')
        vw = [('G', g_m[i])]
        if i >= 3:
            vw.append(('G', g_var[i - 3]))
        v_m2[i] = sc.op('v', (lambda c=c: (lambda e: e.tensor_scalar(
            c['m2'], c['s1'], INV_D, c['m'], ALU.mult, ALU.mult)))(), waits=vw, inc='V')
        gw = [('V', v_m2[i]), ('S', s_sqa[i])]
        if i >= 3:
            gw.append(('S', s_sd[i - 3]))
        g_var[i] = sc.op('g', (lambda c=c: (lambda e: e.tensor_scalar(
            c['var'], c['sq'], INV_D, c['m2'], ALU.mult, ALU.subtract)))(), waits=gw, inc='G')

    def emit_p1_B2(i):
        par = i % 3
        xs = x3[par]
        c = p1c[par]
        sw = [('G', g_var[i])]
        if i >= 3:
            sw.append(('V', v_rstd[i - 3]))
        s_sd[i] = sc.op('s', (lambda c=c: (lambda e: e.activation(
            c['sd'], c['var'], AF.Sqrt, bias=eps_c)))(), waits=sw, inc='S')
        vw = [('S', s_sd[i])]
        if i >= 3:
            vw += [('G', g_bm[i - 3]), ('S', s_lastx[i - 3])]
        v_rstd[i] = sc.op('v', (lambda c=c: (lambda e: e.reciprocal(c['rstd'], c['sd'])))(),
                          waits=vw, inc='V')
        gw = [('V', v_rstd[i])]
        if i >= 3:
            gw.append(('S', s_lastx[i - 3]))
        g_bm[i] = sc.op('g', (lambda c=c: (lambda e: e.tensor_scalar(
            c['bm'], c['m'], -1.0, c['rstd'], ALU.mult, ALU.mult)))(), waits=gw, inc='G')
        s_ln1 = sc.op('s', (lambda xs=xs, c=c: (lambda e: e.activation(
            ln1_sb, xs, AF.Identity, scale=c['rstd'], bias=c['bm'])))(),
            waits=[('G', g_bm[i])], inc='S')
        s_lastx[i] = s_ln1
        if i + 3 < NT:
            emit_xload_p1(i + 3)
        return s_ln1

    def emit_p1_geo(i, s_ln1):
        par = i % 2
        tw = [('S', s_ln1), ('V', v_evA[par]), dq_const, ('V', v_scale_done[0])]
        tt = 0
        for c in range(HC):
            tt = sc.op('t', (lambda c=c, par=par: (lambda e: e.transpose(
                psA[par][:, c * 128:(c + 1) * 128], ln1_sb[:, c * 128:(c + 1) * 128], ident)))(),
                waits=tw if c == 0 else (), inc='T')
        v_evA[par] = sc.op('v', (lambda par=par: (lambda e: e.tensor_copy(ln1T_sb, psA[par])))(),
                           waits=[('T', tt)], inc='V')
        tw = [('V', v_evA[par]), dg_attnw,
              ('V', v_evQ[par]), ('S', s_evK[par]), ('V', v_evV[par])]
        tt = 0
        for c in range(HC):
            tt = sc.op('t', (lambda c=c, par=par: (lambda e: e.matmul(
                Qps[par], wq_sb[:, c * 128:(c + 1) * 128], ln1T_sb[:, c * 128:(c + 1) * 128],
                start=(c == 0), stop=(c == HC - 1))))(), waits=tw if c == 0 else (), inc='T')
        for c in range(HC):
            tt = sc.op('t', (lambda c=c, par=par: (lambda e: e.matmul(
                Kps[par], wk_sb[:, c * 128:(c + 1) * 128], ln1T_sb[:, c * 128:(c + 1) * 128],
                start=(c == 0), stop=(c == HC - 1))))(), inc='T')
        for c in range(HC):
            tt = sc.op('t', (lambda c=c, par=par: (lambda e: e.matmul(
                Vps[par], ln1T_sb[:, c * 128:(c + 1) * 128], wv_sb[:, c * 128:(c + 1) * 128],
                start=(c == 0), stop=(c == HC - 1))))(), inc='T')
        v_evQ[par] = sc.op('v', (lambda i=i, par=par: (lambda e: e.tensor_copy(
            qres[:, i * 128:(i + 1) * 128], Qps[par])))(), waits=[('T', tt)], inc='V')
        s_evK[par] = sc.op('s', (lambda i=i, par=par: (lambda e: e.activation(
            kres[:, i * 128:(i + 1) * 128], Kps[par], AF.Identity)))(), waits=[('T', tt)], inc='S')
        v_evV[par] = sc.op('v', (lambda i=i, par=par: (lambda e: e.tensor_copy(
            vres[:, i * 129:i * 129 + 128], Vps[par])))(), inc='V')

    v_scale_done = [0]
    emit_xload_p1(0)
    emit_xload_p1(1)
    emit_xload_p1(2)
    emit_p1_A(0)
    emit_p1_A(1)
    emit_p1_B1(0)
    w1_next = 0
    for i in range(NT):
        s_ln1x = emit_p1_B2(i)
        if i + 1 < NT:
            emit_p1_B1(i + 1)
        if i + 2 < NT:
            emit_p1_A(i + 2)
        emit_p1_geo(i, s_ln1x)
        for _ in range(3):
            if w1_next < len(wpieces):
                emit_w1_piece(w1_next)
                w1_next += 1
    while w1_next < len(wpieces):
        emit_w1_piece(w1_next)
        w1_next += 1
    v_vones = sc.op('v', lambda e: e.memset(vres[:, 128::129], 1.0), inc='V')
    p1_V = sc.cnt['V']; p1_S = sc.cnt['S']; p1_T = sc.cnt['T']

    def emit_scales():
        vsd = 0
        wcols = [wsums[:, 0:32], wsums[:, 32:64], wsums[:, 64:96]]
        wlast = [s_abs[31], s_abs[63], s_abs[95]]
        for wi in range(3):
            cols = wcols[wi]
            tot_col = st[:, 28:29]
            vv = sc.op('v', (lambda cols=cols, tot_col=tot_col: (lambda e: e.reduce_sum(
                tot_col, cols, axis=AX)))(), waits=[wlast[wi], s_abs[30 + 32 * wi], ('S', p1_S)], inc='V')
            tt = sc.op('t', (lambda tot_col=tot_col: (lambda e: e.matmul(
                pb[0][0:1, 0:1], tot_col, ones_c, start=True, stop=True)))(),
                waits=[('V', vv), ('V', v_evA[0]), ('V', v_evA[1])], inc='T')
            vv = sc.op('v', lambda e: e.tensor_scalar(tot_sb[:, 0:1], pb[0][0:1, 0:1], INV_WN, EPS, ALU.mult, ALU.max),
                       waits=[('T', tt)], inc='V')
            sc.op('v', lambda e: e.drain())
            sc.op('v', lambda e: e.reciprocal(tot_sb[:, 1:2], tot_sb[:, 0:1]))
            sc.op('v', lambda e: e.drain())
            vv = sc.op('v', lambda e: e.tensor_scalar(tot_sb[:, 2:3], tot_sb[:, 0:1], 1.0 / 127.0, None, ALU.mult), inc='V')
            tt = sc.op('t', lambda e: e.matmul(pb[1][0:128, 0:1], ones_r, tot_sb[:, 1:2], start=True, stop=True),
                       waits=[('V', vv), ('V', v_evQ[0]), ('S', s_evK[0])], inc='T')
            tt = sc.op('t', lambda e: e.matmul(pb[1][0:128, 512:513], ones_r, tot_sb[:, 2:3], start=True, stop=True),
                       inc='T')
            s_col = wsc[:, 2 * wi:2 * wi + 1]
            d_col = wsc[:, 2 * wi + 1:2 * wi + 2]
            vv = sc.op('v', (lambda s_col=s_col: (lambda e: e.tensor_copy(s_col, pb[1][0:128, 0:1])))(),
                       waits=[('T', tt)], inc='V')
            vsd = sc.op('v', (lambda d_col=d_col: (lambda e: e.tensor_copy(d_col, pb[1][0:128, 512:513])))(),
                        inc='V')
        v_scale_done[0] = vsd

    swg_c = wsc[:, 0:1]; wgdq_c = wsc[:, 1:2]
    swu_c = wsc[:, 2:3]; wudq_c = wsc[:, 3:4]
    swd_c = wsc[:, 4:5]; wddq_c = wsc[:, 5:6]

    p2pieces = []
    for wi, wd_ap in ((0, wgt_d), (1, wut_d)):
        for c in range(HC):
            for hh in range(2):
                dst = (wgq if wi == 0 else wuq)[:, c * INNER + hh * 2048: c * INNER + (hh + 1) * 2048]
                p2pieces.append((wi, wd_ap[c * 128:(c + 1) * 128, hh * 2048:(hh + 1) * 2048], dst, 2048))
    for p in range(OC):
        p2pieces.append((2, wdt_d[p * 128:(p + 1) * 128, :], p, 1024))

    s_qdone = [0] * len(p2pieces)
    ds_wdq = [0] * len(p2pieces)

    def emit_w2_piece(idx):
        wi, src, dst, wid = p2pieces[idx]
        s_col = wsc[:, 2 * wi:2 * wi + 1]
        par = idx % 4
        dstap = w2stages[par][:, 0:wid]
        if idx < 4:
            w = [('S', p1_S), ('V', p1_V), ('T', p1_T)]
        else:
            w = [('S', s_qdone[idx - 4])]
        v = sc.op('q', (lambda src=src, dstap=dstap: (lambda e: e.dma_start(out=dstap, in_=src)))(),
                  waits=w, inc=w2sems[par], n=16)
        vw = [(w2sems[par], v), ('V', v_konst), ('V', v_scale_done[0])]
        v_q = sc.op('v', (lambda dstap=dstap, s_col=s_col: (lambda e: e.tensor_scalar(
            dstap, dstap, s_col, CLIP_B, ALU.mult, ALU.min)))(), waits=vw, inc='V')
        sc.op('v', lambda e: e.drain())
        v_q = sc.op('v', (lambda dstap=dstap: (lambda e: e.tensor_scalar(
            dstap, dstap, -CLIP_B, cb_c, ALU.max, ALU.add)))(), inc='V')
        if wi < 2:
            s_qdone[idx] = sc.op('s', (lambda dstap=dstap, dst=dst: (lambda e: e.activation(
                dst, dstap, AF.Identity, bias=ng_c)))(), waits=[('V', v_q)], inc='S')
        else:
            pdn = dst
            wds = [('V', v_q)]
            if idx >= 33:
                wds.append(('DSW', ds_wdq[idx - 1]))
            s_qdone[idx] = sc.op('s', (lambda dstap=dstap: (lambda e: e.activation(
                wdq_st, dstap, AF.Identity, bias=ng_c)))(), waits=wds, inc='S')
            ds_wdq[idx] = sc.op('s', (lambda pdn=pdn: (lambda e: e.dma_start(
                out=wdqh_d[pdn], in_=wdq_st)))(),
                waits=[('S', s_qdone[idx])], inc='DSW', n=16)

    stv4 = [pb[0][:, 0:512], pb[0][:, 512:1024], pb[1][:, 0:512], pb[1][:, 512:1024]]
    oaug = [pb[2][:, 0:132], pb[2][:, 512:644], pb[3][:, 0:132], pb[3][:, 512:644]]
    wops = pb[1]
    NKP = 4

    s_exp_last = [0, 0, 0, 0]
    t_av_last = [0, 0, 0, 0]
    dso_last = [0]
    dso_g = [0] * (NT // GRP)
    t_attn_pt = [0]
    dg_xpa = [None] * NT
    v_x1add = [0] * NT
    ds_x1 = [None] * NT
    s_otev_last = 0
    s_oev_last = 0
    v_rec_last = 0

    v_stats = [0] * NT
    s_stats = [0] * NT

    x4 = [x2_st[:, 0:1024], x2_st[:, 1024:2048], wst2[:, 0:1024], wst2[:, 1024:2048]]
    x4sem = ['DGX0', 'DGX1', 'DGX2', 'DGX3']

    def emit_xload_pa(i):
        par = i % 4
        if i >= 4:
            w = [ds_x1[i - 4], ('V', v_stats[i - 4]), ('S', s_stats[i - 4])]
        else:
            w = [('S', s_qdone[-1] if s_qdone[-1] else 0), ds_wd_done]
        v = sc.op('g', (lambda i=i, par=par: (lambda e: e.dma_start(
            out=x4[par], in_=x_d[i * 128:(i + 1) * 128, :])))(), waits=w, inc=x4sem[par], n=16)
        dg_xpa[i] = (x4sem[par], v)

    def emit_pa1_group(g):
        nonlocal s_otev_last, s_oev_last, v_rec_last
        g0 = g * GRP
        nk = g0 + GRP
        t_av_pt = [0, 0, 0]
        pend = []

        def emit_av(rec):
            p0, pp0, pt0, sexp0, gm0 = rec
            tw = [('S', sexp0)]
            if gm0:
                tw.append(('G', gm0))
            first = True
            tt = 0
            for d in range(2):
                k = 2 * p0 + d
                q_lo = max(k, g0)
                off = d * 512 + (q_lo - g0) * 128
                for j in range(g0 + GRP - q_lo):
                    q = q_lo + j
                    tt = sc.op('t', (lambda q=q, k=k, pt0=pt0, g0=g0, j=j, off=off: (lambda e: e.matmul(
                        oaug[q - g0][:, 0:129],
                        pt_row[pt0][:, off + j * 128: off + (j + 1) * 128],
                        vres[:, k * 129: (k + 1) * 129],
                        start=(k == 0), stop=(k == q))))(), waits=tw if first else (), inc='T')
                    first = False
            t_av_pt[pt0] = tt

        for p in range(nk // 2):
            pp = p % 2
            ptx = p % 3
            pbst = pb[pp]
            tw0 = [('S', s_exp_last[pp]), ('V', p1_V), ('S', p1_S)]
            if g == 0 and p == 0:
                tw0 += [('V', v_vones), ('V', v_scale_done[0])]
            if p == 0:
                tw0 += [('S', s_oev_last), ('V', v_rec_last)]
            first = True
            tt = 0
            for d in range(2):
                k = 2 * p + d
                q_lo = max(k, g0)
                off = d * 512 + (q_lo - g0) * 128
                for j in range(g0 + GRP - q_lo):
                    q = q_lo + j
                    tt = sc.op('t', (lambda k=k, q=q, pbst=pbst, off=off, j=j: (lambda e: e.matmul(
                        pbst[:, off + j * 128: off + (j + 1) * 128],
                        kres[:, k * 128:(k + 1) * 128], qres[:, q * 128:(q + 1) * 128],
                        start=True, stop=True)))(), waits=tw0 if first else (), inc='T')
                    first = False
            sw = [('T', tt), ('T', t_av_pt[ptx])]
            if ptx == 2:
                sw.append(('DSO', dso_last[0]))
            if 2 * p >= g0:
                lo0 = (2 * p - g0) * 128
                lo1 = 512 + (2 * p + 1 - g0) * 128
                sc.op('s', (lambda pp=pp, ptx=ptx, lo0=lo0: (lambda e: e.activation(
                    pt_row[ptx][:, lo0:512], pb[pp][:, lo0:512],
                    AF.Exp, scale=smsc_c)))(), waits=sw, inc='S')
                s_exp_last[pp] = sc.op('s', (lambda pp=pp, ptx=ptx, lo1=lo1: (lambda e: e.activation(
                    pt_row[ptx][:, lo1:1024], pb[pp][:, lo1:1024],
                    AF.Exp, scale=smsc_c)))(), inc='S')
            else:
                s_exp_last[pp] = sc.op('s', (lambda pp=pp, ptx=ptx: (lambda e: e.activation(
                    pt_row[ptx][:, 0:1024], pb[pp][:, 0:1024],
                    AF.Exp, scale=smsc_c)))(), waits=sw, inc='S')
            gmask = 0
            for d in range(2):
                k = 2 * p + d
                if k >= g0:
                    gmask = sc.op('g', (lambda ptx=ptx, k=k, g0=g0, d=d: (lambda e: e.tensor_mul(
                        pt_row[ptx][:, d * 512 + (k - g0) * 128: d * 512 + (k - g0 + 1) * 128],
                        pt_row[ptx][:, d * 512 + (k - g0) * 128: d * 512 + (k - g0 + 1) * 128],
                        mask)))(), waits=[('S', s_exp_last[pp]), dq_const], inc='G')
            pend.append((p, pp, ptx, s_exp_last[pp], gmask))
            if len(pend) > 2:
                emit_av(pend.pop(0))
        while pend:
            emit_av(pend.pop(0))
        t_attn_done = sc.cnt['T']
        v_rec = 0
        for j in range(GRP):
            v_rec = sc.op('v', (lambda j=j: (lambda e: e.reciprocal(
                rq_sb[:, j:j + 1], oaug[j][:, 128:129])))(),
                waits=[('T', t_attn_done)] if j == 0 else (), inc='V')
        v_rec_last = v_rec
        s_oev = 0
        for j in range(GRP):
            s_oev = sc.op('s', (lambda j=j: (lambda e: e.activation(
                o_sb[:, j * 128:(j + 1) * 128], oaug[j][:, 0:128], AF.Identity,
                scale=rq_sb[:, j:j + 1])))(), waits=[('V', v_rec)], inc='S')
        s_oev_last = s_oev

        dso_last[0] = sc.op('s', (lambda g=g: (lambda e: e.dma_start(
            out=oh_d[g], in_=o_sb)))(), waits=[('S', s_oev)], inc='DSO', n=16)
        dso_g[g] = dso_last[0]

    emit_scales()
    w2_next = 0
    for g in range(NT // GRP):
        emit_pa1_group(g)
        for _ in range(8):
            if w2_next < len(p2pieces):
                emit_w2_piece(w2_next)
                w2_next += 1
    while w2_next < len(p2pieces):
        emit_w2_piece(w2_next)
        w2_next += 1
    s_wq_done = s_qdone[31]
    ds_wd_done = ('DSW', ds_wdq[-1])

    t_attn_pt[0] = sc.cnt['T']
    emit_xload_pa(0)
    emit_xload_pa(1)
    emit_xload_pa(2)
    emit_xload_pa(3)
    t_attn_end = sc.cnt['T']; s_attn_end = sc.cnt['S']
    g_attn_end = sc.cnt['G']; v_attn_end = sc.cnt['V']
    o_in = [pt_row[0][:, 0:512], pt_row[1][:, 0:512]]
    s1_p = [s1_c, wsums[:, 0:1]]
    sq_p = [sq_c, wsums[:, 1:2]]
    m2_p = [m2_c, wsums[:, 2:3]]
    sqscr = [ln_scr, vres[:, 0:D]]
    dqo = [None] * (NT // GRP)
    t_ot = [0] * (NT // GRP)
    v_otev_g = [0] * (NT // GRP)
    wo_free = {1: 0, 2: 0, 3: 0}

    def emit_oload(g):
        par = g % 2
        w = [('DSO', dso_g[g]), ('T', t_attn_end), ('S', s_attn_end), ('G', g_attn_end)]
        if g >= 2:
            w.append(('T', t_ot[g - 2]))
        v = sc.op('q', (lambda g=g, par=par: (lambda e: e.dma_start(
            out=o_in[par], in_=oh_d[g])))(), waits=w, inc='DQO%d' % par, n=16)
        dqo[g] = ('DQO%d' % par, v)

    emit_oload(0)
    emit_oload(1)
    for g in range(NT // GRP):
        g0 = g * GRP
        par = g % 2
        tw = [dqo[g], ('V', v_otev_g[g - 1] if g >= 1 else 0)]
        if g == 0:
            tw += [('S', s_attn_end), ('V', v_attn_end)]
        tt = 0
        for j in range(GRP):
            tt = sc.op('t', (lambda j=j, par=par: (lambda e: e.transpose(
                pb[0].bitcast(BF16)[:, j * 128:(j + 1) * 128],
                o_in[par][:, j * 128:(j + 1) * 128], ident)))(),
                waits=tw if j == 0 else (), inc='T')
        t_ot[g] = tt
        v_otev_g[g] = sc.op('v', lambda e: e.tensor_copy(ot_sb, pb[0].bitcast(BF16)[:, 0:512]),
                            waits=[('T', tt)], inc='V')
        if g + 2 < NT // GRP:
            emit_oload(g + 2)
        wo_banks = [1, 2, 3, 1]

        def emit_stats(i):
            xb = x4[i % 4]
            jp = i % 2
            sc.op('v', lambda e: e.drain())
            sc.op('v', (lambda xb=xb, jp=jp: (lambda e: e.reduce_sum(
                s1_p[jp], xb, axis=AX)))(), waits=[('V', v_x1add[i])], inc='V')
            sw = [('V', v_x1add[i])]
            if i >= 2:
                sw.append(('S', s_stats[i - 2]))
            s_stats[i] = sc.op('s', (lambda xb=xb, jp=jp: (lambda e: e.activation(
                sqscr[jp], xb, AF.Square, accum_out=sq_p[jp])))(),
                waits=sw, inc='S')
            sc.op('v', lambda e: e.drain())
            sc.op('v', (lambda i=i, jp=jp: (lambda e: e.tensor_scalar(
                m_all[:, i:i + 1], s1_p[jp], INV_D, None, ALU.mult)))(), inc='V')
            sc.op('v', lambda e: e.drain())
            sc.op('v', (lambda i=i, jp=jp: (lambda e: e.tensor_mul(
                m2_p[jp], m_all[:, i:i + 1], m_all[:, i:i + 1])))(), inc='V')
            sc.op('v', lambda e: e.drain())
            v_stats[i] = sc.op('v', (lambda i=i, jp=jp: (lambda e: e.tensor_scalar(
                var_all[:, i:i + 1], sq_p[jp], INV_D, m2_p[jp], ALU.mult, ALU.subtract)))(),
                waits=[('S', s_stats[i])], inc='V')
            if i + 4 < NT:
                emit_xload_pa(i + 4)

        for j in range(GRP):
            i = g0 + j
            wb = wo_banks[j]
            wob = pb[wb]
            tw = [('V', v_otev_g[g]), ('V', wo_free[wb])]
            if g == 0 and j == 0:
                tw += [('S', s_attn_end), ('V', v_attn_end)]
            tt = sc.op('t', (lambda j=j, wob=wob: (lambda e: e.matmul(
                wob[:, 0:512], ot_sb[:, j * 128:(j + 1) * 128], wo_sb[:, 0:512],
                start=True, stop=True)))(), waits=tw, inc='T')
            tt = sc.op('t', (lambda j=j, wob=wob: (lambda e: e.matmul(
                wob[:, 512:1024], ot_sb[:, j * 128:(j + 1) * 128], wo_sb[:, 512:1024],
                start=True, stop=True)))(), inc='T')
            pari = i % 4
            xb = x4[pari]
            vw = [('T', tt), dg_xpa[i]]
            vv = sc.op('v', (lambda xb=xb, wob=wob: (lambda e: e.tensor_add(
                xb[:, 0:512], wob[:, 0:512], xb[:, 0:512])))(), waits=vw, inc='V')
            vv = sc.op('v', (lambda xb=xb, wob=wob: (lambda e: e.tensor_add(
                xb[:, 512:1024], wob[:, 512:1024], xb[:, 512:1024])))(), inc='V')
            v_x1add[i] = vv
            wo_free[wb] = vv
            v = sc.op('s', (lambda i=i, xb=xb: (lambda e: e.dma_start(
                out=x1h_d[i * 128:(i + 1) * 128, :], in_=xb)))(),
                waits=[('V', vv)], inc='DSX%d' % pari, n=16)
            ds_x1[i] = ('DSX%d' % pari, v)
            if j >= 1:
                emit_stats(i - 1)
        emit_stats(g0 + GRP - 1)

    ss = sc.op('s', lambda e: e.activation(var_all, var_all, AF.Sqrt, bias=eps_c),
               waits=[('V', v_stats[NT - 1])], inc='S')
    vv = sc.op('v', lambda e: e.reciprocal(rstd_all, var_all), waits=[('S', ss)], inc='V')
    sc.op('v', lambda e: e.drain())
    vv = sc.op('v', lambda e: e.tensor_mul(bm_all, m_all, rstd_all), inc='V')
    sc.op('v', lambda e: e.drain())
    v_lnstats = sc.op('v', lambda e: e.tensor_scalar(bm_all, bm_all, -1.0, None, ALU.mult), inc='V')
    pa1_V = sc.cnt['V']; pa1_S = sc.cnt['S']

    s2ctx.close()
    s3b = contextlib.ExitStack()
    h_sb = s3b.enter_context(nc.sbuf_tensor("h_sb", [128, INNER], F32)).ap()
    g_s = kres.bitcast(F32)[:, 0:1024]
    xq2 = qres[:, 2048:3072]
    xqTb = [qres[:, 0:1024], qres[:, 1024:2048]]

    dq_x1b = [None] * NT
    s_ln2t = [0] * NT
    s_xq = [0] * NT
    v_rnd1 = [0] * NT
    t_xqA = [0] * NT
    t_E = [0] * NT
    s_silu_q = {}
    v_stt_q = {}
    s_nqh_t = [0] * NT
    v_nqev = [0] * NT
    ds_nq = [None] * NT

    def emit_x1load(i):
        par = i % 2
        w = [ds_x1[i]]
        if i >= 2:
            w.append(('S', s_ln2t[i - 2]))
        else:
            w += [('V', pa1_V), ('S', pa1_S), ds_x1[NT - 4], ds_x1[NT - 3]]
        v = sc.op('q', (lambda i=i, par=par: (lambda e: e.dma_start(
            out=x_st[par], in_=x1h_d[i * 128:(i + 1) * 128, :])))(), waits=w, inc='DQX%d' % par, n=16)
        dq_x1b[i] = ('DQX%d' % par, v)

    def chain_blk1(i):
        par = i % 2
        sw = [dq_x1b[i], ('S', pa1_S)]
        if i >= 1:
            sw.append(('V', v_rnd1[i - 1]))
        s_ln2t[i] = sc.op('s', (lambda par=par, i=i: (lambda e: e.activation(
            ln_scr, x_st[par], AF.Identity, scale=rstd_all[:, i:i + 1],
            bias=bm_all[:, i:i + 1])))(), waits=sw, inc='S')

    def chain_blk2(i):
        sc.op('v', lambda e: e.drain())
        vv = sc.op('v', lambda e: e.reduce_max(mx_c, ln_scr, axis=AX, apply_absolute_value=True),
                   waits=[('S', s_ln2t[i]), ('V', v_lnstats)], inc='V')
        sc.op('v', lambda e: e.drain())
        vv = sc.op('v', lambda e: e.tensor_scalar(mxc_c, mx_c, EPS, None, ALU.max), inc='V')
        sc.op('v', lambda e: e.drain())
        vv = sc.op('v', lambda e: e.reciprocal(r_c, mxc_c), inc='V')

    def chain_blk3(i):
        sc.op('v', lambda e: e.drain())
        sc.op('v', lambda e: e.tensor_scalar(s1q_c, r_c, 127.0, None, ALU.mult), inc='V')
        sc.op('v', (lambda i=i: (lambda e: e.tensor_mul(c1g_p[i % 2], mxc_c, wgdq_c)))(), inc='V')
        sc.op('v', (lambda i=i: (lambda e: e.tensor_mul(c1u_p[i % 2], mxc_c, wudq_c)))(), inc='V')
        sc.op('v', lambda e: e.drain())
        vv = sc.op('v', lambda e: e.tensor_scalar(ln_scr, ln_scr, s1q_c, cb_c, ALU.mult, ALU.add), inc='V')
        v_rnd1[i] = vv
        sw = [('V', vv), ('S', s_wq_done)]
        if i >= 1:
            sw.append(('T', t_xqA[i - 1]))
        else:
            sw += [ds_x1[NT - 2], ds_x1[NT - 1], ds_wd_done]
        s_xq[i] = sc.op('s', lambda e: e.activation(xq2, ln_scr, AF.Identity, bias=ng_c),
                        waits=sw, inc='S')
        if i + 2 < NT:
            emit_x1load(i + 2)

    emit_x1load(0)
    emit_x1load(1)
    chain_blk1(0)
    chain_blk2(0)
    chain_blk3(0)

    def emit_ychain(i):
        sc.op('v', lambda e: e.drain())
        sc.op('v', lambda e: e.tensor_max(mh4[:, 0:2], mh4[:, 0:2], mh4[:, 2:4]), inc='V')
        sc.op('v', lambda e: e.drain())
        sc.op('v', lambda e: e.tensor_max(mh_c, mh4[:, 0:1], mh4[:, 1:2]), inc='V')
        sc.op('v', lambda e: e.drain())
        sc.op('v', lambda e: e.tensor_scalar(mhc_c, mh_c, EPS, None, ALU.max), inc='V')
        sc.op('v', lambda e: e.drain())
        sc.op('v', lambda e: e.reciprocal(rh_c, mhc_c), inc='V')
        sc.op('v', lambda e: e.drain())
        sc.op('v', lambda e: e.tensor_scalar(s2q_c, rh_c, 127.0, None, ALU.mult), inc='V')
        sc.op('v', (lambda i=i: (lambda e: e.tensor_mul(c2_all[:, i:i + 1], mhc_c, wddq_c)))(), inc='V')
        sc.op('v', lambda e: e.drain())
        vv = sc.op('v', (lambda i=i: (lambda e: e.tensor_scalar(
            h_sb, h_sb, s2q_c, cb_c, ALU.mult, ALU.add)))(),
            waits=[('V', v_stt_q[(i, 3)])], inc='V')
        s_nqh_t[i] = sc.op('s', lambda e: e.activation(nqh_sb, h_sb, AF.Identity, bias=ng_c),
                           waits=[('V', vv)], inc='S')

    pair_free = {0: (0, v_x1add[NT - 1]), 1: (0, v_x1add[NT - 1])}
    v_xqt_arr = [0] * NT
    gbank = {0: pb[0], 1: pb[2]}
    ubank = {0: pb[1], 1: pb[3]}

    def emit_Ablock(j):
        apair = j % 2
        aps = pb[0] if apair == 0 else pb[2]
        tw = [('S', s_xq[j]), dq_const,
              ('S', pair_free[apair][0]), ('V', pair_free[apair][1])]
        if j == 0:
            tw += [('V', v_otev_g[NT // GRP - 1]), ('V', v_x1add[NT - 1])]
        tt = 0
        for c in range(HC):
            tt = sc.op('t', (lambda c=c, aps=aps: (lambda e: e.transpose(
                aps.bitcast(BF16)[:, c * 128:(c + 1) * 128],
                xq2[:, c * 128:(c + 1) * 128], ident)))(),
                waits=tw if c == 0 else (), inc='T')
        t_xqA[j] = tt
        vw = [('T', tt)]
        if j < 2:
            vw += [ds_x1[NT - 2], ds_x1[NT - 1], ds_wd_done]
        v_xqt_arr[j] = sc.op('v', (lambda j=j, aps=aps: (lambda e: e.tensor_copy(
            xqTb[j % 2], aps.bitcast(BF16)[:, 0:1024])))(), waits=vw, inc='V')

    emit_Ablock(0)
    for i in range(NT):
        par = i % 2
        qpair = [0, 1, 1, 0] if i % 2 == 0 else [1, 0, 0, 1]
        fpair = 0 if i % 2 == 0 else 1
        v_xqt = v_xqt_arr[i]

        for qtr in range(4):
            pr = qpair[qtr]
            gb = gbank[pr]
            ub = ubank[pr]
            tw = [('S', pair_free[pr][0]), ('V', pair_free[pr][1])]
            if qtr == 0:
                tw += [('V', v_xqt), ('S', s_wq_done)]
                if i == 0:
                    tw += [('V', v_x1add[NT - 1])]
            first = True
            tt = 0
            for c in range(HC):
                if qtr == 3 and c == 2 and i + 1 < NT:
                    emit_Ablock(i + 1)
                for half in range(2):
                    o0 = qtr * 1024 + half * 512
                    tt = sc.op('t', (lambda c=c, o0=o0, gb=gb, half=half, par=par: (lambda e: e.matmul(
                        gb[:, half * 512:(half + 1) * 512],
                        xqTb[par][:, c * 128:(c + 1) * 128],
                        wgq[:, c * INNER + o0: c * INNER + o0 + 512],
                        start=(c == 0), stop=(c == HC - 1))))(),
                        waits=tw if first else (), inc='T')
                    first = False
                    tt = sc.op('t', (lambda c=c, o0=o0, ub=ub, half=half, par=par: (lambda e: e.matmul(
                        ub[:, half * 512:(half + 1) * 512],
                        xqTb[par][:, c * 128:(c + 1) * 128],
                        wuq[:, c * INNER + o0: c * INNER + o0 + 512],
                        start=(c == 0), stop=(c == HC - 1))))(), inc='T')
            if qtr == 3:
                t_E[i] = tt
            sw = [('T', tt)]
            if qtr >= 1:
                sw.append(('V', v_stt_q[(i, qtr - 1)]))
            s_silu_q[(i, qtr)] = sc.op('s', (lambda gb=gb, i=i: (lambda e: e.activation(
                g_s, gb, AF.Silu, scale=c1g_p[i % 2])))(), waits=sw, inc='S')
            v_stt_q[(i, qtr)] = sc.op('v', (lambda qtr=qtr, ub=ub, i=i: (lambda e: e.scalar_tensor_tensor(
                h_sb[:, qtr * 1024:(qtr + 1) * 1024], ub, c1u_p[i % 2], g_s, ALU.mult, ALU.mult)))(),
                waits=[('S', s_silu_q[(i, qtr)])], inc='V')
            pair_free[pr] = (s_silu_q[(i, qtr)], v_stt_q[(i, qtr)])
            sc.op('v', lambda e: e.drain())
            sc.op('v', (lambda qtr=qtr: (lambda e: e.reduce_max(
                mh4[:, qtr:qtr + 1], h_sb[:, qtr * 1024:(qtr + 1) * 1024],
                axis=AX, apply_absolute_value=True)))(), inc='V')

            if i + 1 < NT:
                if qtr == 0:
                    chain_blk1(i + 1)
                elif qtr == 2:
                    chain_blk3(i + 1)
            if qtr == 1 and i >= 1:
                fg = gbank[fpair]
                fu = ubank[fpair]
                tw2 = [('S', s_nqh_t[i - 1]),
                       ('S', pair_free[fpair][0]), ('V', pair_free[fpair][1])]
                tt2 = 0
                for c in range(OC):
                    pbx = fg if c < 16 else fu
                    coff = (c % 16) * 128
                    tt2 = sc.op('t', (lambda c=c, pbx=pbx, coff=coff: (lambda e: e.transpose(
                        pbx.bitcast(BF16)[:, coff:coff + 128],
                        nqh_sb[:, c * 128:(c + 1) * 128], ident)))(),
                        waits=tw2 if c == 0 else (), inc='T')
                sw2 = [('T', tt2)]
                if i >= 2:
                    sw2.append(ds_nq[i - 2])
                ss2 = sc.op('s', (lambda fg=fg: (lambda e: e.activation(
                    nqt_sb[:, 0:2048], fg.bitcast(BF16)[:, 0:2048], AF.Identity)))(),
                    waits=sw2, inc='S')
                ss2 = sc.op('s', (lambda fu=fu: (lambda e: e.activation(
                    nqt_sb[:, 2048:4096], fu.bitcast(BF16)[:, 0:2048], AF.Identity)))(),
                    inc='S')
                s_nqev = ss2
                v_nqev[i - 1] = 0
                pair_free[fpair] = (s_nqev, pair_free[fpair][1])
                v2 = sc.op('s', (lambda i=i: (lambda e: e.dma_start(
                    out=nqh_d[i - 1], in_=nqt_sb)))(), waits=[('S', s_nqev)], inc='DSN', n=16)
                ds_nq[i - 1] = ('DSN', v2)
            if qtr == 1 and i + 1 < NT:
                chain_blk2(i + 1)
        emit_ychain(i)

    tw2 = [('S', s_nqh_t[NT - 1]), ('S', pair_free[0][0]), ('V', pair_free[0][1])]
    tt2 = 0
    for c in range(OC):
        pbx = pb[0] if c < 16 else pb[1]
        coff = (c % 16) * 128
        tt2 = sc.op('t', (lambda c=c, pbx=pbx, coff=coff: (lambda e: e.transpose(
            pbx.bitcast(BF16)[:, coff:coff + 128],
            nqh_sb[:, c * 128:(c + 1) * 128], ident)))(),
            waits=tw2 if c == 0 else (), inc='T')
    sw2 = [('T', tt2), ds_nq[NT - 2]]
    ss2 = sc.op('s', lambda e: e.activation(nqt_sb[:, 0:2048], pb[0].bitcast(BF16)[:, 0:2048], AF.Identity),
                waits=sw2, inc='S')
    ss2 = sc.op('s', lambda e: e.activation(nqt_sb[:, 2048:4096], pb[1].bitcast(BF16)[:, 0:2048], AF.Identity),
                inc='S')
    v2 = sc.op('s', lambda e: e.dma_start(out=nqh_d[NT - 1], in_=nqt_sb),
               waits=[('S', ss2)], inc='DSN', n=16)
    ds_nq[NT - 1] = ('DSN', v2)
    pa2_V = sc.cnt['V']; pa2_S = sc.cnt['S']; pa2_G = sc.cnt['G']

    s3b.close()
    s1ctx.close()
    s4ctx = contextlib.ExitStack()
    wdq_sb = s4ctx.enter_context(nc.sbuf_tensor("wdq_sb", [128, OC * 1024], BF16)).ap()
    nqt_in = [s4ctx.enter_context(nc.sbuf_tensor(f"nqt_in{i}", [128, INNER], BF16)).ap() for i in range(2)]
    x1_in = [s4ctx.enter_context(nc.sbuf_tensor(f"x1_in{i}", [128, D], F32)).ap() for i in range(2)]
    out_sb = [s4ctx.enter_context(nc.sbuf_tensor(f"out_sb{i}", [128, D], F32)).ap() for i in range(2)]

    v = sc.op('q', lambda e: e.dma_start(
        out=wdq_sb.rearrange("p (c d) -> p c d", c=OC),
        in_=wdqh_d.rearrange("c p d -> p c d")),
        waits=[('T', t_E[NT - 1]), ds_wd_done], inc='DQWD', n=16)
    dq_wdq = ('DQWD', v)

    dq_nq = [None] * NT; dq_x1pb = [None] * NT
    v_out = [0] * NT
    ds_y = [None] * NT
    t_down = [0] * NT
    dps = [pb[0], pb[1]]

    def emit_pb_loads(i):
        par = i % 2
        w = [('S', pa2_S), ds_nq[i]]
        if i >= 2:
            w.append(('T', t_down[i - 2]))
        v = sc.op('q', (lambda i=i, par=par: (lambda e: e.dma_start(
            out=nqt_in[par], in_=nqh_d[i])))(), waits=w, inc='DQN%d' % par, n=16)
        dq_nq[i] = ('DQN%d' % par, v)
        w2 = [('S', pa2_S), ('V', pa2_V), ds_x1[i]]
        if i >= 2:
            w2.append(('V', v_out[i - 2]))
        v = sc.op('q', (lambda i=i, par=par: (lambda e: e.dma_start(
            out=x1_in[par], in_=x1h_d[i * 128:(i + 1) * 128, :])))(), waits=w2, inc='DQX%d' % par, n=16)
        dq_x1pb[i] = ('DQX%d' % par, v)

    emit_pb_loads(0)
    emit_pb_loads(1)
    for i in range(NT):
        par = i % 2
        tw = [dq_nq[i], dq_wdq]
        if i < 2:
            tw += [('V', pa2_V), ('S', pa2_S)]
        else:
            tw.append(('V', v_out[i - 2]))
        tt = 0
        for c in range(OC):
            for half in range(2):
                tt = sc.op('t', (lambda c=c, half=half, par=par: (lambda e: e.matmul(
                    dps[par][:, half * 512:(half + 1) * 512],
                    nqt_in[par][:, c * 128:(c + 1) * 128],
                    wdq_sb[:, c * 1024 + half * 512: c * 1024 + (half + 1) * 512],
                    start=(c == 0), stop=(c == OC - 1))))(),
                    waits=tw if (c == 0 and half == 0) else (), inc='T')
        t_down[i] = tt
        vw = [('T', tt), dq_x1pb[i]]
        if i >= 2:
            vw.append(ds_y[i - 2])
        vv = sc.op('v', (lambda i=i, par=par: (lambda e: e.scalar_tensor_tensor(
            out_sb[par][:, 0:512], dps[par][:, 0:512], c2_all[:, i:i + 1],
            x1_in[par][:, 0:512], ALU.mult, ALU.add)))(), waits=vw, inc='V')
        vv = sc.op('v', (lambda i=i, par=par: (lambda e: e.scalar_tensor_tensor(
            out_sb[par][:, 512:1024], dps[par][:, 512:1024], c2_all[:, i:i + 1],
            x1_in[par][:, 512:1024], ALU.mult, ALU.add)))(), inc='V')
        v_out[i] = vv
        v = sc.op('s', (lambda i=i, par=par: (lambda e: e.dma_start(
            out=y_d[i * 128:(i + 1) * 128, :], in_=out_sb[par])))(),
            waits=[('V', vv)], inc='DSY%d' % par, n=16)
        ds_y[i] = ('DSY%d' % par, v)
        if i + 2 < NT:
            emit_pb_loads(i + 2)

    sc.op('q', lambda e: e.nop(), waits=[(nm, sc.cnt[nm]) for nm in sc.cnt if nm.startswith('D')])

    sems = {n: stack.enter_context(nc.semaphore(n)) for n in sc.cnt}
    blk = stack.enter_context(nc.Block())

    def emit(key):
        def body(eng):
            last = {}
            for fn, waits, inc, n in sc.st[key]:
                for s, v in waits:
                    if last.get(s, 0) < v:
                        eng.wait_ge(sems[s], v)
                        last[s] = v
                inst = fn(eng)
                if inc is not None:
                    inst.then_inc(sems[inc], n)
        return body

    blk.tensor(emit('t'))
    blk.vector(emit('v'))
    blk.scalar(emit('s'))
    blk.gpsimd(emit('g'))
    blk.sync(emit('q'))
    s4ctx.close()
    stack.close()
    return nc


def kernel(L0_attn_x, L0_attn_Wq, L0_attn_Wk, L0_attn_Wv, L0_attn_Wo,
           L0_ffn_w_gate, L0_ffn_w_up, L0_ffn_w_down):
    x = np.ascontiguousarray(np.asarray(L0_attn_x, dtype=np.float32))
    assert x.shape == (8, 4096, 1024)
    wq = np.ascontiguousarray(np.asarray(L0_attn_Wq, dtype=np.float32))
    wk = np.ascontiguousarray(np.asarray(L0_attn_Wk, dtype=np.float32))
    wv = np.ascontiguousarray(np.asarray(L0_attn_Wv, dtype=np.float32))
    wo = np.ascontiguousarray(np.asarray(L0_attn_Wo, dtype=np.float32))
    wgt = np.ascontiguousarray(np.asarray(L0_ffn_w_gate, dtype=np.float32).T)
    wut = np.ascontiguousarray(np.asarray(L0_ffn_w_up, dtype=np.float32).T)
    wdt = np.ascontiguousarray(np.asarray(L0_ffn_w_down, dtype=np.float32).T)
    ident = np.eye(128, dtype=ml_dtypes.bfloat16)
    mask = np.triu(np.ones((128, 128), dtype=ml_dtypes.bfloat16))

    nc = _build()
    in_maps = [{"x": x[i], "wq": wq, "wk": wk, "wv": wv, "wo": wo,
                "wgt": wgt, "wut": wut, "wdt": wdt,
                "ident": ident, "mask": mask} for i in range(8)]
    res = run_bass_kernel_spmd(nc, in_maps, core_ids=list(range(8)))
    out = np.stack([np.asarray(res.results[i]["y"]) for i in range(8)], axis=0)
    return out.astype(np.float32)


# version 30
# speedup vs baseline: 1.0481x; 1.0481x over previous
import numpy as np
import ml_dtypes
import concourse.bass as bass
import concourse.mybir as mybir
from concourse.bass_utils import run_bass_kernel_spmd

F32 = mybir.dt.float32
BF16 = mybir.dt.bfloat16
AX = mybir.AxisListType.X
ALU = mybir.AluOpType
AF = mybir.ActivationFunctionType

NT = 32
GRP = 4
D = 1024
DK = 128
INNER = 4096
HC = 8
OC = 32
SM_SCALE = 128.0 ** -0.5
C32 = 12582912.0
C52MC32 = 6755399428472832.0
CLIP_B = 1.49
EPS = 1e-5
INV_D = 1.0 / 1024.0
INV_WN = 1.0 / float(INNER * D)
DEBUG = False


class Sched:
    def __init__(self):
        self.st = {k: [] for k in "tvsgq"}
        self.cnt = dict(T=0, V=0, S=0, G=0, DQC=0, DGA=0, DGX0=0, DGX1=0,
                        DQW0=0, DQW1=0, DSW=0, DSX0=0, DSX1=0, DSN=0,
                        DSY0=0, DSY1=0, DQN0=0, DQN1=0, DQX0=0, DQX1=0, DQWD=0,
                        DSO=0, DQO0=0, DQO1=0, DGX2=0, DGX3=0, DSX2=0, DSX3=0, DQW2=0, DQW3=0,
                        DGW0=0, DGW1=0, DPX0=0, DPX1=0, DPX2=0)

    def op(self, e, fn, waits=(), inc=None, n=1):
        w = [(s, int(v)) for (s, v) in waits if v and v > 0]
        self.st[e].append((fn, w, inc, n))
        if inc is None:
            return None
        self.cnt[inc] += n
        return self.cnt[inc]


def _build():
    nc = bass.Bass("TRN2", target_bir_lowering=False)
    scratch_kind = "ExternalOutput" if DEBUG else "Internal"

    x_d = nc.dram_tensor("x", [4096, D], F32, kind="ExternalInput").ap()
    wq_d = nc.dram_tensor("wq", [D, DK], F32, kind="ExternalInput").ap()
    wk_d = nc.dram_tensor("wk", [D, DK], F32, kind="ExternalInput").ap()
    wv_d = nc.dram_tensor("wv", [D, DK], F32, kind="ExternalInput").ap()
    wo_d = nc.dram_tensor("wo", [DK, D], F32, kind="ExternalInput").ap()
    wgt_d = nc.dram_tensor("wgt", [D, INNER], F32, kind="ExternalInput").ap()
    wut_d = nc.dram_tensor("wut", [D, INNER], F32, kind="ExternalInput").ap()
    wdt_d = nc.dram_tensor("wdt", [INNER, D], F32, kind="ExternalInput").ap()
    ident_d = nc.dram_tensor("ident", [128, 128], BF16, kind="ExternalInput").ap()
    mask_d = nc.dram_tensor("mask", [128, 128], BF16, kind="ExternalInput").ap()
    y_d = nc.dram_tensor("y", [4096, D], F32, kind="ExternalOutput").ap()
    x1h_d = nc.dram_tensor("x1h", [4096, D], F32, kind=scratch_kind).ap()
    nqh_d = nc.dram_tensor("nqh", [NT, 128, INNER], BF16, kind=scratch_kind).ap()
    wdqh_d = nc.dram_tensor("wdqh", [OC, 128, D], BF16, kind="Internal").ap()
    oh_d = nc.dram_tensor("oh", [NT // GRP, 128, GRP * 128], BF16, kind="Internal").ap()

    sc = Sched()
    import contextlib
    stack = contextlib.ExitStack()

    def sb(name, shape, dt):
        return stack.enter_context(nc.sbuf_tensor(name, shape, dt)).ap()

    qres = sb("qres", [128, NT * 128], BF16)
    kres = sb("kres", [128, NT * 128], BF16)
    vres = sb("vres", [128, NT * 129], BF16)
    wo_sb = sb("wo_sb", [128, D], BF16)
    ident = sb("ident_sb", [128, 128], BF16)
    mask = sb("mask_sb", [128, 128], BF16)
    ones_c = sb("ones_c", [128, 1], F32)
    ones_r = sb("ones_r", [1, 128], F32)
    x2_st = sb("x2_st", [128, 2 * D], F32)
    x_st = [x2_st[:, 0:D], x2_st[:, D:2 * D]]
    x3 = None
    ln_scr = sb("ln_scr", [128, D], F32)
    nqh_sb = sb("nqh_sb", [128, INNER], BF16)
    nqt_sb = sb("nqt_sb", [128, INNER], BF16)
    st = sb("st_cols", [128, 32], F32)
    c2_all = sb("c2_all", [128, NT], F32)
    wsums = sb("wsums", [128, 96], F32)
    kn = sb("konst", [128, 8], F32)
    m_all = sb("m_all", [128, NT], F32)
    var_all = sb("var_all", [128, NT], F32)
    rstd_all = sb("rstd_all", [128, NT], F32)
    bm_all = sb("bm_all", [128, NT], F32)
    wsc = sb("wscales", [128, 8], F32)
    tot_sb = sb("tot_sb", [1, 4], F32)

    pi_c = kn[:, 0:1]; cb_c = kn[:, 1:2]; ng_c = kn[:, 2:3]
    eps_c = kn[:, 3:4]; smsc_c = kn[:, 4:5]; c127_c = kn[:, 5:6]
    s1_c = st[:, 0:1]; sq_c = st[:, 1:2]; m_c = st[:, 2:3]; m2_c = st[:, 3:4]
    var_c = st[:, 4:5]; sd_c = st[:, 5:6]; rstd_c = st[:, 6:7]; bm_c = st[:, 7:8]
    mx_c = st[:, 8:9]; mxc_c = st[:, 9:10]; r_c = st[:, 10:11]; s1q_c = st[:, 11:12]
    c1g_p = [st[:, 12:13], st[:, 20:21]]
    c1u_p = [st[:, 13:14], st[:, 21:22]]
    mh4 = st[:, 24:28]
    mh_c = st[:, 15:16]; mhc_c = st[:, 16:17]; rh_c = st[:, 17:18]; s2q_c = st[:, 18:19]

    ln1_sb = nqh_sb[:, 0:D]
    ln1T_sb = nqt_sb[:, 0:D]

    s1ctx = contextlib.ExitStack()
    wgq = s1ctx.enter_context(nc.sbuf_tensor("wgq", [128, HC * INNER], BF16)).ap()
    wuq = s1ctx.enter_context(nc.sbuf_tensor("wuq", [128, HC * INNER], BF16)).ap()

    s2ctx = contextlib.ExitStack()
    wst2 = s2ctx.enter_context(nc.sbuf_tensor("wstage0", [128, 2048], F32)).ap()
    x3 = [x2_st[:, 0:D], x2_st[:, D:2 * D], wst2[:, 0:D]]
    wdq_st = s2ctx.enter_context(nc.sbuf_tensor("wdq_st", [128, 1024], BF16)).ap()
    wq_sb = s2ctx.enter_context(nc.sbuf_tensor("wq_sb", [128, D], BF16)).ap()
    wk_sb = s2ctx.enter_context(nc.sbuf_tensor("wk_sb", [128, D], BF16)).ap()
    wv_sb = s2ctx.enter_context(nc.sbuf_tensor("wv_sb", [128, D], BF16)).ap()
    pt_row = [s2ctx.enter_context(nc.sbuf_tensor(f"pt{i}", [128, 1024], BF16)).ap() for i in range(3)]
    o_sb = pt_row[2][:, 0:GRP * 128]
    ot_sb = s2ctx.enter_context(nc.sbuf_tensor("ot_sb", [128, 512], BF16)).ap()
    rq_sb = s2ctx.enter_context(nc.sbuf_tensor("rq_sb", [128, GRP], F32)).ap()

    pb = [stack.enter_context(nc.psum_tensor(f"pb{i}", [128, 1024], F32)).ap() for i in range(4)]

    sc.op('q', lambda e: e.dma_start(out=ident, in_=ident_d), inc='DQC', n=16)
    sc.op('q', lambda e: e.dma_start(out=mask, in_=mask_d), inc='DQC', n=16)
    dq_const = ('DQC', sc.cnt['DQC'])
    sc.op('v', lambda e: e.memset(ones_c, 1.0))
    sc.op('v', lambda e: e.memset(ones_r, 1.0))
    sc.op('v', lambda e: e.memset(eps_c, EPS))
    sc.op('v', lambda e: e.memset(smsc_c, SM_SCALE))
    sc.op('v', lambda e: e.memset(c127_c, 127.0))
    sc.op('v', lambda e: e.memset(pi_c, 2.0 ** -24))
    sc.op('v', lambda e: e.drain())
    sc.op('v', lambda e: e.tensor_scalar(pi_c, pi_c, 1.0, None, ALU.add))
    sc.op('v', lambda e: e.drain())
    sc.op('v', lambda e: e.tensor_scalar(pi_c, pi_c, 1.0, None, ALU.subtract))
    sc.op('v', lambda e: e.drain())
    sc.op('v', lambda e: e.tensor_scalar(pi_c, pi_c, 2.0 ** 24, None, ALU.mult))
    sc.op('v', lambda e: e.drain())
    sc.op('v', lambda e: e.tensor_scalar(cb_c, pi_c, C52MC32, C32, ALU.mult, ALU.add))
    sc.op('v', lambda e: e.drain())
    v_konst = sc.op('v', lambda e: e.tensor_scalar(ng_c, cb_c, -1.0, None, ALU.mult), inc='V')

    sc.op('g', lambda e: e.dma_start(
        out=wq_sb.rearrange("p (c m) -> p c m", c=HC),
        in_=wq_d.rearrange("(c p) m -> p c m", p=128)), inc='DGA', n=16)
    sc.op('g', lambda e: e.dma_start(
        out=wk_sb.rearrange("p (c m) -> p c m", c=HC),
        in_=wk_d.rearrange("(c p) m -> p c m", p=128)), inc='DGA', n=16)
    sc.op('g', lambda e: e.dma_start(
        out=wv_sb.rearrange("p (c m) -> p c m", c=HC),
        in_=wv_d.rearrange("(c p) m -> p c m", p=128)), inc='DGA', n=16)
    sc.op('g', lambda e: e.dma_start(out=wo_sb, in_=wo_d), inc='DGA', n=16)
    dg_attnw = ('DGA', sc.cnt['DGA'])

    dg_x = [None] * NT
    s_lastx = [0] * NT

    def emit_xload_p1(i):
        par = i % 3
        w = []
        if i >= 3:
            w += [('S', s_lastx[i - 3]), ('V', v_s1[i - 3])]
        v = sc.op('g', (lambda i=i, par=par: (lambda e: e.dma_start(
            out=x3[par], in_=x_d[i * 128:(i + 1) * 128, :])))(), waits=w, inc='DGX%d' % par, n=16)
        dg_x[i] = ('DGX%d' % par, v)

    wpieces = []
    for wi, wd_ap in ((0, wgt_d), (1, wut_d)):
        for c in range(HC):
            for hh in range(4):
                wpieces.append((wi, wd_ap[c * 128:(c + 1) * 128, hh * 1024:(hh + 1) * 1024], 1024))
    for p in range(OC):
        wpieces.append((2, wdt_d[p * 128:(p + 1) * 128, :], 1024))

    dq_p1w = [None] * len(wpieces)
    s_abs = [None] * len(wpieces)
    w1stages = [wst2[:, 1024:2048],
                nqh_sb.bitcast(F32)[:, 512:1536], nqt_sb.bitcast(F32)[:, 512:1536]]
    w2stages = [x2_st, wst2,
                nqh_sb.bitcast(F32)[:, 0:2048], nqt_sb.bitcast(F32)[:, 0:2048]]
    w2sems = ['DQW0', 'DQW1', 'DQW2', 'DQW3']

    def emit_w1_piece(idx):
        wi, src, wid = wpieces[idx]
        par = idx % 3
        dstap = w1stages[par][:, 0:wid]
        w = [s_abs[idx - 3]] if idx >= 3 else []
        v = sc.op('q', (lambda src=src, dstap=dstap: (lambda e: e.dma_start(out=dstap, in_=src)))(),
                  waits=w, inc='DQW%d' % par, n=16)
        dq_p1w[idx] = ('DQW%d' % par, v)
        col = wsums[:, idx:idx + 1]
        if idx % 2 == 0:
            s_abs[idx] = ('S', sc.op('s', (lambda dstap=dstap, col=col: (lambda e: e.activation(
                dstap, dstap, AF.Abs, accum_out=col)))(),
                waits=[dq_p1w[idx]], inc='S'))
        else:
            s_abs[idx] = ('V', sc.op('v', (lambda dstap=dstap, col=col: (lambda e: e.reduce_sum(
                col, dstap, axis=AX, apply_absolute_value=True)))(),
                waits=[dq_p1w[idx]], inc='V'))

    psA = [pb[0].bitcast(BF16)[:, 0:1024], pb[0].bitcast(BF16)[:, 1024:2048]]
    Qps = [pb[1][:, 0:128], pb[2][:, 512:640]]
    Kps = [pb[1][:, 512:640], pb[3][:, 0:128]]
    Vps = [pb[2][:, 0:128], pb[3][:, 512:640]]
    v_evA = [0, 0]
    v_evQ = [0, 0]; s_evK = [0, 0]; v_evV = [0, 0]

    p1c = [dict(s1=st[:, 8 * k + 0:8 * k + 1], sq=st[:, 8 * k + 1:8 * k + 2],
                m=st[:, 8 * k + 2:8 * k + 3], m2=st[:, 8 * k + 3:8 * k + 4],
                var=st[:, 8 * k + 4:8 * k + 5], sd=st[:, 8 * k + 5:8 * k + 6],
                rstd=st[:, 8 * k + 6:8 * k + 7], bm=st[:, 8 * k + 7:8 * k + 8])
           for k in range(3)]
    s_sqa = [0] * NT
    v_s1 = [0] * NT
    g_m = [0] * NT
    v_m2 = [0] * NT
    g_var = [0] * NT
    s_sd = [0] * NT
    v_rstd = [0] * NT
    g_bm = [0] * NT

    def emit_p1_A(i):
        par = i % 3
        xs = x3[par]
        c = p1c[par]
        vw = [dg_x[i]]
        if i >= 3:
            vw.append(('G', g_m[i - 3]))
        v_s1[i] = sc.op('v', (lambda xs=xs, c=c: (lambda e: e.reduce_sum(c['s1'], xs, axis=AX)))(),
                        waits=vw, inc='V')
        sw = [dg_x[i]]
        if i >= 1:
            sw.append(('S', s_sqa[i - 1]))
        if i >= 3:
            sw.append(('G', g_var[i - 3]))
        s_sqa[i] = sc.op('s', (lambda xs=xs, c=c: (lambda e: e.activation(
            ln_scr, xs, AF.Square, accum_out=c['sq'])))(), waits=sw, inc='S')

    def emit_p1_B1(i):
        c = p1c[i % 3]
        gw = [('V', v_s1[i])]
        if i >= 3:
            gw.append(('V', v_m2[i - 3]))
        g_m[i] = sc.op('g', (lambda c=c: (lambda e: e.tensor_scalar(
            c['m'], c['s1'], INV_D, None, ALU.mult)))(), waits=gw, inc='G')
        vw = [('G', g_m[i])]
        if i >= 3:
            vw.append(('G', g_var[i - 3]))
        v_m2[i] = sc.op('v', (lambda c=c: (lambda e: e.tensor_scalar(
            c['m2'], c['s1'], INV_D, c['m'], ALU.mult, ALU.mult)))(), waits=vw, inc='V')
        gw = [('V', v_m2[i]), ('S', s_sqa[i])]
        if i >= 3:
            gw.append(('S', s_sd[i - 3]))
        g_var[i] = sc.op('g', (lambda c=c: (lambda e: e.tensor_scalar(
            c['var'], c['sq'], INV_D, c['m2'], ALU.mult, ALU.subtract)))(), waits=gw, inc='G')

    def emit_p1_B2(i):
        par = i % 3
        xs = x3[par]
        c = p1c[par]
        sw = [('G', g_var[i])]
        if i >= 3:
            sw.append(('V', v_rstd[i - 3]))
        s_sd[i] = sc.op('s', (lambda c=c: (lambda e: e.activation(
            c['sd'], c['var'], AF.Sqrt, bias=eps_c)))(), waits=sw, inc='S')
        vw = [('S', s_sd[i])]
        if i >= 3:
            vw += [('G', g_bm[i - 3]), ('S', s_lastx[i - 3])]
        v_rstd[i] = sc.op('v', (lambda c=c: (lambda e: e.reciprocal(c['rstd'], c['sd'])))(),
                          waits=vw, inc='V')
        gw = [('V', v_rstd[i])]
        if i >= 3:
            gw.append(('S', s_lastx[i - 3]))
        g_bm[i] = sc.op('g', (lambda c=c: (lambda e: e.tensor_scalar(
            c['bm'], c['m'], -1.0, c['rstd'], ALU.mult, ALU.mult)))(), waits=gw, inc='G')
        s_ln1 = sc.op('s', (lambda xs=xs, c=c: (lambda e: e.activation(
            ln1_sb, xs, AF.Identity, scale=c['rstd'], bias=c['bm'])))(),
            waits=[('G', g_bm[i])], inc='S')
        s_lastx[i] = s_ln1
        if i + 3 < NT:
            emit_xload_p1(i + 3)
        return s_ln1

    def emit_p1_geo(i, s_ln1):
        par = i % 2
        tw = [('S', s_ln1), ('V', v_evA[par]), dq_const, ('V', v_scale_done[0])]
        tt = 0
        for c in range(HC):
            tt = sc.op('t', (lambda c=c, par=par: (lambda e: e.transpose(
                psA[par][:, c * 128:(c + 1) * 128], ln1_sb[:, c * 128:(c + 1) * 128], ident)))(),
                waits=tw if c == 0 else (), inc='T')
        v_evA[par] = sc.op('v', (lambda par=par: (lambda e: e.tensor_copy(ln1T_sb, psA[par])))(),
                           waits=[('T', tt)], inc='V')
        tw = [('V', v_evA[par]), dg_attnw,
              ('V', v_evQ[par]), ('S', s_evK[par]), ('V', v_evV[par])]
        tt = 0
        for c in range(HC):
            tt = sc.op('t', (lambda c=c, par=par: (lambda e: e.matmul(
                Qps[par], wq_sb[:, c * 128:(c + 1) * 128], ln1T_sb[:, c * 128:(c + 1) * 128],
                start=(c == 0), stop=(c == HC - 1))))(), waits=tw if c == 0 else (), inc='T')
        for c in range(HC):
            tt = sc.op('t', (lambda c=c, par=par: (lambda e: e.matmul(
                Kps[par], wk_sb[:, c * 128:(c + 1) * 128], ln1T_sb[:, c * 128:(c + 1) * 128],
                start=(c == 0), stop=(c == HC - 1))))(), inc='T')
        for c in range(HC):
            tt = sc.op('t', (lambda c=c, par=par: (lambda e: e.matmul(
                Vps[par], ln1T_sb[:, c * 128:(c + 1) * 128], wv_sb[:, c * 128:(c + 1) * 128],
                start=(c == 0), stop=(c == HC - 1))))(), inc='T')
        v_evQ[par] = sc.op('v', (lambda i=i, par=par: (lambda e: e.tensor_copy(
            qres[:, i * 128:(i + 1) * 128], Qps[par])))(), waits=[('T', tt)], inc='V')
        s_evK[par] = sc.op('s', (lambda i=i, par=par: (lambda e: e.activation(
            kres[:, i * 128:(i + 1) * 128], Kps[par], AF.Identity)))(), waits=[('T', tt)], inc='S')
        v_evV[par] = sc.op('v', (lambda i=i, par=par: (lambda e: e.tensor_copy(
            vres[:, i * 129:i * 129 + 128], Vps[par])))(), inc='V')

    v_scale_done = [0]
    emit_xload_p1(0)
    emit_xload_p1(1)
    emit_xload_p1(2)
    emit_p1_A(0)
    emit_p1_A(1)
    emit_p1_B1(0)
    w1_next = 0
    for i in range(NT):
        s_ln1x = emit_p1_B2(i)
        if i + 1 < NT:
            emit_p1_B1(i + 1)
        if i + 2 < NT:
            emit_p1_A(i + 2)
        emit_p1_geo(i, s_ln1x)
        for _ in range(3):
            if w1_next < len(wpieces):
                emit_w1_piece(w1_next)
                w1_next += 1
    while w1_next < len(wpieces):
        emit_w1_piece(w1_next)
        w1_next += 1
    v_vones = sc.op('v', lambda e: e.memset(vres[:, 128::129], 1.0), inc='V')
    p1_V = sc.cnt['V']; p1_S = sc.cnt['S']; p1_T = sc.cnt['T']

    def emit_scales():
        vsd = 0
        wcols = [wsums[:, 0:32], wsums[:, 32:64], wsums[:, 64:96]]
        wlast = [s_abs[31], s_abs[63], s_abs[95]]
        for wi in range(3):
            cols = wcols[wi]
            tot_col = st[:, 28:29]
            vv = sc.op('v', (lambda cols=cols, tot_col=tot_col: (lambda e: e.reduce_sum(
                tot_col, cols, axis=AX)))(), waits=[wlast[wi], s_abs[30 + 32 * wi], ('S', p1_S)], inc='V')
            tt = sc.op('t', (lambda tot_col=tot_col: (lambda e: e.matmul(
                pb[0][0:1, 0:1], tot_col, ones_c, start=True, stop=True)))(),
                waits=[('V', vv), ('V', v_evA[0]), ('V', v_evA[1])], inc='T')
            vv = sc.op('v', lambda e: e.tensor_scalar(tot_sb[:, 0:1], pb[0][0:1, 0:1], INV_WN, EPS, ALU.mult, ALU.max),
                       waits=[('T', tt)], inc='V')
            sc.op('v', lambda e: e.drain())
            sc.op('v', lambda e: e.reciprocal(tot_sb[:, 1:2], tot_sb[:, 0:1]))
            sc.op('v', lambda e: e.drain())
            vv = sc.op('v', lambda e: e.tensor_scalar(tot_sb[:, 2:3], tot_sb[:, 0:1], 1.0 / 127.0, None, ALU.mult), inc='V')
            tt = sc.op('t', lambda e: e.matmul(pb[1][0:128, 0:1], ones_r, tot_sb[:, 1:2], start=True, stop=True),
                       waits=[('V', vv), ('V', v_evQ[0]), ('S', s_evK[0])], inc='T')
            tt = sc.op('t', lambda e: e.matmul(pb[1][0:128, 512:513], ones_r, tot_sb[:, 2:3], start=True, stop=True),
                       inc='T')
            s_col = wsc[:, 2 * wi:2 * wi + 1]
            d_col = wsc[:, 2 * wi + 1:2 * wi + 2]
            vv = sc.op('v', (lambda s_col=s_col: (lambda e: e.tensor_copy(s_col, pb[1][0:128, 0:1])))(),
                       waits=[('T', tt)], inc='V')
            vsd = sc.op('v', (lambda d_col=d_col: (lambda e: e.tensor_copy(d_col, pb[1][0:128, 512:513])))(),
                        inc='V')
        v_scale_done[0] = vsd

    swg_c = wsc[:, 0:1]; wgdq_c = wsc[:, 1:2]
    swu_c = wsc[:, 2:3]; wudq_c = wsc[:, 3:4]
    swd_c = wsc[:, 4:5]; wddq_c = wsc[:, 5:6]

    p2pieces = []
    for wi, wd_ap in ((0, wgt_d), (1, wut_d)):
        for c in range(HC):
            for hh in range(2):
                dst = (wgq if wi == 0 else wuq)[:, c * INNER + hh * 2048: c * INNER + (hh + 1) * 2048]
                p2pieces.append((wi, wd_ap[c * 128:(c + 1) * 128, hh * 2048:(hh + 1) * 2048], dst, 2048))
    for p in range(OC):
        p2pieces.append((2, wdt_d[p * 128:(p + 1) * 128, :], p, 1024))

    s_qdone = [0] * len(p2pieces)
    ds_wdq = [0] * len(p2pieces)

    def emit_w2_piece(idx):
        wi, src, dst, wid = p2pieces[idx]
        s_col = wsc[:, 2 * wi:2 * wi + 1]
        par = idx % 4
        dstap = w2stages[par][:, 0:wid]
        if idx < 4:
            w = [('S', p1_S), ('V', p1_V), ('T', p1_T)]
        else:
            w = [('S', s_qdone[idx - 4])]
        v = sc.op('q', (lambda src=src, dstap=dstap: (lambda e: e.dma_start(out=dstap, in_=src)))(),
                  waits=w, inc=w2sems[par], n=16)
        vw = [(w2sems[par], v), ('V', v_konst), ('V', v_scale_done[0])]
        v_q = sc.op('v', (lambda dstap=dstap, s_col=s_col: (lambda e: e.tensor_scalar(
            dstap, dstap, s_col, CLIP_B, ALU.mult, ALU.min)))(), waits=vw, inc='V')
        sc.op('v', lambda e: e.drain())
        v_q = sc.op('v', (lambda dstap=dstap: (lambda e: e.tensor_scalar(
            dstap, dstap, -CLIP_B, cb_c, ALU.max, ALU.add)))(), inc='V')
        if wi < 2:
            s_qdone[idx] = sc.op('s', (lambda dstap=dstap, dst=dst: (lambda e: e.activation(
                dst, dstap, AF.Identity, bias=ng_c)))(), waits=[('V', v_q)], inc='S')
        else:
            pdn = dst
            wds = [('V', v_q)]
            if idx >= 33:
                wds.append(('DSW', ds_wdq[idx - 1]))
            s_qdone[idx] = sc.op('s', (lambda dstap=dstap: (lambda e: e.activation(
                wdq_st, dstap, AF.Identity, bias=ng_c)))(), waits=wds, inc='S')
            ds_wdq[idx] = sc.op('s', (lambda pdn=pdn: (lambda e: e.dma_start(
                out=wdqh_d[pdn], in_=wdq_st)))(),
                waits=[('S', s_qdone[idx])], inc='DSW', n=16)

    stv4 = [pb[0][:, 0:512], pb[0][:, 512:1024], pb[1][:, 0:512], pb[1][:, 512:1024]]
    oaug = [pb[2][:, 0:132], pb[2][:, 512:644], pb[3][:, 0:132], pb[3][:, 512:644]]
    wops = pb[1]
    NKP = 4

    s_exp_last = [0, 0, 0, 0]
    t_av_last = [0, 0, 0, 0]
    dso_last = [0]
    dso_g = [0] * (NT // GRP)
    t_attn_pt = [0]
    dg_xpa = [None] * NT
    v_x1add = [0] * NT
    ds_x1 = [None] * NT
    s_otev_last = 0
    s_oev_last = 0
    v_rec_last = 0

    v_stats = [0] * NT
    s_stats = [0] * NT

    x4 = [x2_st[:, 0:1024], x2_st[:, 1024:2048], wst2[:, 0:1024], wst2[:, 1024:2048]]
    x4sem = ['DGX0', 'DGX1', 'DGX2', 'DGX3']

    def emit_xload_pa(i):
        par = i % 4
        if i >= 4:
            w = [ds_x1[i - 4], ('V', v_stats[i - 4]), ('S', s_stats[i - 4])]
        else:
            w = [('S', s_qdone[-1] if s_qdone[-1] else 0), ds_wd_done]
        v = sc.op('g', (lambda i=i, par=par: (lambda e: e.dma_start(
            out=x4[par], in_=x_d[i * 128:(i + 1) * 128, :])))(), waits=w, inc=x4sem[par], n=16)
        dg_xpa[i] = (x4sem[par], v)

    def emit_pa1_group(g):
        nonlocal s_otev_last, s_oev_last, v_rec_last
        g0 = g * GRP
        nk = g0 + GRP
        t_av_pt = [0, 0, 0]
        pend = []

        def emit_av(rec):
            p0, pp0, pt0, sexp0, gm0 = rec
            tw = [('S', sexp0)]
            if gm0:
                tw.append(('G', gm0))
            first = True
            tt = 0
            for d in range(2):
                k = 2 * p0 + d
                q_lo = max(k, g0)
                off = d * 512 + (q_lo - g0) * 128
                for j in range(g0 + GRP - q_lo):
                    q = q_lo + j
                    tt = sc.op('t', (lambda q=q, k=k, pt0=pt0, g0=g0, j=j, off=off: (lambda e: e.matmul(
                        oaug[q - g0][:, 0:129],
                        pt_row[pt0][:, off + j * 128: off + (j + 1) * 128],
                        vres[:, k * 129: (k + 1) * 129],
                        start=(k == 0), stop=(k == q))))(), waits=tw if first else (), inc='T')
                    first = False
            t_av_pt[pt0] = tt

        for p in range(nk // 2):
            pp = p % 2
            ptx = p % 3
            pbst = pb[pp]
            tw0 = [('S', s_exp_last[pp]), ('V', p1_V), ('S', p1_S)]
            if g == 0 and p == 0:
                tw0 += [('V', v_vones), ('V', v_scale_done[0])]
            if p == 0:
                tw0 += [('S', s_oev_last), ('V', v_rec_last)]
            first = True
            tt = 0
            for d in range(2):
                k = 2 * p + d
                q_lo = max(k, g0)
                off = d * 512 + (q_lo - g0) * 128
                for j in range(g0 + GRP - q_lo):
                    q = q_lo + j
                    tt = sc.op('t', (lambda k=k, q=q, pbst=pbst, off=off, j=j: (lambda e: e.matmul(
                        pbst[:, off + j * 128: off + (j + 1) * 128],
                        kres[:, k * 128:(k + 1) * 128], qres[:, q * 128:(q + 1) * 128],
                        start=True, stop=True)))(), waits=tw0 if first else (), inc='T')
                    first = False
            sw = [('T', tt), ('T', t_av_pt[ptx])]
            if ptx == 2:
                sw.append(('DSO', dso_last[0]))
            if 2 * p >= g0:
                lo0 = (2 * p - g0) * 128
                lo1 = 512 + (2 * p + 1 - g0) * 128
                sc.op('s', (lambda pp=pp, ptx=ptx, lo0=lo0: (lambda e: e.activation(
                    pt_row[ptx][:, lo0:512], pb[pp][:, lo0:512],
                    AF.Exp, scale=smsc_c)))(), waits=sw, inc='S')
                s_exp_last[pp] = sc.op('s', (lambda pp=pp, ptx=ptx, lo1=lo1: (lambda e: e.activation(
                    pt_row[ptx][:, lo1:1024], pb[pp][:, lo1:1024],
                    AF.Exp, scale=smsc_c)))(), inc='S')
            else:
                s_exp_last[pp] = sc.op('s', (lambda pp=pp, ptx=ptx: (lambda e: e.activation(
                    pt_row[ptx][:, 0:1024], pb[pp][:, 0:1024],
                    AF.Exp, scale=smsc_c)))(), waits=sw, inc='S')
            gmask = 0
            for d in range(2):
                k = 2 * p + d
                if k >= g0:
                    gmask = sc.op('g', (lambda ptx=ptx, k=k, g0=g0, d=d: (lambda e: e.tensor_mul(
                        pt_row[ptx][:, d * 512 + (k - g0) * 128: d * 512 + (k - g0 + 1) * 128],
                        pt_row[ptx][:, d * 512 + (k - g0) * 128: d * 512 + (k - g0 + 1) * 128],
                        mask)))(), waits=[('S', s_exp_last[pp]), dq_const], inc='G')
            pend.append((p, pp, ptx, s_exp_last[pp], gmask))
            if len(pend) > 2:
                emit_av(pend.pop(0))
        while pend:
            emit_av(pend.pop(0))
        t_attn_done = sc.cnt['T']
        v_rec = 0
        for j in range(GRP):
            v_rec = sc.op('v', (lambda j=j: (lambda e: e.reciprocal(
                rq_sb[:, j:j + 1], oaug[j][:, 128:129])))(),
                waits=[('T', t_attn_done)] if j == 0 else (), inc='V')
        v_rec_last = v_rec
        s_oev = 0
        for j in range(GRP):
            s_oev = sc.op('s', (lambda j=j: (lambda e: e.activation(
                o_sb[:, j * 128:(j + 1) * 128], oaug[j][:, 0:128], AF.Identity,
                scale=rq_sb[:, j:j + 1])))(), waits=[('V', v_rec)], inc='S')
        s_oev_last = s_oev

        dso_last[0] = sc.op('s', (lambda g=g: (lambda e: e.dma_start(
            out=oh_d[g], in_=o_sb)))(), waits=[('S', s_oev)], inc='DSO', n=16)
        dso_g[g] = dso_last[0]

    emit_scales()
    w2_next = 0
    for g in range(NT // GRP):
        emit_pa1_group(g)
        for _ in range(8):
            if w2_next < len(p2pieces):
                emit_w2_piece(w2_next)
                w2_next += 1
    while w2_next < len(p2pieces):
        emit_w2_piece(w2_next)
        w2_next += 1
    s_wq_done = s_qdone[31]
    ds_wd_done = ('DSW', ds_wdq[-1])

    t_attn_pt[0] = sc.cnt['T']
    emit_xload_pa(0)
    emit_xload_pa(1)
    emit_xload_pa(2)
    emit_xload_pa(3)
    t_attn_end = sc.cnt['T']; s_attn_end = sc.cnt['S']
    g_attn_end = sc.cnt['G']; v_attn_end = sc.cnt['V']
    o_in = [pt_row[0][:, 0:512], pt_row[1][:, 0:512]]
    s1_p = [s1_c, wsums[:, 0:1]]
    sq_p = [sq_c, wsums[:, 1:2]]
    m2_p = [m2_c, wsums[:, 2:3]]
    sqscr = [ln_scr, vres[:, 0:D]]
    dqo = [None] * (NT // GRP)
    t_ot = [0] * (NT // GRP)
    v_otev_g = [0] * (NT // GRP)
    wo_free = {1: 0, 2: 0, 3: 0}

    def emit_oload(g):
        par = g % 2
        w = [('DSO', dso_g[g]), ('T', t_attn_end), ('S', s_attn_end), ('G', g_attn_end)]
        if g >= 2:
            w.append(('T', t_ot[g - 2]))
        v = sc.op('q', (lambda g=g, par=par: (lambda e: e.dma_start(
            out=o_in[par], in_=oh_d[g])))(), waits=w, inc='DQO%d' % par, n=16)
        dqo[g] = ('DQO%d' % par, v)

    emit_oload(0)
    emit_oload(1)
    for g in range(NT // GRP):
        g0 = g * GRP
        par = g % 2
        tw = [dqo[g], ('V', v_otev_g[g - 1] if g >= 1 else 0)]
        if g == 0:
            tw += [('S', s_attn_end), ('V', v_attn_end)]
        tt = 0
        for j in range(GRP):
            tt = sc.op('t', (lambda j=j, par=par: (lambda e: e.transpose(
                pb[0].bitcast(BF16)[:, j * 128:(j + 1) * 128],
                o_in[par][:, j * 128:(j + 1) * 128], ident)))(),
                waits=tw if j == 0 else (), inc='T')
        t_ot[g] = tt
        v_otev_g[g] = sc.op('v', lambda e: e.tensor_copy(ot_sb, pb[0].bitcast(BF16)[:, 0:512]),
                            waits=[('T', tt)], inc='V')
        if g + 2 < NT // GRP:
            emit_oload(g + 2)
        wo_banks = [1, 2, 3, 1]

        def emit_stats(i):
            xb = x4[i % 4]
            jp = i % 2
            sc.op('v', lambda e: e.drain())
            sc.op('v', (lambda xb=xb, jp=jp: (lambda e: e.reduce_sum(
                s1_p[jp], xb, axis=AX)))(), waits=[('V', v_x1add[i])], inc='V')
            sw = [('V', v_x1add[i])]
            if i >= 2:
                sw.append(('S', s_stats[i - 2]))
            s_stats[i] = sc.op('s', (lambda xb=xb, jp=jp: (lambda e: e.activation(
                sqscr[jp], xb, AF.Square, accum_out=sq_p[jp])))(),
                waits=sw, inc='S')
            sc.op('v', lambda e: e.drain())
            sc.op('v', (lambda i=i, jp=jp: (lambda e: e.tensor_scalar(
                m_all[:, i:i + 1], s1_p[jp], INV_D, None, ALU.mult)))(), inc='V')
            sc.op('v', lambda e: e.drain())
            sc.op('v', (lambda i=i, jp=jp: (lambda e: e.tensor_mul(
                m2_p[jp], m_all[:, i:i + 1], m_all[:, i:i + 1])))(), inc='V')
            sc.op('v', lambda e: e.drain())
            v_stats[i] = sc.op('v', (lambda i=i, jp=jp: (lambda e: e.tensor_scalar(
                var_all[:, i:i + 1], sq_p[jp], INV_D, m2_p[jp], ALU.mult, ALU.subtract)))(),
                waits=[('S', s_stats[i])], inc='V')
            if i + 4 < NT:
                emit_xload_pa(i + 4)

        for j in range(GRP):
            i = g0 + j
            wb = wo_banks[j]
            wob = pb[wb]
            tw = [('V', v_otev_g[g]), ('V', wo_free[wb])]
            if g == 0 and j == 0:
                tw += [('S', s_attn_end), ('V', v_attn_end)]
            tt = sc.op('t', (lambda j=j, wob=wob: (lambda e: e.matmul(
                wob[:, 0:512], ot_sb[:, j * 128:(j + 1) * 128], wo_sb[:, 0:512],
                start=True, stop=True)))(), waits=tw, inc='T')
            tt = sc.op('t', (lambda j=j, wob=wob: (lambda e: e.matmul(
                wob[:, 512:1024], ot_sb[:, j * 128:(j + 1) * 128], wo_sb[:, 512:1024],
                start=True, stop=True)))(), inc='T')
            pari = i % 4
            xb = x4[pari]
            vw = [('T', tt), dg_xpa[i]]
            vv = sc.op('v', (lambda xb=xb, wob=wob: (lambda e: e.tensor_add(
                xb[:, 0:512], wob[:, 0:512], xb[:, 0:512])))(), waits=vw, inc='V')
            vv = sc.op('v', (lambda xb=xb, wob=wob: (lambda e: e.tensor_add(
                xb[:, 512:1024], wob[:, 512:1024], xb[:, 512:1024])))(), inc='V')
            v_x1add[i] = vv
            wo_free[wb] = vv
            v = sc.op('s', (lambda i=i, xb=xb: (lambda e: e.dma_start(
                out=x1h_d[i * 128:(i + 1) * 128, :], in_=xb)))(),
                waits=[('V', vv)], inc='DSX%d' % pari, n=16)
            ds_x1[i] = ('DSX%d' % pari, v)
            if j >= 1:
                emit_stats(i - 1)
        emit_stats(g0 + GRP - 1)

    ss = sc.op('s', lambda e: e.activation(var_all, var_all, AF.Sqrt, bias=eps_c),
               waits=[('V', v_stats[NT - 1])], inc='S')
    vv = sc.op('v', lambda e: e.reciprocal(rstd_all, var_all), waits=[('S', ss)], inc='V')
    sc.op('v', lambda e: e.drain())
    vv = sc.op('v', lambda e: e.tensor_mul(bm_all, m_all, rstd_all), inc='V')
    sc.op('v', lambda e: e.drain())
    v_lnstats = sc.op('v', lambda e: e.tensor_scalar(bm_all, bm_all, -1.0, None, ALU.mult), inc='V')
    pa1_V = sc.cnt['V']; pa1_S = sc.cnt['S']

    s2ctx.close()
    s3b = contextlib.ExitStack()
    h_sb = s3b.enter_context(nc.sbuf_tensor("h_sb", [128, INNER], F32)).ap()
    g_s = kres.bitcast(F32)[:, 0:1024]
    xq2 = qres[:, 2048:3072]
    xqTb = [qres[:, 0:1024], qres[:, 1024:2048]]

    dq_x1b = [None] * NT
    s_ln2t = [0] * NT
    s_xq = [0] * NT
    v_rnd1 = [0] * NT
    t_xqA = [0] * NT
    t_E = [0] * NT
    s_silu_q = {}
    v_stt_q = {}
    s_nqh_t = [0] * NT
    v_nqev = [0] * NT
    ds_nq = [None] * NT

    def emit_x1load(i):
        par = i % 2
        w = [ds_x1[i]]
        if i >= 2:
            w.append(('S', s_ln2t[i - 2]))
        else:
            w += [('V', pa1_V), ('S', pa1_S), ds_x1[NT - 4], ds_x1[NT - 3]]
        v = sc.op('q', (lambda i=i, par=par: (lambda e: e.dma_start(
            out=x_st[par], in_=x1h_d[i * 128:(i + 1) * 128, :])))(), waits=w, inc='DQX%d' % par, n=16)
        dq_x1b[i] = ('DQX%d' % par, v)

    def chain_blk1(i):
        par = i % 2
        sw = [dq_x1b[i], ('S', pa1_S)]
        if i >= 1:
            sw.append(('V', v_rnd1[i - 1]))
        s_ln2t[i] = sc.op('s', (lambda par=par, i=i: (lambda e: e.activation(
            ln_scr, x_st[par], AF.Identity, scale=rstd_all[:, i:i + 1],
            bias=bm_all[:, i:i + 1])))(), waits=sw, inc='S')

    def chain_blk2(i):
        sc.op('v', lambda e: e.drain())
        vv = sc.op('v', lambda e: e.reduce_max(mx_c, ln_scr, axis=AX, apply_absolute_value=True),
                   waits=[('S', s_ln2t[i]), ('V', v_lnstats)], inc='V')
        sc.op('v', lambda e: e.drain())
        vv = sc.op('v', lambda e: e.tensor_scalar(mxc_c, mx_c, EPS, None, ALU.max), inc='V')
        sc.op('v', lambda e: e.drain())
        vv = sc.op('v', lambda e: e.reciprocal(r_c, mxc_c), inc='V')

    def chain_blk3(i):
        sc.op('v', lambda e: e.drain())
        sc.op('v', lambda e: e.tensor_scalar(s1q_c, r_c, 127.0, None, ALU.mult), inc='V')
        sc.op('v', (lambda i=i: (lambda e: e.tensor_mul(c1g_p[i % 2], mxc_c, wgdq_c)))(), inc='V')
        sc.op('v', (lambda i=i: (lambda e: e.tensor_mul(c1u_p[i % 2], mxc_c, wudq_c)))(), inc='V')
        sc.op('v', lambda e: e.drain())
        vv = sc.op('v', lambda e: e.tensor_scalar(ln_scr, ln_scr, s1q_c, cb_c, ALU.mult, ALU.add), inc='V')
        v_rnd1[i] = vv
        sw = [('V', vv), ('S', s_wq_done)]
        if i >= 1:
            sw.append(('T', t_xqA[i - 1]))
        else:
            sw += [ds_x1[NT - 2], ds_x1[NT - 1], ds_wd_done]
        s_xq[i] = sc.op('s', lambda e: e.activation(xq2, ln_scr, AF.Identity, bias=ng_c),
                        waits=sw, inc='S')
        if i + 2 < NT:
            emit_x1load(i + 2)

    emit_x1load(0)
    emit_x1load(1)
    chain_blk1(0)
    chain_blk2(0)
    chain_blk3(0)

    def emit_ychain(i):
        sc.op('v', lambda e: e.drain())
        sc.op('v', lambda e: e.tensor_max(mh4[:, 0:2], mh4[:, 0:2], mh4[:, 2:4]), inc='V')
        sc.op('v', lambda e: e.drain())
        sc.op('v', lambda e: e.tensor_max(mh_c, mh4[:, 0:1], mh4[:, 1:2]), inc='V')
        sc.op('v', lambda e: e.drain())
        sc.op('v', lambda e: e.tensor_scalar(mhc_c, mh_c, EPS, None, ALU.max), inc='V')
        sc.op('v', lambda e: e.drain())
        sc.op('v', lambda e: e.reciprocal(rh_c, mhc_c), inc='V')
        sc.op('v', lambda e: e.drain())
        sc.op('v', lambda e: e.tensor_scalar(s2q_c, rh_c, 127.0, None, ALU.mult), inc='V')
        sc.op('v', (lambda i=i: (lambda e: e.tensor_mul(c2_all[:, i:i + 1], mhc_c, wddq_c)))(), inc='V')
        sc.op('v', lambda e: e.drain())
        vv = sc.op('v', (lambda i=i: (lambda e: e.tensor_scalar(
            h_sb, h_sb, s2q_c, cb_c, ALU.mult, ALU.add)))(),
            waits=[('V', v_stt_q[(i, 3)])], inc='V')
        s_nqh_t[i] = sc.op('s', lambda e: e.activation(nqh_sb, h_sb, AF.Identity, bias=ng_c),
                           waits=[('V', vv)], inc='S')

    pair_free = {0: (0, v_x1add[NT - 1]), 1: (0, v_x1add[NT - 1])}
    v_xqt_arr = [0] * NT
    gbank = {0: pb[0], 1: pb[2]}
    ubank = {0: pb[1], 1: pb[3]}

    def emit_Ablock(j):
        apair = j % 2
        aps = pb[0] if apair == 0 else pb[2]
        tw = [('S', s_xq[j]), dq_const,
              ('S', pair_free[apair][0]), ('V', pair_free[apair][1])]
        if j == 0:
            tw += [('V', v_otev_g[NT // GRP - 1]), ('V', v_x1add[NT - 1])]
        tt = 0
        for c in range(HC):
            tt = sc.op('t', (lambda c=c, aps=aps: (lambda e: e.transpose(
                aps.bitcast(BF16)[:, c * 128:(c + 1) * 128],
                xq2[:, c * 128:(c + 1) * 128], ident)))(),
                waits=tw if c == 0 else (), inc='T')
        t_xqA[j] = tt
        vw = [('T', tt)]
        if j < 2:
            vw += [ds_x1[NT - 2], ds_x1[NT - 1], ds_wd_done]
        v_xqt_arr[j] = sc.op('v', (lambda j=j, aps=aps: (lambda e: e.tensor_copy(
            xqTb[j % 2], aps.bitcast(BF16)[:, 0:1024])))(), waits=vw, inc='V')

    emit_Ablock(0)
    for i in range(NT):
        par = i % 2
        qpair = [0, 1, 1, 0] if i % 2 == 0 else [1, 0, 0, 1]
        fpair = 0 if i % 2 == 0 else 1
        v_xqt = v_xqt_arr[i]
        if i >= 1:
            emit_ychain(i - 1)

        for qtr in range(4):
            pr = qpair[qtr]
            gb = gbank[pr]
            ub = ubank[pr]
            tw = [('S', pair_free[pr][0]), ('V', pair_free[pr][1])]
            if qtr == 0:
                tw += [('V', v_xqt), ('S', s_wq_done)]
                if i == 0:
                    tw += [('V', v_x1add[NT - 1])]
            first = True
            tt = 0
            for c in range(HC):
                if qtr == 3 and c == 5 and i + 1 < NT:
                    emit_Ablock(i + 1)
                for half in range(2):
                    o0 = qtr * 1024 + half * 512
                    tt = sc.op('t', (lambda c=c, o0=o0, gb=gb, half=half, par=par: (lambda e: e.matmul(
                        gb[:, half * 512:(half + 1) * 512],
                        xqTb[par][:, c * 128:(c + 1) * 128],
                        wgq[:, c * INNER + o0: c * INNER + o0 + 512],
                        start=(c == 0), stop=(c == HC - 1))))(),
                        waits=tw if first else (), inc='T')
                    first = False
                    tt = sc.op('t', (lambda c=c, o0=o0, ub=ub, half=half, par=par: (lambda e: e.matmul(
                        ub[:, half * 512:(half + 1) * 512],
                        xqTb[par][:, c * 128:(c + 1) * 128],
                        wuq[:, c * INNER + o0: c * INNER + o0 + 512],
                        start=(c == 0), stop=(c == HC - 1))))(), inc='T')
            if qtr == 3:
                t_E[i] = tt
            sw = [('T', tt)]
            if qtr >= 1:
                sw.append(('V', v_stt_q[(i, qtr - 1)]))
            s_silu_q[(i, qtr)] = sc.op('s', (lambda gb=gb, i=i: (lambda e: e.activation(
                g_s, gb, AF.Silu, scale=c1g_p[i % 2])))(), waits=sw, inc='S')
            v_stt_q[(i, qtr)] = sc.op('v', (lambda qtr=qtr, ub=ub, i=i: (lambda e: e.scalar_tensor_tensor(
                h_sb[:, qtr * 1024:(qtr + 1) * 1024], ub, c1u_p[i % 2], g_s, ALU.mult, ALU.mult)))(),
                waits=[('S', s_silu_q[(i, qtr)])], inc='V')
            pair_free[pr] = (s_silu_q[(i, qtr)], v_stt_q[(i, qtr)])
            sc.op('v', lambda e: e.drain())
            sc.op('v', (lambda qtr=qtr: (lambda e: e.reduce_max(
                mh4[:, qtr:qtr + 1], h_sb[:, qtr * 1024:(qtr + 1) * 1024],
                axis=AX, apply_absolute_value=True)))(), inc='V')

            if i + 1 < NT:
                if qtr == 0:
                    chain_blk1(i + 1)
                elif qtr == 2:
                    chain_blk3(i + 1)
            if qtr == 1 and i >= 1:
                fg = gbank[fpair]
                fu = ubank[fpair]
                tw2 = [('S', s_nqh_t[i - 1]),
                       ('S', pair_free[fpair][0]), ('V', pair_free[fpair][1])]
                tt2 = 0
                for c in range(OC):
                    pbx = fg if c < 16 else fu
                    coff = (c % 16) * 128
                    tt2 = sc.op('t', (lambda c=c, pbx=pbx, coff=coff: (lambda e: e.transpose(
                        pbx.bitcast(BF16)[:, coff:coff + 128],
                        nqh_sb[:, c * 128:(c + 1) * 128], ident)))(),
                        waits=tw2 if c == 0 else (), inc='T')
                sw2 = [('T', tt2)]
                if i >= 2:
                    sw2.append(ds_nq[i - 2])
                ss2 = sc.op('s', (lambda fg=fg: (lambda e: e.activation(
                    nqt_sb[:, 0:2048], fg.bitcast(BF16)[:, 0:2048], AF.Identity)))(),
                    waits=sw2, inc='S')
                ss2 = sc.op('s', (lambda fu=fu: (lambda e: e.activation(
                    nqt_sb[:, 2048:4096], fu.bitcast(BF16)[:, 0:2048], AF.Identity)))(),
                    inc='S')
                s_nqev = ss2
                v_nqev[i - 1] = 0
                pair_free[fpair] = (s_nqev, pair_free[fpair][1])
                v2 = sc.op('s', (lambda i=i: (lambda e: e.dma_start(
                    out=nqh_d[i - 1], in_=nqt_sb)))(), waits=[('S', s_nqev)], inc='DSN', n=16)
                ds_nq[i - 1] = ('DSN', v2)
            if qtr == 1 and i + 1 < NT:
                chain_blk2(i + 1)
    emit_ychain(NT - 1)

    tw2 = [('S', s_nqh_t[NT - 1]), ('S', pair_free[0][0]), ('V', pair_free[0][1])]
    tt2 = 0
    for c in range(OC):
        pbx = pb[0] if c < 16 else pb[1]
        coff = (c % 16) * 128
        tt2 = sc.op('t', (lambda c=c, pbx=pbx, coff=coff: (lambda e: e.transpose(
            pbx.bitcast(BF16)[:, coff:coff + 128],
            nqh_sb[:, c * 128:(c + 1) * 128], ident)))(),
            waits=tw2 if c == 0 else (), inc='T')
    sw2 = [('T', tt2), ds_nq[NT - 2]]
    ss2 = sc.op('s', lambda e: e.activation(nqt_sb[:, 0:2048], pb[0].bitcast(BF16)[:, 0:2048], AF.Identity),
                waits=sw2, inc='S')
    ss2 = sc.op('s', lambda e: e.activation(nqt_sb[:, 2048:4096], pb[1].bitcast(BF16)[:, 0:2048], AF.Identity),
                inc='S')
    v2 = sc.op('s', lambda e: e.dma_start(out=nqh_d[NT - 1], in_=nqt_sb),
               waits=[('S', ss2)], inc='DSN', n=16)
    ds_nq[NT - 1] = ('DSN', v2)
    pa2_V = sc.cnt['V']; pa2_S = sc.cnt['S']; pa2_G = sc.cnt['G']

    s3b.close()
    s1ctx.close()
    s4ctx = contextlib.ExitStack()
    wdq_sb = s4ctx.enter_context(nc.sbuf_tensor("wdq_sb", [128, OC * 1024], BF16)).ap()
    nqt_in = [s4ctx.enter_context(nc.sbuf_tensor(f"nqt_in{i}", [128, INNER], BF16)).ap() for i in range(2)]
    x1_in = [s4ctx.enter_context(nc.sbuf_tensor(f"x1_in{i}", [128, D], F32)).ap() for i in range(2)]
    out_sb = [s4ctx.enter_context(nc.sbuf_tensor(f"out_sb{i}", [128, D], F32)).ap() for i in range(2)]

    v = sc.op('q', lambda e: e.dma_start(
        out=wdq_sb.rearrange("p (c d) -> p c d", c=OC),
        in_=wdqh_d.rearrange("c p d -> p c d")),
        waits=[('T', t_E[NT - 1]), ds_wd_done], inc='DQWD', n=16)
    dq_wdq = ('DQWD', v)

    dq_nq = [None] * NT; dq_x1pb = [None] * NT
    v_out = [0] * NT
    ds_y = [None] * NT
    t_down = [0] * NT
    dps = [pb[0], pb[1]]

    def emit_pb_loads(i):
        par = i % 2
        w = [('S', pa2_S), ds_nq[i]]
        if i >= 2:
            w.append(('T', t_down[i - 2]))
        v = sc.op('q', (lambda i=i, par=par: (lambda e: e.dma_start(
            out=nqt_in[par], in_=nqh_d[i])))(), waits=w, inc='DQN%d' % par, n=16)
        dq_nq[i] = ('DQN%d' % par, v)
        w2 = [('S', pa2_S), ('V', pa2_V), ds_x1[i]]
        if i >= 2:
            w2.append(('V', v_out[i - 2]))
        v = sc.op('q', (lambda i=i, par=par: (lambda e: e.dma_start(
            out=x1_in[par], in_=x1h_d[i * 128:(i + 1) * 128, :])))(), waits=w2, inc='DQX%d' % par, n=16)
        dq_x1pb[i] = ('DQX%d' % par, v)

    emit_pb_loads(0)
    emit_pb_loads(1)
    for i in range(NT):
        par = i % 2
        tw = [dq_nq[i], dq_wdq]
        if i < 2:
            tw += [('V', pa2_V), ('S', pa2_S)]
        else:
            tw.append(('V', v_out[i - 2]))
        tt = 0
        for c in range(OC):
            for half in range(2):
                tt = sc.op('t', (lambda c=c, half=half, par=par: (lambda e: e.matmul(
                    dps[par][:, half * 512:(half + 1) * 512],
                    nqt_in[par][:, c * 128:(c + 1) * 128],
                    wdq_sb[:, c * 1024 + half * 512: c * 1024 + (half + 1) * 512],
                    start=(c == 0), stop=(c == OC - 1))))(),
                    waits=tw if (c == 0 and half == 0) else (), inc='T')
        t_down[i] = tt
        vw = [('T', tt), dq_x1pb[i]]
        if i >= 2:
            vw.append(ds_y[i - 2])
        vv = sc.op('v', (lambda i=i, par=par: (lambda e: e.scalar_tensor_tensor(
            out_sb[par][:, 0:512], dps[par][:, 0:512], c2_all[:, i:i + 1],
            x1_in[par][:, 0:512], ALU.mult, ALU.add)))(), waits=vw, inc='V')
        vv = sc.op('v', (lambda i=i, par=par: (lambda e: e.scalar_tensor_tensor(
            out_sb[par][:, 512:1024], dps[par][:, 512:1024], c2_all[:, i:i + 1],
            x1_in[par][:, 512:1024], ALU.mult, ALU.add)))(), inc='V')
        v_out[i] = vv
        v = sc.op('s', (lambda i=i, par=par: (lambda e: e.dma_start(
            out=y_d[i * 128:(i + 1) * 128, :], in_=out_sb[par])))(),
            waits=[('V', vv)], inc='DSY%d' % par, n=16)
        ds_y[i] = ('DSY%d' % par, v)
        if i + 2 < NT:
            emit_pb_loads(i + 2)

    sc.op('q', lambda e: e.nop(), waits=[(nm, sc.cnt[nm]) for nm in sc.cnt if nm.startswith('D')])

    sems = {n: stack.enter_context(nc.semaphore(n)) for n in sc.cnt}
    blk = stack.enter_context(nc.Block())

    def emit(key):
        def body(eng):
            last = {}
            for fn, waits, inc, n in sc.st[key]:
                for s, v in waits:
                    if last.get(s, 0) < v:
                        eng.wait_ge(sems[s], v)
                        last[s] = v
                inst = fn(eng)
                if inc is not None:
                    inst.then_inc(sems[inc], n)
        return body

    blk.tensor(emit('t'))
    blk.vector(emit('v'))
    blk.scalar(emit('s'))
    blk.gpsimd(emit('g'))
    blk.sync(emit('q'))
    s4ctx.close()
    stack.close()
    return nc


def kernel(L0_attn_x, L0_attn_Wq, L0_attn_Wk, L0_attn_Wv, L0_attn_Wo,
           L0_ffn_w_gate, L0_ffn_w_up, L0_ffn_w_down):
    x = np.ascontiguousarray(np.asarray(L0_attn_x, dtype=np.float32))
    assert x.shape == (8, 4096, 1024)
    wq = np.ascontiguousarray(np.asarray(L0_attn_Wq, dtype=np.float32))
    wk = np.ascontiguousarray(np.asarray(L0_attn_Wk, dtype=np.float32))
    wv = np.ascontiguousarray(np.asarray(L0_attn_Wv, dtype=np.float32))
    wo = np.ascontiguousarray(np.asarray(L0_attn_Wo, dtype=np.float32))
    wgt = np.ascontiguousarray(np.asarray(L0_ffn_w_gate, dtype=np.float32).T)
    wut = np.ascontiguousarray(np.asarray(L0_ffn_w_up, dtype=np.float32).T)
    wdt = np.ascontiguousarray(np.asarray(L0_ffn_w_down, dtype=np.float32).T)
    ident = np.eye(128, dtype=ml_dtypes.bfloat16)
    mask = np.triu(np.ones((128, 128), dtype=ml_dtypes.bfloat16))

    nc = _build()
    in_maps = [{"x": x[i], "wq": wq, "wk": wk, "wv": wv, "wo": wo,
                "wgt": wgt, "wut": wut, "wdt": wdt,
                "ident": ident, "mask": mask} for i in range(8)]
    res = run_bass_kernel_spmd(nc, in_maps, core_ids=list(range(8)))
    out = np.stack([np.asarray(res.results[i]["y"]) for i in range(8)], axis=0)
    return out.astype(np.float32)


# version 31
# speedup vs baseline: 1.0620x; 1.0133x over previous
import numpy as np
import ml_dtypes
import concourse.bass as bass
import concourse.mybir as mybir
from concourse.bass_utils import run_bass_kernel_spmd

F32 = mybir.dt.float32
BF16 = mybir.dt.bfloat16
AX = mybir.AxisListType.X
ALU = mybir.AluOpType
AF = mybir.ActivationFunctionType

NT = 32
GRP = 4
D = 1024
DK = 128
INNER = 4096
HC = 8
OC = 32
SM_SCALE = 128.0 ** -0.5
C32 = 12582912.0
C52MC32 = 6755399428472832.0
CLIP_B = 1.49
EPS = 1e-5
INV_D = 1.0 / 1024.0
INV_WN = 1.0 / float(INNER * D)
DEBUG = False


class Sched:
    def __init__(self):
        self.st = {k: [] for k in "tvsgq"}
        self.cnt = dict(T=0, V=0, S=0, G=0, DQC=0, DGA=0, DGX0=0, DGX1=0,
                        DQW0=0, DQW1=0, DSW=0, DSX0=0, DSX1=0, DSN=0,
                        DSY0=0, DSY1=0, DQN0=0, DQN1=0, DQX0=0, DQX1=0, DQWD=0,
                        DSO=0, DQO0=0, DQO1=0, DGX2=0, DGX3=0, DSX2=0, DSX3=0, DQW2=0, DQW3=0,
                        DGW0=0, DGW1=0, DPX0=0, DPX1=0, DPX2=0)

    def op(self, e, fn, waits=(), inc=None, n=1):
        w = [(s, int(v)) for (s, v) in waits if v and v > 0]
        self.st[e].append((fn, w, inc, n))
        if inc is None:
            return None
        self.cnt[inc] += n
        return self.cnt[inc]


def _build():
    nc = bass.Bass("TRN2", target_bir_lowering=False)
    scratch_kind = "ExternalOutput" if DEBUG else "Internal"

    x_d = nc.dram_tensor("x", [4096, D], F32, kind="ExternalInput").ap()
    wq_d = nc.dram_tensor("wq", [D, DK], F32, kind="ExternalInput").ap()
    wk_d = nc.dram_tensor("wk", [D, DK], F32, kind="ExternalInput").ap()
    wv_d = nc.dram_tensor("wv", [D, DK], F32, kind="ExternalInput").ap()
    wo_d = nc.dram_tensor("wo", [DK, D], F32, kind="ExternalInput").ap()
    wgt_d = nc.dram_tensor("wgt", [D, INNER], F32, kind="ExternalInput").ap()
    wut_d = nc.dram_tensor("wut", [D, INNER], F32, kind="ExternalInput").ap()
    wdt_d = nc.dram_tensor("wdt", [INNER, D], F32, kind="ExternalInput").ap()
    ident_d = nc.dram_tensor("ident", [128, 128], BF16, kind="ExternalInput").ap()
    mask_d = nc.dram_tensor("mask", [128, 128], BF16, kind="ExternalInput").ap()
    y_d = nc.dram_tensor("y", [4096, D], F32, kind="ExternalOutput").ap()
    x1h_d = nc.dram_tensor("x1h", [4096, D], F32, kind=scratch_kind).ap()
    nqh_d = nc.dram_tensor("nqh", [NT, 128, INNER], BF16, kind=scratch_kind).ap()
    wdqh_d = nc.dram_tensor("wdqh", [OC, 128, D], BF16, kind="Internal").ap()
    oh_d = nc.dram_tensor("oh", [NT // GRP, 128, GRP * 128], BF16, kind="Internal").ap()

    sc = Sched()
    import contextlib
    stack = contextlib.ExitStack()

    def sb(name, shape, dt):
        return stack.enter_context(nc.sbuf_tensor(name, shape, dt)).ap()

    qres = sb("qres", [128, NT * 128], BF16)
    kres = sb("kres", [128, NT * 128], BF16)
    vres = sb("vres", [128, NT * 129], BF16)
    wo_sb = sb("wo_sb", [128, D], BF16)
    ident = sb("ident_sb", [128, 128], BF16)
    mask = sb("mask_sb", [128, 128], BF16)
    ones_c = sb("ones_c", [128, 1], F32)
    ones_r = sb("ones_r", [1, 128], F32)
    x2_st = sb("x2_st", [128, 2 * D], F32)
    x_st = [x2_st[:, 0:D], x2_st[:, D:2 * D]]
    x3 = None
    ln_scr = sb("ln_scr", [128, D], F32)
    nqh_sb = sb("nqh_sb", [128, INNER], BF16)
    nqt_sb = sb("nqt_sb", [128, INNER], BF16)
    st = sb("st_cols", [128, 32], F32)
    c2_all = sb("c2_all", [128, NT], F32)
    wsums = sb("wsums", [128, 96], F32)
    kn = sb("konst", [128, 8], F32)
    m_all = sb("m_all", [128, NT], F32)
    var_all = sb("var_all", [128, NT], F32)
    rstd_all = sb("rstd_all", [128, NT], F32)
    bm_all = sb("bm_all", [128, NT], F32)
    wsc = sb("wscales", [128, 8], F32)
    tot_sb = sb("tot_sb", [1, 4], F32)

    pi_c = kn[:, 0:1]; cb_c = kn[:, 1:2]; ng_c = kn[:, 2:3]
    eps_c = kn[:, 3:4]; smsc_c = kn[:, 4:5]; c127_c = kn[:, 5:6]
    s1_c = st[:, 0:1]; sq_c = st[:, 1:2]; m_c = st[:, 2:3]; m2_c = st[:, 3:4]
    var_c = st[:, 4:5]; sd_c = st[:, 5:6]; rstd_c = st[:, 6:7]; bm_c = st[:, 7:8]
    mx_c = st[:, 8:9]; mxc_c = st[:, 9:10]; r_c = st[:, 10:11]; s1q_c = st[:, 11:12]
    c1g_p = [st[:, 12:13], st[:, 20:21]]
    c1u_p = [st[:, 13:14], st[:, 21:22]]
    mh4 = st[:, 24:28]
    mh_c = st[:, 15:16]; mhc_c = st[:, 16:17]; rh_c = st[:, 17:18]; s2q_c = st[:, 18:19]

    ln1_sb = nqh_sb[:, 0:D]
    ln1T_sb = nqt_sb[:, 0:D]

    s1ctx = contextlib.ExitStack()
    wgq = s1ctx.enter_context(nc.sbuf_tensor("wgq", [128, HC * INNER], BF16)).ap()
    wuq = s1ctx.enter_context(nc.sbuf_tensor("wuq", [128, HC * INNER], BF16)).ap()

    s2ctx = contextlib.ExitStack()
    wst2 = s2ctx.enter_context(nc.sbuf_tensor("wstage0", [128, 2048], F32)).ap()
    x3 = [x2_st[:, 0:D], x2_st[:, D:2 * D], wst2[:, 0:D]]
    wdq_st = s2ctx.enter_context(nc.sbuf_tensor("wdq_st", [128, 1024], BF16)).ap()
    wq_sb = s2ctx.enter_context(nc.sbuf_tensor("wq_sb", [128, D], BF16)).ap()
    wk_sb = s2ctx.enter_context(nc.sbuf_tensor("wk_sb", [128, D], BF16)).ap()
    wv_sb = s2ctx.enter_context(nc.sbuf_tensor("wv_sb", [128, D], BF16)).ap()
    pt_row = [s2ctx.enter_context(nc.sbuf_tensor(f"pt{i}", [128, 1024], BF16)).ap() for i in range(3)]
    o_sb = pt_row[2][:, 0:GRP * 128]
    ot_sb = s2ctx.enter_context(nc.sbuf_tensor("ot_sb", [128, 512], BF16)).ap()
    rq_sb = s2ctx.enter_context(nc.sbuf_tensor("rq_sb", [128, GRP], F32)).ap()

    pb = [stack.enter_context(nc.psum_tensor(f"pb{i}", [128, 1024], F32)).ap() for i in range(4)]

    sc.op('q', lambda e: e.dma_start(out=ident, in_=ident_d), inc='DQC', n=16)
    sc.op('q', lambda e: e.dma_start(out=mask, in_=mask_d), inc='DQC', n=16)
    dq_const = ('DQC', sc.cnt['DQC'])
    sc.op('v', lambda e: e.memset(ones_c, 1.0))
    sc.op('v', lambda e: e.memset(ones_r, 1.0))
    sc.op('v', lambda e: e.memset(eps_c, EPS))
    sc.op('v', lambda e: e.memset(smsc_c, SM_SCALE))
    sc.op('v', lambda e: e.memset(c127_c, 127.0))
    sc.op('v', lambda e: e.memset(pi_c, 2.0 ** -24))
    sc.op('v', lambda e: e.drain())
    sc.op('v', lambda e: e.tensor_scalar(pi_c, pi_c, 1.0, None, ALU.add))
    sc.op('v', lambda e: e.drain())
    sc.op('v', lambda e: e.tensor_scalar(pi_c, pi_c, 1.0, None, ALU.subtract))
    sc.op('v', lambda e: e.drain())
    sc.op('v', lambda e: e.tensor_scalar(pi_c, pi_c, 2.0 ** 24, None, ALU.mult))
    sc.op('v', lambda e: e.drain())
    sc.op('v', lambda e: e.tensor_scalar(cb_c, pi_c, C52MC32, C32, ALU.mult, ALU.add))
    sc.op('v', lambda e: e.drain())
    v_konst = sc.op('v', lambda e: e.tensor_scalar(ng_c, cb_c, -1.0, None, ALU.mult), inc='V')

    sc.op('g', lambda e: e.dma_start(
        out=wq_sb.rearrange("p (c m) -> p c m", c=HC),
        in_=wq_d.rearrange("(c p) m -> p c m", p=128)), inc='DGA', n=16)
    sc.op('g', lambda e: e.dma_start(
        out=wk_sb.rearrange("p (c m) -> p c m", c=HC),
        in_=wk_d.rearrange("(c p) m -> p c m", p=128)), inc='DGA', n=16)
    sc.op('g', lambda e: e.dma_start(
        out=wv_sb.rearrange("p (c m) -> p c m", c=HC),
        in_=wv_d.rearrange("(c p) m -> p c m", p=128)), inc='DGA', n=16)
    sc.op('g', lambda e: e.dma_start(out=wo_sb, in_=wo_d), inc='DGA', n=16)
    dg_attnw = ('DGA', sc.cnt['DGA'])

    dg_x = [None] * NT
    s_lastx = [0] * NT

    def emit_xload_p1(i):
        par = i % 3
        w = []
        if i >= 3:
            w += [('S', s_lastx[i - 3]), ('V', v_s1[i - 3])]
        v = sc.op('g', (lambda i=i, par=par: (lambda e: e.dma_start(
            out=x3[par], in_=x_d[i * 128:(i + 1) * 128, :])))(), waits=w, inc='DGX%d' % par, n=16)
        dg_x[i] = ('DGX%d' % par, v)

    wpieces = []
    for wi, wd_ap in ((0, wgt_d), (1, wut_d)):
        for c in range(HC):
            for hh in range(4):
                wpieces.append((wi, wd_ap[c * 128:(c + 1) * 128, hh * 1024:(hh + 1) * 1024], 1024))
    for p in range(OC):
        wpieces.append((2, wdt_d[p * 128:(p + 1) * 128, :], 1024))

    dq_p1w = [None] * len(wpieces)
    s_abs = [None] * len(wpieces)
    w1stages = [wst2[:, 1024:2048],
                nqh_sb.bitcast(F32)[:, 512:1536], nqt_sb.bitcast(F32)[:, 512:1536]]
    w2stages = [x2_st, wst2,
                nqh_sb.bitcast(F32)[:, 0:2048], nqt_sb.bitcast(F32)[:, 0:2048]]
    w2sems = ['DQW0', 'DQW1', 'DQW2', 'DQW3']

    def emit_w1_piece(idx):
        wi, src, wid = wpieces[idx]
        par = idx % 3
        dstap = w1stages[par][:, 0:wid]
        w = [s_abs[idx - 3]] if idx >= 3 else []
        v = sc.op('q', (lambda src=src, dstap=dstap: (lambda e: e.dma_start(out=dstap, in_=src)))(),
                  waits=w, inc='DQW%d' % par, n=16)
        dq_p1w[idx] = ('DQW%d' % par, v)
        col = wsums[:, idx:idx + 1]
        if idx % 2 == 0:
            s_abs[idx] = ('S', sc.op('s', (lambda dstap=dstap, col=col: (lambda e: e.activation(
                dstap, dstap, AF.Abs, accum_out=col)))(),
                waits=[dq_p1w[idx]], inc='S'))
        else:
            s_abs[idx] = ('V', sc.op('v', (lambda dstap=dstap, col=col: (lambda e: e.reduce_sum(
                col, dstap, axis=AX, apply_absolute_value=True)))(),
                waits=[dq_p1w[idx]], inc='V'))

    psA = [pb[0].bitcast(BF16)[:, 0:1024], pb[0].bitcast(BF16)[:, 1024:2048]]
    Qps = [pb[1][:, 0:128], pb[2][:, 512:640]]
    Kps = [pb[1][:, 512:640], pb[3][:, 0:128]]
    Vps = [pb[2][:, 0:128], pb[3][:, 512:640]]
    v_evA = [0, 0]
    v_evQ = [0, 0]; s_evK = [0, 0]; v_evV = [0, 0]

    p1c = [dict(s1=st[:, 8 * k + 0:8 * k + 1], sq=st[:, 8 * k + 1:8 * k + 2],
                m=st[:, 8 * k + 2:8 * k + 3], m2=st[:, 8 * k + 3:8 * k + 4],
                var=st[:, 8 * k + 4:8 * k + 5], sd=st[:, 8 * k + 5:8 * k + 6],
                rstd=st[:, 8 * k + 6:8 * k + 7], bm=st[:, 8 * k + 7:8 * k + 8])
           for k in range(3)]
    s_sqa = [0] * NT
    v_s1 = [0] * NT
    g_m = [0] * NT
    v_m2 = [0] * NT
    g_var = [0] * NT
    s_sd = [0] * NT
    v_rstd = [0] * NT
    g_bm = [0] * NT

    def emit_p1_A(i):
        par = i % 3
        xs = x3[par]
        c = p1c[par]
        vw = [dg_x[i]]
        if i >= 3:
            vw.append(('G', g_m[i - 3]))
        v_s1[i] = sc.op('v', (lambda xs=xs, c=c: (lambda e: e.reduce_sum(c['s1'], xs, axis=AX)))(),
                        waits=vw, inc='V')
        sw = [dg_x[i]]
        if i >= 1:
            sw.append(('S', s_sqa[i - 1]))
        if i >= 3:
            sw.append(('G', g_var[i - 3]))
        s_sqa[i] = sc.op('s', (lambda xs=xs, c=c: (lambda e: e.activation(
            ln_scr, xs, AF.Square, accum_out=c['sq'])))(), waits=sw, inc='S')

    def emit_p1_B1(i):
        c = p1c[i % 3]
        gw = [('V', v_s1[i])]
        if i >= 3:
            gw.append(('V', v_m2[i - 3]))
        g_m[i] = sc.op('g', (lambda c=c: (lambda e: e.tensor_scalar(
            c['m'], c['s1'], INV_D, None, ALU.mult)))(), waits=gw, inc='G')
        vw = [('G', g_m[i])]
        if i >= 3:
            vw.append(('G', g_var[i - 3]))
        v_m2[i] = sc.op('v', (lambda c=c: (lambda e: e.tensor_scalar(
            c['m2'], c['s1'], INV_D, c['m'], ALU.mult, ALU.mult)))(), waits=vw, inc='V')
        gw = [('V', v_m2[i]), ('S', s_sqa[i])]
        if i >= 3:
            gw.append(('S', s_sd[i - 3]))
        g_var[i] = sc.op('g', (lambda c=c: (lambda e: e.tensor_scalar(
            c['var'], c['sq'], INV_D, c['m2'], ALU.mult, ALU.subtract)))(), waits=gw, inc='G')

    def emit_p1_B2(i):
        par = i % 3
        xs = x3[par]
        c = p1c[par]
        sw = [('G', g_var[i])]
        if i >= 3:
            sw.append(('V', v_rstd[i - 3]))
        s_sd[i] = sc.op('s', (lambda c=c: (lambda e: e.activation(
            c['sd'], c['var'], AF.Sqrt, bias=eps_c)))(), waits=sw, inc='S')
        vw = [('S', s_sd[i])]
        if i >= 3:
            vw += [('G', g_bm[i - 3]), ('S', s_lastx[i - 3])]
        v_rstd[i] = sc.op('v', (lambda c=c: (lambda e: e.reciprocal(c['rstd'], c['sd'])))(),
                          waits=vw, inc='V')
        gw = [('V', v_rstd[i])]
        if i >= 3:
            gw.append(('S', s_lastx[i - 3]))
        g_bm[i] = sc.op('g', (lambda c=c: (lambda e: e.tensor_scalar(
            c['bm'], c['m'], -1.0, c['rstd'], ALU.mult, ALU.mult)))(), waits=gw, inc='G')
        s_ln1 = sc.op('s', (lambda xs=xs, c=c: (lambda e: e.activation(
            ln1_sb, xs, AF.Identity, scale=c['rstd'], bias=c['bm'])))(),
            waits=[('G', g_bm[i])], inc='S')
        s_lastx[i] = s_ln1
        if i + 3 < NT:
            emit_xload_p1(i + 3)
        return s_ln1

    def emit_p1_geo(i, s_ln1):
        par = i % 2
        tw = [('S', s_ln1), ('V', v_evA[par]), dq_const, ('V', v_scale_done[0])]
        tt = 0
        for c in range(HC):
            tt = sc.op('t', (lambda c=c, par=par: (lambda e: e.transpose(
                psA[par][:, c * 128:(c + 1) * 128], ln1_sb[:, c * 128:(c + 1) * 128], ident)))(),
                waits=tw if c == 0 else (), inc='T')
        v_evA[par] = sc.op('v', (lambda par=par: (lambda e: e.tensor_copy(ln1T_sb, psA[par])))(),
                           waits=[('T', tt)], inc='V')
        tw = [('V', v_evA[par]), dg_attnw,
              ('V', v_evQ[par]), ('S', s_evK[par]), ('V', v_evV[par])]
        tt = 0
        for c in range(HC):
            tt = sc.op('t', (lambda c=c, par=par: (lambda e: e.matmul(
                Qps[par], wq_sb[:, c * 128:(c + 1) * 128], ln1T_sb[:, c * 128:(c + 1) * 128],
                start=(c == 0), stop=(c == HC - 1))))(), waits=tw if c == 0 else (), inc='T')
        for c in range(HC):
            tt = sc.op('t', (lambda c=c, par=par: (lambda e: e.matmul(
                Kps[par], wk_sb[:, c * 128:(c + 1) * 128], ln1T_sb[:, c * 128:(c + 1) * 128],
                start=(c == 0), stop=(c == HC - 1))))(), inc='T')
        for c in range(HC):
            tt = sc.op('t', (lambda c=c, par=par: (lambda e: e.matmul(
                Vps[par], ln1T_sb[:, c * 128:(c + 1) * 128], wv_sb[:, c * 128:(c + 1) * 128],
                start=(c == 0), stop=(c == HC - 1))))(), inc='T')
        v_evQ[par] = sc.op('v', (lambda i=i, par=par: (lambda e: e.tensor_copy(
            qres[:, i * 128:(i + 1) * 128], Qps[par])))(), waits=[('T', tt)], inc='V')
        s_evK[par] = sc.op('s', (lambda i=i, par=par: (lambda e: e.activation(
            kres[:, i * 128:(i + 1) * 128], Kps[par], AF.Identity)))(), waits=[('T', tt)], inc='S')
        v_evV[par] = sc.op('v', (lambda i=i, par=par: (lambda e: e.tensor_copy(
            vres[:, i * 129:i * 129 + 128], Vps[par])))(), inc='V')

    v_scale_done = [0]
    emit_xload_p1(0)
    emit_xload_p1(1)
    emit_xload_p1(2)
    emit_p1_A(0)
    emit_p1_A(1)
    emit_p1_B1(0)
    w1_next = 0
    for i in range(NT):
        s_ln1x = emit_p1_B2(i)
        if i + 1 < NT:
            emit_p1_B1(i + 1)
        if i + 2 < NT:
            emit_p1_A(i + 2)
        emit_p1_geo(i, s_ln1x)
        for _ in range(3):
            if w1_next < len(wpieces):
                emit_w1_piece(w1_next)
                w1_next += 1
    while w1_next < len(wpieces):
        emit_w1_piece(w1_next)
        w1_next += 1
    v_vones = sc.op('v', lambda e: e.memset(vres[:, 128::129], 1.0), inc='V')
    p1_V = sc.cnt['V']; p1_S = sc.cnt['S']; p1_T = sc.cnt['T']

    def emit_scales():
        vsd = 0
        wcols = [wsums[:, 0:32], wsums[:, 32:64], wsums[:, 64:96]]
        wlast = [s_abs[31], s_abs[63], s_abs[95]]
        for wi in range(3):
            cols = wcols[wi]
            tot_col = st[:, 28:29]
            vv = sc.op('v', (lambda cols=cols, tot_col=tot_col: (lambda e: e.reduce_sum(
                tot_col, cols, axis=AX)))(), waits=[wlast[wi], s_abs[30 + 32 * wi], ('S', p1_S)], inc='V')
            tt = sc.op('t', (lambda tot_col=tot_col: (lambda e: e.matmul(
                pb[0][0:1, 0:1], tot_col, ones_c, start=True, stop=True)))(),
                waits=[('V', vv), ('V', v_evA[0]), ('V', v_evA[1])], inc='T')
            vv = sc.op('v', lambda e: e.tensor_scalar(tot_sb[:, 0:1], pb[0][0:1, 0:1], INV_WN, EPS, ALU.mult, ALU.max),
                       waits=[('T', tt)], inc='V')
            sc.op('v', lambda e: e.drain())
            sc.op('v', lambda e: e.reciprocal(tot_sb[:, 1:2], tot_sb[:, 0:1]))
            sc.op('v', lambda e: e.drain())
            vv = sc.op('v', lambda e: e.tensor_scalar(tot_sb[:, 2:3], tot_sb[:, 0:1], 1.0 / 127.0, None, ALU.mult), inc='V')
            tt = sc.op('t', lambda e: e.matmul(pb[1][0:128, 0:1], ones_r, tot_sb[:, 1:2], start=True, stop=True),
                       waits=[('V', vv), ('V', v_evQ[0]), ('S', s_evK[0])], inc='T')
            tt = sc.op('t', lambda e: e.matmul(pb[1][0:128, 512:513], ones_r, tot_sb[:, 2:3], start=True, stop=True),
                       inc='T')
            s_col = wsc[:, 2 * wi:2 * wi + 1]
            d_col = wsc[:, 2 * wi + 1:2 * wi + 2]
            vv = sc.op('v', (lambda s_col=s_col: (lambda e: e.tensor_copy(s_col, pb[1][0:128, 0:1])))(),
                       waits=[('T', tt)], inc='V')
            vsd = sc.op('v', (lambda d_col=d_col: (lambda e: e.tensor_copy(d_col, pb[1][0:128, 512:513])))(),
                        inc='V')
        v_scale_done[0] = vsd

    swg_c = wsc[:, 0:1]; wgdq_c = wsc[:, 1:2]
    swu_c = wsc[:, 2:3]; wudq_c = wsc[:, 3:4]
    swd_c = wsc[:, 4:5]; wddq_c = wsc[:, 5:6]

    p2pieces = []
    for wi, wd_ap in ((0, wgt_d), (1, wut_d)):
        for c in range(HC):
            for hh in range(2):
                dst = (wgq if wi == 0 else wuq)[:, c * INNER + hh * 2048: c * INNER + (hh + 1) * 2048]
                p2pieces.append((wi, wd_ap[c * 128:(c + 1) * 128, hh * 2048:(hh + 1) * 2048], dst, 2048))
    for p in range(OC):
        p2pieces.append((2, wdt_d[p * 128:(p + 1) * 128, :], p, 1024))

    s_qdone = [0] * len(p2pieces)
    ds_wdq = [0] * len(p2pieces)

    def emit_w2_piece(idx):
        wi, src, dst, wid = p2pieces[idx]
        s_col = wsc[:, 2 * wi:2 * wi + 1]
        par = idx % 4
        dstap = w2stages[par][:, 0:wid]
        if idx < 4:
            w = [('S', p1_S), ('V', p1_V), ('T', p1_T)]
        else:
            w = [('S', s_qdone[idx - 4])]
        v = sc.op('q', (lambda src=src, dstap=dstap: (lambda e: e.dma_start(out=dstap, in_=src)))(),
                  waits=w, inc=w2sems[par], n=16)
        vw = [(w2sems[par], v), ('V', v_konst), ('V', v_scale_done[0])]
        v_q = sc.op('v', (lambda dstap=dstap, s_col=s_col: (lambda e: e.tensor_scalar(
            dstap, dstap, s_col, CLIP_B, ALU.mult, ALU.min)))(), waits=vw, inc='V')
        sc.op('v', lambda e: e.drain())
        v_q = sc.op('v', (lambda dstap=dstap: (lambda e: e.tensor_scalar(
            dstap, dstap, -CLIP_B, cb_c, ALU.max, ALU.add)))(), inc='V')
        if wi < 2:
            s_qdone[idx] = sc.op('s', (lambda dstap=dstap, dst=dst: (lambda e: e.activation(
                dst, dstap, AF.Identity, bias=ng_c)))(), waits=[('V', v_q)], inc='S')
        else:
            pdn = dst
            wds = [('V', v_q)]
            if idx >= 33:
                wds.append(('DSW', ds_wdq[idx - 1]))
            s_qdone[idx] = sc.op('s', (lambda dstap=dstap: (lambda e: e.activation(
                wdq_st, dstap, AF.Identity, bias=ng_c)))(), waits=wds, inc='S')
            ds_wdq[idx] = sc.op('s', (lambda pdn=pdn: (lambda e: e.dma_start(
                out=wdqh_d[pdn], in_=wdq_st)))(),
                waits=[('S', s_qdone[idx])], inc='DSW', n=16)

    stv4 = [pb[0][:, 0:512], pb[0][:, 512:1024], pb[1][:, 0:512], pb[1][:, 512:1024]]
    oaug = [pb[2][:, 0:132], pb[2][:, 512:644], pb[3][:, 0:132], pb[3][:, 512:644]]
    wops = pb[1]
    NKP = 4

    s_exp_last = [0, 0, 0, 0]
    t_av_last = [0, 0, 0, 0]
    dso_last = [0]
    dso_g = [0] * (NT // GRP)
    t_attn_pt = [0]
    dg_xpa = [None] * NT
    v_x1add = [0] * NT
    ds_x1 = [None] * NT
    s_otev_last = 0
    s_oev_last = 0
    v_rec_last = 0

    v_stats = [0] * NT
    s_stats = [0] * NT

    x4 = [x2_st[:, 0:1024], x2_st[:, 1024:2048], wst2[:, 0:1024], wst2[:, 1024:2048]]
    x4sem = ['DGX0', 'DGX1', 'DGX2', 'DGX3']

    def emit_xload_pa(i):
        par = i % 4
        if i >= 4:
            w = [ds_x1[i - 4], ('V', v_stats[i - 4]), ('S', s_stats[i - 4])]
        else:
            w = [('S', s_qdone[-1] if s_qdone[-1] else 0), ds_wd_done]
        v = sc.op('g', (lambda i=i, par=par: (lambda e: e.dma_start(
            out=x4[par], in_=x_d[i * 128:(i + 1) * 128, :])))(), waits=w, inc=x4sem[par], n=16)
        dg_xpa[i] = (x4sem[par], v)

    def emit_pa1_group(g):
        nonlocal s_otev_last, s_oev_last, v_rec_last
        g0 = g * GRP
        nk = g0 + GRP
        t_av_pt = [0, 0, 0]
        pend = []

        def emit_av(rec):
            p0, pp0, pt0, sexp0, gm0 = rec
            tw = [('S', sexp0)]
            if gm0:
                tw.append(('G', gm0))
            first = True
            tt = 0
            for d in range(2):
                k = 2 * p0 + d
                q_lo = max(k, g0)
                off = d * 512 + (q_lo - g0) * 128
                for j in range(g0 + GRP - q_lo):
                    q = q_lo + j
                    tt = sc.op('t', (lambda q=q, k=k, pt0=pt0, g0=g0, j=j, off=off: (lambda e: e.matmul(
                        oaug[q - g0][:, 0:129],
                        pt_row[pt0][:, off + j * 128: off + (j + 1) * 128],
                        vres[:, k * 129: (k + 1) * 129],
                        start=(k == 0), stop=(k == q))))(), waits=tw if first else (), inc='T')
                    first = False
            t_av_pt[pt0] = tt

        for p in range(nk // 2):
            pp = p % 2
            ptx = p % 3
            pbst = pb[pp]
            tw0 = [('S', s_exp_last[pp]), ('V', p1_V), ('S', p1_S)]
            if g == 0 and p == 0:
                tw0 += [('V', v_vones), ('V', v_scale_done[0])]
            if p == 0:
                tw0 += [('S', s_oev_last), ('V', v_rec_last)]
            first = True
            tt = 0
            for d in range(2):
                k = 2 * p + d
                q_lo = max(k, g0)
                off = d * 512 + (q_lo - g0) * 128
                for j in range(g0 + GRP - q_lo):
                    q = q_lo + j
                    tt = sc.op('t', (lambda k=k, q=q, pbst=pbst, off=off, j=j: (lambda e: e.matmul(
                        pbst[:, off + j * 128: off + (j + 1) * 128],
                        kres[:, k * 128:(k + 1) * 128], qres[:, q * 128:(q + 1) * 128],
                        start=True, stop=True)))(), waits=tw0 if first else (), inc='T')
                    first = False
            sw = [('T', tt), ('T', t_av_pt[ptx])]
            if ptx == 2:
                sw.append(('DSO', dso_last[0]))
            if 2 * p >= g0:
                lo0 = (2 * p - g0) * 128
                lo1 = 512 + (2 * p + 1 - g0) * 128
                sc.op('s', (lambda pp=pp, ptx=ptx, lo0=lo0: (lambda e: e.activation(
                    pt_row[ptx][:, lo0:512], pb[pp][:, lo0:512],
                    AF.Exp, scale=smsc_c)))(), waits=sw, inc='S')
                s_exp_last[pp] = sc.op('s', (lambda pp=pp, ptx=ptx, lo1=lo1: (lambda e: e.activation(
                    pt_row[ptx][:, lo1:1024], pb[pp][:, lo1:1024],
                    AF.Exp, scale=smsc_c)))(), inc='S')
            else:
                s_exp_last[pp] = sc.op('s', (lambda pp=pp, ptx=ptx: (lambda e: e.activation(
                    pt_row[ptx][:, 0:1024], pb[pp][:, 0:1024],
                    AF.Exp, scale=smsc_c)))(), waits=sw, inc='S')
            gmask = 0
            for d in range(2):
                k = 2 * p + d
                if k >= g0:
                    gmask = sc.op('g', (lambda ptx=ptx, k=k, g0=g0, d=d: (lambda e: e.tensor_mul(
                        pt_row[ptx][:, d * 512 + (k - g0) * 128: d * 512 + (k - g0 + 1) * 128],
                        pt_row[ptx][:, d * 512 + (k - g0) * 128: d * 512 + (k - g0 + 1) * 128],
                        mask)))(), waits=[('S', s_exp_last[pp]), dq_const], inc='G')
            pend.append((p, pp, ptx, s_exp_last[pp], gmask))
            if len(pend) > 2:
                emit_av(pend.pop(0))
        while pend:
            emit_av(pend.pop(0))
        t_attn_done = sc.cnt['T']
        v_rec = 0
        for j in range(GRP):
            v_rec = sc.op('v', (lambda j=j: (lambda e: e.reciprocal(
                rq_sb[:, j:j + 1], oaug[j][:, 128:129])))(),
                waits=[('T', t_attn_done)] if j == 0 else (), inc='V')
        v_rec_last = v_rec
        s_oev = 0
        for j in range(GRP):
            s_oev = sc.op('s', (lambda j=j: (lambda e: e.activation(
                o_sb[:, j * 128:(j + 1) * 128], oaug[j][:, 0:128], AF.Identity,
                scale=rq_sb[:, j:j + 1])))(), waits=[('V', v_rec)], inc='S')
        s_oev_last = s_oev

        dso_last[0] = sc.op('s', (lambda g=g: (lambda e: e.dma_start(
            out=oh_d[g], in_=o_sb)))(), waits=[('S', s_oev)], inc='DSO', n=16)
        dso_g[g] = dso_last[0]

    emit_scales()
    w2_next = 0
    for g in range(NT // GRP):
        emit_pa1_group(g)
        for _ in range(8):
            if w2_next < len(p2pieces):
                emit_w2_piece(w2_next)
                w2_next += 1
    while w2_next < len(p2pieces):
        emit_w2_piece(w2_next)
        w2_next += 1
    s_wq_done = s_qdone[31]
    ds_wd_done = ('DSW', ds_wdq[-1])

    t_attn_pt[0] = sc.cnt['T']
    emit_xload_pa(0)
    emit_xload_pa(1)
    emit_xload_pa(2)
    emit_xload_pa(3)
    t_attn_end = sc.cnt['T']; s_attn_end = sc.cnt['S']
    g_attn_end = sc.cnt['G']; v_attn_end = sc.cnt['V']
    o_in = [pt_row[0][:, 0:512], pt_row[1][:, 0:512]]
    s1_p = [s1_c, wsums[:, 0:1]]
    sq_p = [sq_c, wsums[:, 1:2]]
    m2_p = [m2_c, wsums[:, 2:3]]
    sqscr = [ln_scr, vres[:, 0:D]]
    dqo = [None] * (NT // GRP)
    t_ot = [0] * (NT // GRP)
    v_otev_g = [0] * (NT // GRP)
    wo_free = {1: 0, 2: 0, 3: 0}

    def emit_oload(g):
        par = g % 2
        w = [('DSO', dso_g[g]), ('T', t_attn_end), ('S', s_attn_end), ('G', g_attn_end)]
        if g >= 2:
            w.append(('T', t_ot[g - 2]))
        v = sc.op('q', (lambda g=g, par=par: (lambda e: e.dma_start(
            out=o_in[par], in_=oh_d[g])))(), waits=w, inc='DQO%d' % par, n=16)
        dqo[g] = ('DQO%d' % par, v)

    emit_oload(0)
    emit_oload(1)
    for g in range(NT // GRP):
        g0 = g * GRP
        par = g % 2
        tw = [dqo[g], ('V', v_otev_g[g - 1] if g >= 1 else 0)]
        if g == 0:
            tw += [('S', s_attn_end), ('V', v_attn_end)]
        tt = 0
        for j in range(GRP):
            tt = sc.op('t', (lambda j=j, par=par: (lambda e: e.transpose(
                pb[0].bitcast(BF16)[:, j * 128:(j + 1) * 128],
                o_in[par][:, j * 128:(j + 1) * 128], ident)))(),
                waits=tw if j == 0 else (), inc='T')
        t_ot[g] = tt
        v_otev_g[g] = sc.op('v', lambda e: e.tensor_copy(ot_sb, pb[0].bitcast(BF16)[:, 0:512]),
                            waits=[('T', tt)], inc='V')
        if g + 2 < NT // GRP:
            emit_oload(g + 2)
        wo_banks = [1, 2, 3, 1]

        def emit_stats(i):
            xb = x4[i % 4]
            jp = i % 2
            sc.op('v', lambda e: e.drain())
            sc.op('v', (lambda xb=xb, jp=jp: (lambda e: e.reduce_sum(
                s1_p[jp], xb, axis=AX)))(), waits=[('V', v_x1add[i])], inc='V')
            sw = [('V', v_x1add[i])]
            if i >= 2:
                sw.append(('S', s_stats[i - 2]))
            s_stats[i] = sc.op('s', (lambda xb=xb, jp=jp: (lambda e: e.activation(
                sqscr[jp], xb, AF.Square, accum_out=sq_p[jp])))(),
                waits=sw, inc='S')
            sc.op('v', lambda e: e.drain())
            sc.op('v', (lambda i=i, jp=jp: (lambda e: e.tensor_scalar(
                m_all[:, i:i + 1], s1_p[jp], INV_D, None, ALU.mult)))(), inc='V')
            sc.op('v', lambda e: e.drain())
            sc.op('v', (lambda i=i, jp=jp: (lambda e: e.tensor_mul(
                m2_p[jp], m_all[:, i:i + 1], m_all[:, i:i + 1])))(), inc='V')
            sc.op('v', lambda e: e.drain())
            v_stats[i] = sc.op('v', (lambda i=i, jp=jp: (lambda e: e.tensor_scalar(
                var_all[:, i:i + 1], sq_p[jp], INV_D, m2_p[jp], ALU.mult, ALU.subtract)))(),
                waits=[('S', s_stats[i])], inc='V')
            if i + 4 < NT:
                emit_xload_pa(i + 4)

        for j in range(GRP):
            i = g0 + j
            wb = wo_banks[j]
            wob = pb[wb]
            tw = [('V', v_otev_g[g]), ('V', wo_free[wb])]
            if g == 0 and j == 0:
                tw += [('S', s_attn_end), ('V', v_attn_end)]
            tt = sc.op('t', (lambda j=j, wob=wob: (lambda e: e.matmul(
                wob[:, 0:512], ot_sb[:, j * 128:(j + 1) * 128], wo_sb[:, 0:512],
                start=True, stop=True)))(), waits=tw, inc='T')
            tt = sc.op('t', (lambda j=j, wob=wob: (lambda e: e.matmul(
                wob[:, 512:1024], ot_sb[:, j * 128:(j + 1) * 128], wo_sb[:, 512:1024],
                start=True, stop=True)))(), inc='T')
            pari = i % 4
            xb = x4[pari]
            vw = [('T', tt), dg_xpa[i]]
            vv = sc.op('v', (lambda xb=xb, wob=wob: (lambda e: e.tensor_add(
                xb[:, 0:512], wob[:, 0:512], xb[:, 0:512])))(), waits=vw, inc='V')
            vv = sc.op('v', (lambda xb=xb, wob=wob: (lambda e: e.tensor_add(
                xb[:, 512:1024], wob[:, 512:1024], xb[:, 512:1024])))(), inc='V')
            v_x1add[i] = vv
            wo_free[wb] = vv
            v = sc.op('s', (lambda i=i, xb=xb: (lambda e: e.dma_start(
                out=x1h_d[i * 128:(i + 1) * 128, :], in_=xb)))(),
                waits=[('V', vv)], inc='DSX%d' % pari, n=16)
            ds_x1[i] = ('DSX%d' % pari, v)
            if j >= 1:
                emit_stats(i - 1)
        emit_stats(g0 + GRP - 1)

    ss = sc.op('s', lambda e: e.activation(var_all, var_all, AF.Sqrt, bias=eps_c),
               waits=[('V', v_stats[NT - 1])], inc='S')
    vv = sc.op('v', lambda e: e.reciprocal(rstd_all, var_all), waits=[('S', ss)], inc='V')
    sc.op('v', lambda e: e.drain())
    vv = sc.op('v', lambda e: e.tensor_mul(bm_all, m_all, rstd_all), inc='V')
    sc.op('v', lambda e: e.drain())
    v_lnstats = sc.op('v', lambda e: e.tensor_scalar(bm_all, bm_all, -1.0, None, ALU.mult), inc='V')
    pa1_V = sc.cnt['V']; pa1_S = sc.cnt['S']

    s2ctx.close()
    s3b = contextlib.ExitStack()
    h_sb = s3b.enter_context(nc.sbuf_tensor("h_sb", [128, INNER], F32)).ap()
    g_s = kres.bitcast(F32)[:, 0:1024]
    xq2 = qres[:, 2048:3072]
    xqTb = [qres[:, 0:1024], qres[:, 1024:2048]]

    dq_x1b = [None] * NT
    s_ln2t = [0] * NT
    s_xq = [0] * NT
    v_rnd1 = [0] * NT
    t_xqA = [0] * NT
    t_E = [0] * NT
    s_silu_q = {}
    v_stt_q = {}
    s_nqh_t = [0] * NT
    s_nqh_h0 = [0] * NT
    v_nqev = [0] * NT
    ds_nq = [None] * NT

    def emit_x1load(i):
        par = i % 2
        w = [ds_x1[i]]
        if i >= 2:
            w.append(('S', s_ln2t[i - 2]))
        else:
            w += [('V', pa1_V), ('S', pa1_S), ds_x1[NT - 4], ds_x1[NT - 3]]
        v = sc.op('q', (lambda i=i, par=par: (lambda e: e.dma_start(
            out=x_st[par], in_=x1h_d[i * 128:(i + 1) * 128, :])))(), waits=w, inc='DQX%d' % par, n=16)
        dq_x1b[i] = ('DQX%d' % par, v)

    def chain_blk1(i):
        par = i % 2
        sw = [dq_x1b[i], ('S', pa1_S)]
        if i >= 1:
            sw.append(('V', v_rnd1[i - 1]))
        s_ln2t[i] = sc.op('s', (lambda par=par, i=i: (lambda e: e.activation(
            ln_scr, x_st[par], AF.Identity, scale=rstd_all[:, i:i + 1],
            bias=bm_all[:, i:i + 1])))(), waits=sw, inc='S')

    def chain_blk2(i):
        sc.op('v', lambda e: e.drain())
        vv = sc.op('v', lambda e: e.reduce_max(mx_c, ln_scr, axis=AX, apply_absolute_value=True),
                   waits=[('S', s_ln2t[i]), ('V', v_lnstats)], inc='V')
        sc.op('v', lambda e: e.drain())
        vv = sc.op('v', lambda e: e.tensor_scalar(mxc_c, mx_c, EPS, None, ALU.max), inc='V')
        sc.op('v', lambda e: e.drain())
        vv = sc.op('v', lambda e: e.reciprocal(r_c, mxc_c), inc='V')

    def chain_blk3(i):
        sc.op('v', lambda e: e.drain())
        sc.op('v', lambda e: e.tensor_scalar(s1q_c, r_c, 127.0, None, ALU.mult), inc='V')
        sc.op('v', (lambda i=i: (lambda e: e.tensor_mul(c1g_p[i % 2], mxc_c, wgdq_c)))(), inc='V')
        sc.op('v', (lambda i=i: (lambda e: e.tensor_mul(c1u_p[i % 2], mxc_c, wudq_c)))(), inc='V')
        sc.op('v', lambda e: e.drain())
        vv = sc.op('v', lambda e: e.tensor_scalar(ln_scr, ln_scr, s1q_c, cb_c, ALU.mult, ALU.add), inc='V')
        v_rnd1[i] = vv
        sw = [('V', vv), ('S', s_wq_done)]
        if i >= 1:
            sw.append(('T', t_xqA[i - 1]))
        else:
            sw += [ds_x1[NT - 2], ds_x1[NT - 1], ds_wd_done]
        s_xq[i] = sc.op('s', lambda e: e.activation(xq2, ln_scr, AF.Identity, bias=ng_c),
                        waits=sw, inc='S')
        if i + 2 < NT:
            emit_x1load(i + 2)

    emit_x1load(0)
    emit_x1load(1)
    chain_blk1(0)
    chain_blk2(0)
    chain_blk3(0)

    def emit_ychain(i):
        sc.op('v', lambda e: e.drain())
        sc.op('v', lambda e: e.tensor_max(mh4[:, 0:2], mh4[:, 0:2], mh4[:, 2:4]), inc='V')
        sc.op('v', lambda e: e.drain())
        sc.op('v', lambda e: e.tensor_max(mh_c, mh4[:, 0:1], mh4[:, 1:2]), inc='V')
        sc.op('v', lambda e: e.drain())
        sc.op('v', lambda e: e.tensor_scalar(mhc_c, mh_c, EPS, None, ALU.max), inc='V')
        sc.op('v', lambda e: e.drain())
        sc.op('v', lambda e: e.reciprocal(rh_c, mhc_c), inc='V')
        sc.op('v', lambda e: e.drain())
        sc.op('v', lambda e: e.tensor_scalar(s2q_c, rh_c, 127.0, None, ALU.mult), inc='V')
        sc.op('v', (lambda i=i: (lambda e: e.tensor_mul(c2_all[:, i:i + 1], mhc_c, wddq_c)))(), inc='V')
        sc.op('v', lambda e: e.drain())
        vv = sc.op('v', (lambda i=i: (lambda e: e.tensor_scalar(
            h_sb[:, 0:2048], h_sb[:, 0:2048], s2q_c, cb_c, ALU.mult, ALU.add)))(),
            waits=[('V', v_stt_q[(i, 3)])], inc='V')
        s_nqh_h0[i] = sc.op('s', lambda e: e.activation(
            nqh_sb[:, 0:2048], h_sb[:, 0:2048], AF.Identity, bias=ng_c),
            waits=[('V', vv)], inc='S')
        vv = sc.op('v', (lambda i=i: (lambda e: e.tensor_scalar(
            h_sb[:, 2048:4096], h_sb[:, 2048:4096], s2q_c, cb_c, ALU.mult, ALU.add)))(), inc='V')
        s_nqh_t[i] = sc.op('s', lambda e: e.activation(
            nqh_sb[:, 2048:4096], h_sb[:, 2048:4096], AF.Identity, bias=ng_c),
            waits=[('V', vv)], inc='S')

    pair_free = {0: (0, v_x1add[NT - 1]), 1: (0, v_x1add[NT - 1])}
    v_xqt_arr = [0] * NT
    gbank = {0: pb[0], 1: pb[2]}
    ubank = {0: pb[1], 1: pb[3]}

    def emit_Ablock(j):
        apair = j % 2
        aps = pb[0] if apair == 0 else pb[2]
        tw = [('S', s_xq[j]), dq_const,
              ('S', pair_free[apair][0]), ('V', pair_free[apair][1])]
        if j == 0:
            tw += [('V', v_otev_g[NT // GRP - 1]), ('V', v_x1add[NT - 1])]
        tt = 0
        for c in range(HC):
            tt = sc.op('t', (lambda c=c, aps=aps: (lambda e: e.transpose(
                aps.bitcast(BF16)[:, c * 128:(c + 1) * 128],
                xq2[:, c * 128:(c + 1) * 128], ident)))(),
                waits=tw if c == 0 else (), inc='T')
        t_xqA[j] = tt
        vw = [('T', tt)]
        if j < 2:
            vw += [ds_x1[NT - 2], ds_x1[NT - 1], ds_wd_done]
        v_xqt_arr[j] = sc.op('v', (lambda j=j, aps=aps: (lambda e: e.tensor_copy(
            xqTb[j % 2], aps.bitcast(BF16)[:, 0:1024])))(), waits=vw, inc='V')

    emit_Ablock(0)
    for i in range(NT):
        par = i % 2
        qpair = [0, 1, 1, 0] if i % 2 == 0 else [1, 0, 0, 1]
        fpair = 0 if i % 2 == 0 else 1
        v_xqt = v_xqt_arr[i]
        if i >= 1:
            emit_ychain(i - 1)

        for qtr in range(4):
            pr = qpair[qtr]
            gb = gbank[pr]
            ub = ubank[pr]
            tw = [('S', pair_free[pr][0]), ('V', pair_free[pr][1])]
            if qtr == 0:
                tw += [('V', v_xqt), ('S', s_wq_done)]
                if i == 0:
                    tw += [('V', v_x1add[NT - 1])]
            first = True
            tt = 0
            for c in range(HC):
                if qtr == 3 and c == 5 and i + 1 < NT:
                    emit_Ablock(i + 1)
                for half in range(2):
                    o0 = qtr * 1024 + half * 512
                    tt = sc.op('t', (lambda c=c, o0=o0, gb=gb, half=half, par=par: (lambda e: e.matmul(
                        gb[:, half * 512:(half + 1) * 512],
                        xqTb[par][:, c * 128:(c + 1) * 128],
                        wgq[:, c * INNER + o0: c * INNER + o0 + 512],
                        start=(c == 0), stop=(c == HC - 1))))(),
                        waits=tw if first else (), inc='T')
                    first = False
                    tt = sc.op('t', (lambda c=c, o0=o0, ub=ub, half=half, par=par: (lambda e: e.matmul(
                        ub[:, half * 512:(half + 1) * 512],
                        xqTb[par][:, c * 128:(c + 1) * 128],
                        wuq[:, c * INNER + o0: c * INNER + o0 + 512],
                        start=(c == 0), stop=(c == HC - 1))))(), inc='T')
            if qtr == 3:
                t_E[i] = tt
            sw = [('T', tt)]
            if qtr >= 1:
                sw.append(('V', v_stt_q[(i, qtr - 1)]))
            s_silu_q[(i, qtr)] = sc.op('s', (lambda gb=gb, i=i: (lambda e: e.activation(
                g_s, gb, AF.Silu, scale=c1g_p[i % 2])))(), waits=sw, inc='S')
            v_stt_q[(i, qtr)] = sc.op('v', (lambda qtr=qtr, ub=ub, i=i: (lambda e: e.scalar_tensor_tensor(
                h_sb[:, qtr * 1024:(qtr + 1) * 1024], ub, c1u_p[i % 2], g_s, ALU.mult, ALU.mult)))(),
                waits=[('S', s_silu_q[(i, qtr)])], inc='V')
            pair_free[pr] = (s_silu_q[(i, qtr)], v_stt_q[(i, qtr)])
            sc.op('v', lambda e: e.drain())
            sc.op('v', (lambda qtr=qtr: (lambda e: e.reduce_max(
                mh4[:, qtr:qtr + 1], h_sb[:, qtr * 1024:(qtr + 1) * 1024],
                axis=AX, apply_absolute_value=True)))(), inc='V')

            if i + 1 < NT:
                if qtr == 0:
                    chain_blk1(i + 1)
                elif qtr == 2:
                    chain_blk3(i + 1)
            if qtr == 1 and i >= 1:
                fg = gbank[fpair]
                fu = ubank[fpair]
                tw2 = [('S', s_nqh_h0[i - 1]),
                       ('S', pair_free[fpair][0]), ('V', pair_free[fpair][1])]
                tt2 = 0
                for c in range(OC):
                    pbx = fg if c < 16 else fu
                    coff = (c % 16) * 128
                    cw = tw2 if c == 0 else ([('S', s_nqh_t[i - 1])] if c == 16 else ())
                    tt2 = sc.op('t', (lambda c=c, pbx=pbx, coff=coff: (lambda e: e.transpose(
                        pbx.bitcast(BF16)[:, coff:coff + 128],
                        nqh_sb[:, c * 128:(c + 1) * 128], ident)))(),
                        waits=cw, inc='T')
                sw2 = [('T', tt2)]
                if i >= 2:
                    sw2.append(ds_nq[i - 2])
                ss2 = sc.op('s', (lambda fg=fg: (lambda e: e.activation(
                    nqt_sb[:, 0:2048], fg.bitcast(BF16)[:, 0:2048], AF.Identity)))(),
                    waits=sw2, inc='S')
                ss2 = sc.op('s', (lambda fu=fu: (lambda e: e.activation(
                    nqt_sb[:, 2048:4096], fu.bitcast(BF16)[:, 0:2048], AF.Identity)))(),
                    inc='S')
                s_nqev = ss2
                v_nqev[i - 1] = 0
                pair_free[fpair] = (s_nqev, pair_free[fpair][1])
                v2 = sc.op('s', (lambda i=i: (lambda e: e.dma_start(
                    out=nqh_d[i - 1], in_=nqt_sb)))(), waits=[('S', s_nqev)], inc='DSN', n=16)
                ds_nq[i - 1] = ('DSN', v2)
            if qtr == 1 and i + 1 < NT:
                chain_blk2(i + 1)
    emit_ychain(NT - 1)

    tw2 = [('S', s_nqh_t[NT - 1]), ('S', pair_free[0][0]), ('V', pair_free[0][1])]
    tt2 = 0
    for c in range(OC):
        pbx = pb[0] if c < 16 else pb[1]
        coff = (c % 16) * 128
        tt2 = sc.op('t', (lambda c=c, pbx=pbx, coff=coff: (lambda e: e.transpose(
            pbx.bitcast(BF16)[:, coff:coff + 128],
            nqh_sb[:, c * 128:(c + 1) * 128], ident)))(),
            waits=tw2 if c == 0 else (), inc='T')
    sw2 = [('T', tt2), ds_nq[NT - 2]]
    ss2 = sc.op('s', lambda e: e.activation(nqt_sb[:, 0:2048], pb[0].bitcast(BF16)[:, 0:2048], AF.Identity),
                waits=sw2, inc='S')
    ss2 = sc.op('s', lambda e: e.activation(nqt_sb[:, 2048:4096], pb[1].bitcast(BF16)[:, 0:2048], AF.Identity),
                inc='S')
    v2 = sc.op('s', lambda e: e.dma_start(out=nqh_d[NT - 1], in_=nqt_sb),
               waits=[('S', ss2)], inc='DSN', n=16)
    ds_nq[NT - 1] = ('DSN', v2)
    pa2_V = sc.cnt['V']; pa2_S = sc.cnt['S']; pa2_G = sc.cnt['G']

    s3b.close()
    s1ctx.close()
    s4ctx = contextlib.ExitStack()
    wdq_sb = s4ctx.enter_context(nc.sbuf_tensor("wdq_sb", [128, OC * 1024], BF16)).ap()
    nqt_in = [s4ctx.enter_context(nc.sbuf_tensor(f"nqt_in{i}", [128, INNER], BF16)).ap() for i in range(2)]
    x1_in = [s4ctx.enter_context(nc.sbuf_tensor(f"x1_in{i}", [128, D], F32)).ap() for i in range(2)]
    out_sb = [s4ctx.enter_context(nc.sbuf_tensor(f"out_sb{i}", [128, D], F32)).ap() for i in range(2)]

    v = sc.op('q', lambda e: e.dma_start(
        out=wdq_sb.rearrange("p (c d) -> p c d", c=OC),
        in_=wdqh_d.rearrange("c p d -> p c d")),
        waits=[('T', t_E[NT - 1]), ds_wd_done], inc='DQWD', n=16)
    dq_wdq = ('DQWD', v)

    dq_nq = [None] * NT; dq_x1pb = [None] * NT
    v_out = [0] * NT
    ds_y = [None] * NT
    t_down = [0] * NT
    dps = [pb[0], pb[1]]

    def emit_pb_loads(i):
        par = i % 2
        w = [('S', pa2_S), ds_nq[i]]
        if i >= 2:
            w.append(('T', t_down[i - 2]))
        v = sc.op('q', (lambda i=i, par=par: (lambda e: e.dma_start(
            out=nqt_in[par], in_=nqh_d[i])))(), waits=w, inc='DQN%d' % par, n=16)
        dq_nq[i] = ('DQN%d' % par, v)
        w2 = [('S', pa2_S), ('V', pa2_V), ds_x1[i]]
        if i >= 2:
            w2.append(('V', v_out[i - 2]))
        v = sc.op('q', (lambda i=i, par=par: (lambda e: e.dma_start(
            out=x1_in[par], in_=x1h_d[i * 128:(i + 1) * 128, :])))(), waits=w2, inc='DQX%d' % par, n=16)
        dq_x1pb[i] = ('DQX%d' % par, v)

    emit_pb_loads(0)
    emit_pb_loads(1)
    for i in range(NT):
        par = i % 2
        tw = [dq_nq[i], dq_wdq]
        if i < 2:
            tw += [('V', pa2_V), ('S', pa2_S)]
        else:
            tw.append(('V', v_out[i - 2]))
        tt = 0
        for c in range(OC):
            for half in range(2):
                tt = sc.op('t', (lambda c=c, half=half, par=par: (lambda e: e.matmul(
                    dps[par][:, half * 512:(half + 1) * 512],
                    nqt_in[par][:, c * 128:(c + 1) * 128],
                    wdq_sb[:, c * 1024 + half * 512: c * 1024 + (half + 1) * 512],
                    start=(c == 0), stop=(c == OC - 1))))(),
                    waits=tw if (c == 0 and half == 0) else (), inc='T')
        t_down[i] = tt
        vw = [('T', tt), dq_x1pb[i]]
        if i >= 2:
            vw.append(ds_y[i - 2])
        vv = sc.op('v', (lambda i=i, par=par: (lambda e: e.scalar_tensor_tensor(
            out_sb[par][:, 0:512], dps[par][:, 0:512], c2_all[:, i:i + 1],
            x1_in[par][:, 0:512], ALU.mult, ALU.add)))(), waits=vw, inc='V')
        vv = sc.op('v', (lambda i=i, par=par: (lambda e: e.scalar_tensor_tensor(
            out_sb[par][:, 512:1024], dps[par][:, 512:1024], c2_all[:, i:i + 1],
            x1_in[par][:, 512:1024], ALU.mult, ALU.add)))(), inc='V')
        v_out[i] = vv
        v = sc.op('s', (lambda i=i, par=par: (lambda e: e.dma_start(
            out=y_d[i * 128:(i + 1) * 128, :], in_=out_sb[par])))(),
            waits=[('V', vv)], inc='DSY%d' % par, n=16)
        ds_y[i] = ('DSY%d' % par, v)
        if i + 2 < NT:
            emit_pb_loads(i + 2)

    sc.op('q', lambda e: e.nop(), waits=[(nm, sc.cnt[nm]) for nm in sc.cnt if nm.startswith('D')])

    sems = {n: stack.enter_context(nc.semaphore(n)) for n in sc.cnt}
    blk = stack.enter_context(nc.Block())

    def emit(key):
        def body(eng):
            last = {}
            for fn, waits, inc, n in sc.st[key]:
                for s, v in waits:
                    if last.get(s, 0) < v:
                        eng.wait_ge(sems[s], v)
                        last[s] = v
                inst = fn(eng)
                if inc is not None:
                    inst.then_inc(sems[inc], n)
        return body

    blk.tensor(emit('t'))
    blk.vector(emit('v'))
    blk.scalar(emit('s'))
    blk.gpsimd(emit('g'))
    blk.sync(emit('q'))
    s4ctx.close()
    stack.close()
    return nc


def kernel(L0_attn_x, L0_attn_Wq, L0_attn_Wk, L0_attn_Wv, L0_attn_Wo,
           L0_ffn_w_gate, L0_ffn_w_up, L0_ffn_w_down):
    x = np.ascontiguousarray(np.asarray(L0_attn_x, dtype=np.float32))
    assert x.shape == (8, 4096, 1024)
    wq = np.ascontiguousarray(np.asarray(L0_attn_Wq, dtype=np.float32))
    wk = np.ascontiguousarray(np.asarray(L0_attn_Wk, dtype=np.float32))
    wv = np.ascontiguousarray(np.asarray(L0_attn_Wv, dtype=np.float32))
    wo = np.ascontiguousarray(np.asarray(L0_attn_Wo, dtype=np.float32))
    wgt = np.ascontiguousarray(np.asarray(L0_ffn_w_gate, dtype=np.float32).T)
    wut = np.ascontiguousarray(np.asarray(L0_ffn_w_up, dtype=np.float32).T)
    wdt = np.ascontiguousarray(np.asarray(L0_ffn_w_down, dtype=np.float32).T)
    ident = np.eye(128, dtype=ml_dtypes.bfloat16)
    mask = np.triu(np.ones((128, 128), dtype=ml_dtypes.bfloat16))

    nc = _build()
    in_maps = [{"x": x[i], "wq": wq, "wk": wk, "wv": wv, "wo": wo,
                "wgt": wgt, "wut": wut, "wdt": wdt,
                "ident": ident, "mask": mask} for i in range(8)]
    res = run_bass_kernel_spmd(nc, in_maps, core_ids=list(range(8)))
    out = np.stack([np.asarray(res.results[i]["y"]) for i in range(8)], axis=0)
    return out.astype(np.float32)


# version 32
# speedup vs baseline: 1.0675x; 1.0052x over previous
import numpy as np
import ml_dtypes
import concourse.bass as bass
import concourse.mybir as mybir
from concourse.bass_utils import run_bass_kernel_spmd

F32 = mybir.dt.float32
BF16 = mybir.dt.bfloat16
AX = mybir.AxisListType.X
ALU = mybir.AluOpType
AF = mybir.ActivationFunctionType

NT = 32
GRP = 4
D = 1024
DK = 128
INNER = 4096
HC = 8
OC = 32
SM_SCALE = 128.0 ** -0.5
C32 = 12582912.0
C52MC32 = 6755399428472832.0
CLIP_B = 1.49
EPS = 1e-5
INV_D = 1.0 / 1024.0
INV_WN = 1.0 / float(INNER * D)
DEBUG = False


class Sched:
    def __init__(self):
        self.st = {k: [] for k in "tvsgq"}
        self.cnt = dict(T=0, V=0, S=0, G=0, DQC=0, DGA=0, DGX0=0, DGX1=0,
                        DQW0=0, DQW1=0, DSW=0, DSX0=0, DSX1=0, DSN=0,
                        DSY0=0, DSY1=0, DQN0=0, DQN1=0, DQX0=0, DQX1=0, DQWD=0,
                        DSO=0, DQO0=0, DQO1=0, DGX2=0, DGX3=0, DSX2=0, DSX3=0, DQW2=0, DQW3=0,
                        DGW0=0, DGW1=0, DPX0=0, DPX1=0, DPX2=0)

    def op(self, e, fn, waits=(), inc=None, n=1):
        w = [(s, int(v)) for (s, v) in waits if v and v > 0]
        self.st[e].append((fn, w, inc, n))
        if inc is None:
            return None
        self.cnt[inc] += n
        return self.cnt[inc]


def _build():
    nc = bass.Bass("TRN2", target_bir_lowering=False)
    scratch_kind = "ExternalOutput" if DEBUG else "Internal"

    x_d = nc.dram_tensor("x", [4096, D], F32, kind="ExternalInput").ap()
    wq_d = nc.dram_tensor("wq", [D, DK], F32, kind="ExternalInput").ap()
    wk_d = nc.dram_tensor("wk", [D, DK], F32, kind="ExternalInput").ap()
    wv_d = nc.dram_tensor("wv", [D, DK], F32, kind="ExternalInput").ap()
    wo_d = nc.dram_tensor("wo", [DK, D], F32, kind="ExternalInput").ap()
    wgt_d = nc.dram_tensor("wgt", [D, INNER], F32, kind="ExternalInput").ap()
    wut_d = nc.dram_tensor("wut", [D, INNER], F32, kind="ExternalInput").ap()
    wdt_d = nc.dram_tensor("wdt", [INNER, D], F32, kind="ExternalInput").ap()
    ident_d = nc.dram_tensor("ident", [128, 128], BF16, kind="ExternalInput").ap()
    mask_d = nc.dram_tensor("mask", [128, 128], BF16, kind="ExternalInput").ap()
    y_d = nc.dram_tensor("y", [4096, D], F32, kind="ExternalOutput").ap()
    x1h_d = nc.dram_tensor("x1h", [4096, D], F32, kind=scratch_kind).ap()
    nqh_d = nc.dram_tensor("nqh", [NT, 128, INNER], BF16, kind=scratch_kind).ap()
    wdqh_d = nc.dram_tensor("wdqh", [OC, 128, D], BF16, kind="Internal").ap()
    oh_d = nc.dram_tensor("oh", [NT // GRP, 128, GRP * 128], BF16, kind="Internal").ap()

    sc = Sched()
    import contextlib
    stack = contextlib.ExitStack()

    def sb(name, shape, dt):
        return stack.enter_context(nc.sbuf_tensor(name, shape, dt)).ap()

    qres = sb("qres", [128, NT * 128], BF16)
    kres = sb("kres", [128, NT * 128], BF16)
    vres = sb("vres", [128, NT * 129], BF16)
    wo_sb = sb("wo_sb", [128, D], BF16)
    ident = sb("ident_sb", [128, 128], BF16)
    mask = sb("mask_sb", [128, 128], BF16)
    ones_c = sb("ones_c", [128, 1], F32)
    ones_r = sb("ones_r", [1, 128], F32)
    x2_st = sb("x2_st", [128, 2 * D], F32)
    x_st = [x2_st[:, 0:D], x2_st[:, D:2 * D]]
    x3 = None
    ln_scr = sb("ln_scr", [128, D], F32)
    nqh_sb = sb("nqh_sb", [128, INNER], BF16)
    nqt_sb = sb("nqt_sb", [128, INNER], BF16)
    st = sb("st_cols", [128, 32], F32)
    c2_all = sb("c2_all", [128, NT], F32)
    wsums = sb("wsums", [128, 96], F32)
    kn = sb("konst", [128, 8], F32)
    m_all = sb("m_all", [128, NT], F32)
    var_all = sb("var_all", [128, NT], F32)
    rstd_all = sb("rstd_all", [128, NT], F32)
    bm_all = sb("bm_all", [128, NT], F32)
    wsc = sb("wscales", [128, 8], F32)
    tot_sb = sb("tot_sb", [1, 4], F32)

    pi_c = kn[:, 0:1]; cb_c = kn[:, 1:2]; ng_c = kn[:, 2:3]
    eps_c = kn[:, 3:4]; smsc_c = kn[:, 4:5]; c127_c = kn[:, 5:6]
    s1_c = st[:, 0:1]; sq_c = st[:, 1:2]; m_c = st[:, 2:3]; m2_c = st[:, 3:4]
    var_c = st[:, 4:5]; sd_c = st[:, 5:6]; rstd_c = st[:, 6:7]; bm_c = st[:, 7:8]
    mx_c = st[:, 8:9]; mxc_c = st[:, 9:10]; r_c = st[:, 10:11]; s1q_c = st[:, 11:12]
    c1g_p = [st[:, 12:13], st[:, 20:21]]
    c1u_p = [st[:, 13:14], st[:, 21:22]]
    mh4 = st[:, 24:28]
    mh_c = st[:, 15:16]; mhc_c = st[:, 16:17]; rh_c = st[:, 17:18]; s2q_c = st[:, 18:19]

    ln1_sb = nqh_sb[:, 0:D]
    ln1T_sb = nqt_sb[:, 0:D]

    s1ctx = contextlib.ExitStack()
    wgq = s1ctx.enter_context(nc.sbuf_tensor("wgq", [128, HC * INNER], BF16)).ap()
    wuq = s1ctx.enter_context(nc.sbuf_tensor("wuq", [128, HC * INNER], BF16)).ap()

    s2ctx = contextlib.ExitStack()
    wst2 = s2ctx.enter_context(nc.sbuf_tensor("wstage0", [128, 2048], F32)).ap()
    x3 = [x2_st[:, 0:D], x2_st[:, D:2 * D], wst2[:, 0:D]]
    wdq_st = s2ctx.enter_context(nc.sbuf_tensor("wdq_st", [128, 1024], BF16)).ap()
    wq_sb = s2ctx.enter_context(nc.sbuf_tensor("wq_sb", [128, D], BF16)).ap()
    wk_sb = s2ctx.enter_context(nc.sbuf_tensor("wk_sb", [128, D], BF16)).ap()
    wv_sb = s2ctx.enter_context(nc.sbuf_tensor("wv_sb", [128, D], BF16)).ap()
    pt_row = [s2ctx.enter_context(nc.sbuf_tensor(f"pt{i}", [128, 1024], BF16)).ap() for i in range(3)]
    o_sb = pt_row[2][:, 0:GRP * 128]
    ot_sb = s2ctx.enter_context(nc.sbuf_tensor("ot_sb", [128, 512], BF16)).ap()
    rq_sb = s2ctx.enter_context(nc.sbuf_tensor("rq_sb", [128, GRP], F32)).ap()

    pb = [stack.enter_context(nc.psum_tensor(f"pb{i}", [128, 1024], F32)).ap() for i in range(4)]

    sc.op('q', lambda e: e.dma_start(out=ident, in_=ident_d), inc='DQC', n=16)
    sc.op('q', lambda e: e.dma_start(out=mask, in_=mask_d), inc='DQC', n=16)
    dq_const = ('DQC', sc.cnt['DQC'])
    sc.op('v', lambda e: e.memset(ones_c, 1.0))
    sc.op('v', lambda e: e.memset(ones_r, 1.0))
    sc.op('v', lambda e: e.memset(eps_c, EPS))
    sc.op('v', lambda e: e.memset(smsc_c, SM_SCALE))
    sc.op('v', lambda e: e.memset(c127_c, 127.0))
    sc.op('v', lambda e: e.memset(pi_c, 2.0 ** -24))
    sc.op('v', lambda e: e.drain())
    sc.op('v', lambda e: e.tensor_scalar(pi_c, pi_c, 1.0, None, ALU.add))
    sc.op('v', lambda e: e.drain())
    sc.op('v', lambda e: e.tensor_scalar(pi_c, pi_c, 1.0, None, ALU.subtract))
    sc.op('v', lambda e: e.drain())
    sc.op('v', lambda e: e.tensor_scalar(pi_c, pi_c, 2.0 ** 24, None, ALU.mult))
    sc.op('v', lambda e: e.drain())
    sc.op('v', lambda e: e.tensor_scalar(cb_c, pi_c, C52MC32, C32, ALU.mult, ALU.add))
    sc.op('v', lambda e: e.drain())
    v_konst = sc.op('v', lambda e: e.tensor_scalar(ng_c, cb_c, -1.0, None, ALU.mult), inc='V')

    sc.op('g', lambda e: e.dma_start(
        out=wq_sb.rearrange("p (c m) -> p c m", c=HC),
        in_=wq_d.rearrange("(c p) m -> p c m", p=128)), inc='DGA', n=16)
    sc.op('g', lambda e: e.dma_start(
        out=wk_sb.rearrange("p (c m) -> p c m", c=HC),
        in_=wk_d.rearrange("(c p) m -> p c m", p=128)), inc='DGA', n=16)
    sc.op('g', lambda e: e.dma_start(
        out=wv_sb.rearrange("p (c m) -> p c m", c=HC),
        in_=wv_d.rearrange("(c p) m -> p c m", p=128)), inc='DGA', n=16)
    sc.op('g', lambda e: e.dma_start(out=wo_sb, in_=wo_d), inc='DGA', n=16)
    dg_attnw = ('DGA', sc.cnt['DGA'])

    dg_x = [None] * NT
    s_lastx = [0] * NT

    def emit_xload_p1(i):
        par = i % 3
        w = []
        if i >= 3:
            w += [('S', s_lastx[i - 3]), ('V', v_s1[i - 3])]
        v = sc.op('g', (lambda i=i, par=par: (lambda e: e.dma_start(
            out=x3[par], in_=x_d[i * 128:(i + 1) * 128, :])))(), waits=w, inc='DGX%d' % par, n=16)
        dg_x[i] = ('DGX%d' % par, v)

    wpieces = []
    for wi, wd_ap in ((0, wgt_d), (1, wut_d)):
        for c in range(HC):
            for hh in range(4):
                wpieces.append((wi, wd_ap[c * 128:(c + 1) * 128, hh * 1024:(hh + 1) * 1024], 1024))
    for p in range(OC):
        wpieces.append((2, wdt_d[p * 128:(p + 1) * 128, :], 1024))

    dq_p1w = [None] * len(wpieces)
    s_abs = [None] * len(wpieces)
    w1stages = [wst2[:, 1024:2048],
                nqh_sb.bitcast(F32)[:, 512:1536], nqt_sb.bitcast(F32)[:, 512:1536]]
    w2stages = [x2_st, wst2,
                nqh_sb.bitcast(F32)[:, 0:2048], nqt_sb.bitcast(F32)[:, 0:2048]]
    w2sems = ['DQW0', 'DQW1', 'DQW2', 'DQW3']

    def emit_w1_piece(idx):
        wi, src, wid = wpieces[idx]
        par = idx % 3
        dstap = w1stages[par][:, 0:wid]
        w = [s_abs[idx - 3]] if idx >= 3 else []
        v = sc.op('q', (lambda src=src, dstap=dstap: (lambda e: e.dma_start(out=dstap, in_=src)))(),
                  waits=w, inc='DQW%d' % par, n=16)
        dq_p1w[idx] = ('DQW%d' % par, v)
        col = wsums[:, idx:idx + 1]
        if idx % 2 == 0:
            s_abs[idx] = ('S', sc.op('s', (lambda dstap=dstap, col=col: (lambda e: e.activation(
                dstap, dstap, AF.Abs, accum_out=col)))(),
                waits=[dq_p1w[idx]], inc='S'))
        else:
            s_abs[idx] = ('V', sc.op('v', (lambda dstap=dstap, col=col: (lambda e: e.reduce_sum(
                col, dstap, axis=AX, apply_absolute_value=True)))(),
                waits=[dq_p1w[idx]], inc='V'))

    psA = [pb[0].bitcast(BF16)[:, 0:1024], pb[0].bitcast(BF16)[:, 1024:2048]]
    Qps = [pb[1][:, 0:128], pb[2][:, 512:640]]
    Kps = [pb[1][:, 512:640], pb[3][:, 0:128]]
    Vps = [pb[2][:, 0:128], pb[3][:, 512:640]]
    v_evA = [0, 0]
    v_evQ = [0, 0]; s_evK = [0, 0]; v_evV = [0, 0]

    p1c = [dict(s1=st[:, 8 * k + 0:8 * k + 1], sq=st[:, 8 * k + 1:8 * k + 2],
                m=st[:, 8 * k + 2:8 * k + 3], m2=st[:, 8 * k + 3:8 * k + 4],
                var=st[:, 8 * k + 4:8 * k + 5], sd=st[:, 8 * k + 5:8 * k + 6],
                rstd=st[:, 8 * k + 6:8 * k + 7], bm=st[:, 8 * k + 7:8 * k + 8])
           for k in range(3)]
    s_sqa = [0] * NT
    v_s1 = [0] * NT
    g_m = [0] * NT
    v_m2 = [0] * NT
    g_var = [0] * NT
    s_sd = [0] * NT
    v_rstd = [0] * NT
    g_bm = [0] * NT

    def emit_p1_A(i):
        par = i % 3
        xs = x3[par]
        c = p1c[par]
        vw = [dg_x[i]]
        if i >= 3:
            vw.append(('G', g_m[i - 3]))
        v_s1[i] = sc.op('v', (lambda xs=xs, c=c: (lambda e: e.reduce_sum(c['s1'], xs, axis=AX)))(),
                        waits=vw, inc='V')
        sw = [dg_x[i]]
        if i >= 1:
            sw.append(('S', s_sqa[i - 1]))
        if i >= 3:
            sw.append(('G', g_var[i - 3]))
        s_sqa[i] = sc.op('s', (lambda xs=xs, c=c: (lambda e: e.activation(
            ln_scr, xs, AF.Square, accum_out=c['sq'])))(), waits=sw, inc='S')

    def emit_p1_B1(i):
        c = p1c[i % 3]
        gw = [('V', v_s1[i])]
        if i >= 3:
            gw.append(('V', v_m2[i - 3]))
        g_m[i] = sc.op('g', (lambda c=c: (lambda e: e.tensor_scalar(
            c['m'], c['s1'], INV_D, None, ALU.mult)))(), waits=gw, inc='G')
        vw = [('G', g_m[i])]
        if i >= 3:
            vw.append(('G', g_var[i - 3]))
        v_m2[i] = sc.op('v', (lambda c=c: (lambda e: e.tensor_scalar(
            c['m2'], c['s1'], INV_D, c['m'], ALU.mult, ALU.mult)))(), waits=vw, inc='V')
        gw = [('V', v_m2[i]), ('S', s_sqa[i])]
        if i >= 3:
            gw.append(('S', s_sd[i - 3]))
        g_var[i] = sc.op('g', (lambda c=c: (lambda e: e.tensor_scalar(
            c['var'], c['sq'], INV_D, c['m2'], ALU.mult, ALU.subtract)))(), waits=gw, inc='G')

    def emit_p1_B2(i):
        par = i % 3
        xs = x3[par]
        c = p1c[par]
        sw = [('G', g_var[i])]
        if i >= 3:
            sw.append(('V', v_rstd[i - 3]))
        s_sd[i] = sc.op('s', (lambda c=c: (lambda e: e.activation(
            c['sd'], c['var'], AF.Sqrt, bias=eps_c)))(), waits=sw, inc='S')
        vw = [('S', s_sd[i])]
        if i >= 3:
            vw += [('G', g_bm[i - 3]), ('S', s_lastx[i - 3])]
        v_rstd[i] = sc.op('v', (lambda c=c: (lambda e: e.reciprocal(c['rstd'], c['sd'])))(),
                          waits=vw, inc='V')
        gw = [('V', v_rstd[i])]
        if i >= 3:
            gw.append(('S', s_lastx[i - 3]))
        g_bm[i] = sc.op('g', (lambda c=c: (lambda e: e.tensor_scalar(
            c['bm'], c['m'], -1.0, c['rstd'], ALU.mult, ALU.mult)))(), waits=gw, inc='G')
        s_ln1 = sc.op('s', (lambda xs=xs, c=c: (lambda e: e.activation(
            ln1_sb, xs, AF.Identity, scale=c['rstd'], bias=c['bm'])))(),
            waits=[('G', g_bm[i])], inc='S')
        s_lastx[i] = s_ln1
        if i + 3 < NT:
            emit_xload_p1(i + 3)
        return s_ln1

    def emit_p1_geo(i, s_ln1):
        par = i % 2
        tw = [('S', s_ln1), ('V', v_evA[par]), dq_const, ('V', v_scale_done[0])]
        tt = 0
        for c in range(HC):
            tt = sc.op('t', (lambda c=c, par=par: (lambda e: e.transpose(
                psA[par][:, c * 128:(c + 1) * 128], ln1_sb[:, c * 128:(c + 1) * 128], ident)))(),
                waits=tw if c == 0 else (), inc='T')
        v_evA[par] = sc.op('v', (lambda par=par: (lambda e: e.tensor_copy(ln1T_sb, psA[par])))(),
                           waits=[('T', tt)], inc='V')
        tw = [('V', v_evA[par]), dg_attnw,
              ('V', v_evQ[par]), ('S', s_evK[par]), ('V', v_evV[par])]
        tt = 0
        for c in range(HC):
            tt = sc.op('t', (lambda c=c, par=par: (lambda e: e.matmul(
                Qps[par], wq_sb[:, c * 128:(c + 1) * 128], ln1T_sb[:, c * 128:(c + 1) * 128],
                start=(c == 0), stop=(c == HC - 1))))(), waits=tw if c == 0 else (), inc='T')
        for c in range(HC):
            tt = sc.op('t', (lambda c=c, par=par: (lambda e: e.matmul(
                Kps[par], wk_sb[:, c * 128:(c + 1) * 128], ln1T_sb[:, c * 128:(c + 1) * 128],
                start=(c == 0), stop=(c == HC - 1))))(), inc='T')
        for c in range(HC):
            tt = sc.op('t', (lambda c=c, par=par: (lambda e: e.matmul(
                Vps[par], ln1T_sb[:, c * 128:(c + 1) * 128], wv_sb[:, c * 128:(c + 1) * 128],
                start=(c == 0), stop=(c == HC - 1))))(), inc='T')
        v_evQ[par] = sc.op('v', (lambda i=i, par=par: (lambda e: e.tensor_copy(
            qres[:, i * 128:(i + 1) * 128], Qps[par])))(), waits=[('T', tt)], inc='V')
        s_evK[par] = sc.op('s', (lambda i=i, par=par: (lambda e: e.activation(
            kres[:, i * 128:(i + 1) * 128], Kps[par], AF.Identity)))(), waits=[('T', tt)], inc='S')
        v_evV[par] = sc.op('v', (lambda i=i, par=par: (lambda e: e.tensor_copy(
            vres[:, i * 129:i * 129 + 128], Vps[par])))(), inc='V')

    v_scale_done = [0]
    emit_xload_p1(0)
    emit_xload_p1(1)
    emit_xload_p1(2)
    emit_p1_A(0)
    emit_p1_A(1)
    emit_p1_B1(0)
    w1_next = 0
    for i in range(NT):
        s_ln1x = emit_p1_B2(i)
        if i + 1 < NT:
            emit_p1_B1(i + 1)
        if i + 2 < NT:
            emit_p1_A(i + 2)
        emit_p1_geo(i, s_ln1x)
        for _ in range(3):
            if w1_next < len(wpieces):
                emit_w1_piece(w1_next)
                w1_next += 1
    while w1_next < len(wpieces):
        emit_w1_piece(w1_next)
        w1_next += 1
    v_vones = sc.op('v', lambda e: e.memset(vres[:, 128::129], 1.0), inc='V')
    p1_V = sc.cnt['V']; p1_S = sc.cnt['S']; p1_T = sc.cnt['T']

    def emit_scales():
        vsd = 0
        wcols = [wsums[:, 0:32], wsums[:, 32:64], wsums[:, 64:96]]
        wlast = [s_abs[31], s_abs[63], s_abs[95]]
        for wi in range(3):
            cols = wcols[wi]
            tot_col = st[:, 28:29]
            vv = sc.op('v', (lambda cols=cols, tot_col=tot_col: (lambda e: e.reduce_sum(
                tot_col, cols, axis=AX)))(), waits=[wlast[wi], s_abs[30 + 32 * wi], ('S', p1_S)], inc='V')
            tt = sc.op('t', (lambda tot_col=tot_col: (lambda e: e.matmul(
                pb[0][0:1, 0:1], tot_col, ones_c, start=True, stop=True)))(),
                waits=[('V', vv), ('V', v_evA[0]), ('V', v_evA[1])], inc='T')
            vv = sc.op('v', lambda e: e.tensor_scalar(tot_sb[:, 0:1], pb[0][0:1, 0:1], INV_WN, EPS, ALU.mult, ALU.max),
                       waits=[('T', tt)], inc='V')
            sc.op('v', lambda e: e.drain())
            sc.op('v', lambda e: e.reciprocal(tot_sb[:, 1:2], tot_sb[:, 0:1]))
            sc.op('v', lambda e: e.drain())
            vv = sc.op('v', lambda e: e.tensor_scalar(tot_sb[:, 2:3], tot_sb[:, 0:1], 1.0 / 127.0, None, ALU.mult), inc='V')
            tt = sc.op('t', lambda e: e.matmul(pb[1][0:128, 0:1], ones_r, tot_sb[:, 1:2], start=True, stop=True),
                       waits=[('V', vv), ('V', v_evQ[0]), ('S', s_evK[0])], inc='T')
            tt = sc.op('t', lambda e: e.matmul(pb[1][0:128, 512:513], ones_r, tot_sb[:, 2:3], start=True, stop=True),
                       inc='T')
            s_col = wsc[:, 2 * wi:2 * wi + 1]
            d_col = wsc[:, 2 * wi + 1:2 * wi + 2]
            vv = sc.op('v', (lambda s_col=s_col: (lambda e: e.tensor_copy(s_col, pb[1][0:128, 0:1])))(),
                       waits=[('T', tt)], inc='V')
            vsd = sc.op('v', (lambda d_col=d_col: (lambda e: e.tensor_copy(d_col, pb[1][0:128, 512:513])))(),
                        inc='V')
        v_scale_done[0] = vsd

    swg_c = wsc[:, 0:1]; wgdq_c = wsc[:, 1:2]
    swu_c = wsc[:, 2:3]; wudq_c = wsc[:, 3:4]
    swd_c = wsc[:, 4:5]; wddq_c = wsc[:, 5:6]

    p2pieces = []
    for wi, wd_ap in ((0, wgt_d), (1, wut_d)):
        for c in range(HC):
            for hh in range(2):
                dst = (wgq if wi == 0 else wuq)[:, c * INNER + hh * 2048: c * INNER + (hh + 1) * 2048]
                p2pieces.append((wi, wd_ap[c * 128:(c + 1) * 128, hh * 2048:(hh + 1) * 2048], dst, 2048))
    for p in range(OC):
        p2pieces.append((2, wdt_d[p * 128:(p + 1) * 128, :], p, 1024))

    s_qdone = [0] * len(p2pieces)
    ds_wdq = [0] * len(p2pieces)

    def emit_w2_piece(idx):
        wi, src, dst, wid = p2pieces[idx]
        s_col = wsc[:, 2 * wi:2 * wi + 1]
        par = idx % 4
        dstap = w2stages[par][:, 0:wid]
        if idx < 4:
            w = [('S', p1_S), ('V', p1_V), ('T', p1_T)]
        else:
            w = [('S', s_qdone[idx - 4])]
        v = sc.op('q', (lambda src=src, dstap=dstap: (lambda e: e.dma_start(out=dstap, in_=src)))(),
                  waits=w, inc=w2sems[par], n=16)
        vw = [(w2sems[par], v), ('V', v_konst), ('V', v_scale_done[0])]
        v_q = sc.op('v', (lambda dstap=dstap, s_col=s_col: (lambda e: e.tensor_scalar(
            dstap, dstap, s_col, CLIP_B, ALU.mult, ALU.min)))(), waits=vw, inc='V')
        sc.op('v', lambda e: e.drain())
        v_q = sc.op('v', (lambda dstap=dstap: (lambda e: e.tensor_scalar(
            dstap, dstap, -CLIP_B, cb_c, ALU.max, ALU.add)))(), inc='V')
        if wi < 2:
            s_qdone[idx] = sc.op('s', (lambda dstap=dstap, dst=dst: (lambda e: e.activation(
                dst, dstap, AF.Identity, bias=ng_c)))(), waits=[('V', v_q)], inc='S')
        else:
            pdn = dst
            wds = [('V', v_q)]
            if idx >= 33:
                wds.append(('DSW', ds_wdq[idx - 1]))
            s_qdone[idx] = sc.op('s', (lambda dstap=dstap: (lambda e: e.activation(
                wdq_st, dstap, AF.Identity, bias=ng_c)))(), waits=wds, inc='S')
            ds_wdq[idx] = sc.op('s', (lambda pdn=pdn: (lambda e: e.dma_start(
                out=wdqh_d[pdn], in_=wdq_st)))(),
                waits=[('S', s_qdone[idx])], inc='DSW', n=16)

    stv4 = [pb[0][:, 0:512], pb[0][:, 512:1024], pb[1][:, 0:512], pb[1][:, 512:1024]]
    oaug = [pb[2][:, 0:132], pb[2][:, 512:644], pb[3][:, 0:132], pb[3][:, 512:644]]
    wops = pb[1]
    NKP = 4

    s_exp_last = [0, 0, 0, 0]
    t_av_last = [0, 0, 0, 0]
    dso_last = [0]
    dso_g = [0] * (NT // GRP)
    t_attn_pt = [0]
    dg_xpa = [None] * NT
    v_x1add = [0] * NT
    ds_x1 = [None] * NT
    s_otev_last = 0
    s_oev_last = 0
    v_rec_last = 0

    v_stats = [0] * NT
    s_stats = [0] * NT

    x4 = [x2_st[:, 0:1024], x2_st[:, 1024:2048], wst2[:, 0:1024], wst2[:, 1024:2048]]
    x4sem = ['DGX0', 'DGX1', 'DGX2', 'DGX3']

    def emit_xload_pa(i):
        par = i % 4
        if i >= 4:
            w = [ds_x1[i - 4], ('V', v_stats[i - 4]), ('S', s_stats[i - 4])]
        else:
            w = [('S', s_qdone[-1] if s_qdone[-1] else 0), ds_wd_done]
        v = sc.op('g', (lambda i=i, par=par: (lambda e: e.dma_start(
            out=x4[par], in_=x_d[i * 128:(i + 1) * 128, :])))(), waits=w, inc=x4sem[par], n=16)
        dg_xpa[i] = (x4sem[par], v)

    def emit_pa1_group(g):
        nonlocal s_otev_last, s_oev_last, v_rec_last
        g0 = g * GRP
        nk = g0 + GRP
        t_av_pt = [0, 0, 0]
        pend = []

        def emit_av(rec):
            p0, pp0, pt0, sexp0, gm0 = rec
            tw = [('S', sexp0)]
            if gm0:
                tw.append(('G', gm0))
            first = True
            tt = 0
            for d in range(2):
                k = 2 * p0 + d
                q_lo = max(k, g0)
                off = d * 512 + (q_lo - g0) * 128
                for j in range(g0 + GRP - q_lo):
                    q = q_lo + j
                    tt = sc.op('t', (lambda q=q, k=k, pt0=pt0, g0=g0, j=j, off=off: (lambda e: e.matmul(
                        oaug[q - g0][:, 0:129],
                        pt_row[pt0][:, off + j * 128: off + (j + 1) * 128],
                        vres[:, k * 129: (k + 1) * 129],
                        start=(k == 0), stop=(k == q))))(), waits=tw if first else (), inc='T')
                    first = False
            t_av_pt[pt0] = tt

        for p in range(nk // 2):
            pp = p % 2
            ptx = p % 3
            pbst = pb[pp]
            tw0 = [('S', s_exp_last[pp]), ('V', p1_V), ('S', p1_S)]
            if g == 0 and p == 0:
                tw0 += [('V', v_vones), ('V', v_scale_done[0])]
            if p == 0:
                tw0 += [('S', s_oev_last), ('V', v_rec_last)]
            first = True
            tt = 0
            for d in range(2):
                k = 2 * p + d
                q_lo = max(k, g0)
                off = d * 512 + (q_lo - g0) * 128
                for j in range(g0 + GRP - q_lo):
                    q = q_lo + j
                    tt = sc.op('t', (lambda k=k, q=q, pbst=pbst, off=off, j=j: (lambda e: e.matmul(
                        pbst[:, off + j * 128: off + (j + 1) * 128],
                        kres[:, k * 128:(k + 1) * 128], qres[:, q * 128:(q + 1) * 128],
                        start=True, stop=True)))(), waits=tw0 if first else (), inc='T')
                    first = False
            sw = [('T', tt), ('T', t_av_pt[ptx])]
            if ptx == 2:
                sw.append(('DSO', dso_last[0]))
            if 2 * p >= g0:
                lo0 = (2 * p - g0) * 128
                lo1 = 512 + (2 * p + 1 - g0) * 128
                sc.op('s', (lambda pp=pp, ptx=ptx, lo0=lo0: (lambda e: e.activation(
                    pt_row[ptx][:, lo0:512], pb[pp][:, lo0:512],
                    AF.Exp, scale=smsc_c)))(), waits=sw, inc='S')
                s_exp_last[pp] = sc.op('s', (lambda pp=pp, ptx=ptx, lo1=lo1: (lambda e: e.activation(
                    pt_row[ptx][:, lo1:1024], pb[pp][:, lo1:1024],
                    AF.Exp, scale=smsc_c)))(), inc='S')
            else:
                s_exp_last[pp] = sc.op('s', (lambda pp=pp, ptx=ptx: (lambda e: e.activation(
                    pt_row[ptx][:, 0:1024], pb[pp][:, 0:1024],
                    AF.Exp, scale=smsc_c)))(), waits=sw, inc='S')
            gmask = 0
            for d in range(2):
                k = 2 * p + d
                if k >= g0:
                    gmask = sc.op('g', (lambda ptx=ptx, k=k, g0=g0, d=d: (lambda e: e.tensor_mul(
                        pt_row[ptx][:, d * 512 + (k - g0) * 128: d * 512 + (k - g0 + 1) * 128],
                        pt_row[ptx][:, d * 512 + (k - g0) * 128: d * 512 + (k - g0 + 1) * 128],
                        mask)))(), waits=[('S', s_exp_last[pp]), dq_const], inc='G')
            pend.append((p, pp, ptx, s_exp_last[pp], gmask))
            if len(pend) > 2:
                emit_av(pend.pop(0))
        while pend:
            emit_av(pend.pop(0))
        t_attn_done = sc.cnt['T']
        v_rec = 0
        for j in range(GRP):
            v_rec = sc.op('v', (lambda j=j: (lambda e: e.reciprocal(
                rq_sb[:, j:j + 1], oaug[j][:, 128:129])))(),
                waits=[('T', t_attn_done)] if j == 0 else (), inc='V')
        v_rec_last = v_rec
        s_oev = 0
        for j in range(GRP):
            s_oev = sc.op('s', (lambda j=j: (lambda e: e.activation(
                o_sb[:, j * 128:(j + 1) * 128], oaug[j][:, 0:128], AF.Identity,
                scale=rq_sb[:, j:j + 1])))(), waits=[('V', v_rec)], inc='S')
        s_oev_last = s_oev

        dso_last[0] = sc.op('s', (lambda g=g: (lambda e: e.dma_start(
            out=oh_d[g], in_=o_sb)))(), waits=[('S', s_oev)], inc='DSO', n=16)
        dso_g[g] = dso_last[0]

    emit_scales()
    w2_next = 0
    for g in range(NT // GRP):
        emit_pa1_group(g)
        for _ in range(8):
            if w2_next < len(p2pieces):
                emit_w2_piece(w2_next)
                w2_next += 1
    while w2_next < len(p2pieces):
        emit_w2_piece(w2_next)
        w2_next += 1
    s_wq_done = s_qdone[31]
    ds_wd_done = ('DSW', ds_wdq[-1])

    t_attn_pt[0] = sc.cnt['T']
    emit_xload_pa(0)
    emit_xload_pa(1)
    emit_xload_pa(2)
    emit_xload_pa(3)
    t_attn_end = sc.cnt['T']; s_attn_end = sc.cnt['S']
    g_attn_end = sc.cnt['G']; v_attn_end = sc.cnt['V']
    o_in = [pt_row[0][:, 0:512], pt_row[1][:, 0:512]]
    s1_p = [s1_c, wsums[:, 0:1]]
    sq_p = [sq_c, wsums[:, 1:2]]
    m2_p = [m2_c, wsums[:, 2:3]]
    sqscr = [ln_scr, vres[:, 0:D]]
    dqo = [None] * (NT // GRP)
    t_ot = [0] * (NT // GRP)
    v_otev_g = [0] * (NT // GRP)
    wo_free = {1: 0, 2: 0, 3: 0}

    def emit_oload(g):
        par = g % 2
        w = [('DSO', dso_g[g]), ('T', t_attn_end), ('S', s_attn_end), ('G', g_attn_end)]
        if g >= 2:
            w.append(('T', t_ot[g - 2]))
        v = sc.op('q', (lambda g=g, par=par: (lambda e: e.dma_start(
            out=o_in[par], in_=oh_d[g])))(), waits=w, inc='DQO%d' % par, n=16)
        dqo[g] = ('DQO%d' % par, v)

    emit_oload(0)
    emit_oload(1)
    for g in range(NT // GRP):
        g0 = g * GRP
        par = g % 2
        tw = [dqo[g], ('V', v_otev_g[g - 1] if g >= 1 else 0)]
        if g == 0:
            tw += [('S', s_attn_end), ('V', v_attn_end)]
        tt = 0
        for j in range(GRP):
            tt = sc.op('t', (lambda j=j, par=par: (lambda e: e.transpose(
                pb[0].bitcast(BF16)[:, j * 128:(j + 1) * 128],
                o_in[par][:, j * 128:(j + 1) * 128], ident)))(),
                waits=tw if j == 0 else (), inc='T')
        t_ot[g] = tt
        v_otev_g[g] = sc.op('v', lambda e: e.tensor_copy(ot_sb, pb[0].bitcast(BF16)[:, 0:512]),
                            waits=[('T', tt)], inc='V')
        if g + 2 < NT // GRP:
            emit_oload(g + 2)
        wo_banks = [1, 2, 3, 1]

        def emit_stats(i):
            xb = x4[i % 4]
            jp = i % 2
            sc.op('v', lambda e: e.drain())
            sc.op('v', (lambda xb=xb, jp=jp: (lambda e: e.reduce_sum(
                s1_p[jp], xb, axis=AX)))(), waits=[('V', v_x1add[i])], inc='V')
            sw = [('V', v_x1add[i])]
            if i >= 2:
                sw.append(('S', s_stats[i - 2]))
            s_stats[i] = sc.op('s', (lambda xb=xb, jp=jp: (lambda e: e.activation(
                sqscr[jp], xb, AF.Square, accum_out=sq_p[jp])))(),
                waits=sw, inc='S')
            sc.op('v', lambda e: e.drain())
            sc.op('v', (lambda i=i, jp=jp: (lambda e: e.tensor_scalar(
                m_all[:, i:i + 1], s1_p[jp], INV_D, None, ALU.mult)))(), inc='V')
            sc.op('v', lambda e: e.drain())
            sc.op('v', (lambda i=i, jp=jp: (lambda e: e.tensor_mul(
                m2_p[jp], m_all[:, i:i + 1], m_all[:, i:i + 1])))(), inc='V')
            sc.op('v', lambda e: e.drain())
            v_stats[i] = sc.op('v', (lambda i=i, jp=jp: (lambda e: e.tensor_scalar(
                var_all[:, i:i + 1], sq_p[jp], INV_D, m2_p[jp], ALU.mult, ALU.subtract)))(),
                waits=[('S', s_stats[i])], inc='V')
            if i + 4 < NT:
                emit_xload_pa(i + 4)

        for j in range(GRP):
            i = g0 + j
            wb = wo_banks[j]
            wob = pb[wb]
            tw = [('V', v_otev_g[g]), ('V', wo_free[wb])]
            if g == 0 and j == 0:
                tw += [('S', s_attn_end), ('V', v_attn_end)]
            tt = sc.op('t', (lambda j=j, wob=wob: (lambda e: e.matmul(
                wob[:, 0:512], ot_sb[:, j * 128:(j + 1) * 128], wo_sb[:, 0:512],
                start=True, stop=True)))(), waits=tw, inc='T')
            tt = sc.op('t', (lambda j=j, wob=wob: (lambda e: e.matmul(
                wob[:, 512:1024], ot_sb[:, j * 128:(j + 1) * 128], wo_sb[:, 512:1024],
                start=True, stop=True)))(), inc='T')
            pari = i % 4
            xb = x4[pari]
            vw = [('T', tt), dg_xpa[i]]
            vv = sc.op('v', (lambda xb=xb, wob=wob: (lambda e: e.tensor_add(
                xb[:, 0:512], wob[:, 0:512], xb[:, 0:512])))(), waits=vw, inc='V')
            vv = sc.op('v', (lambda xb=xb, wob=wob: (lambda e: e.tensor_add(
                xb[:, 512:1024], wob[:, 512:1024], xb[:, 512:1024])))(), inc='V')
            v_x1add[i] = vv
            wo_free[wb] = vv
            v = sc.op('s', (lambda i=i, xb=xb: (lambda e: e.dma_start(
                out=x1h_d[i * 128:(i + 1) * 128, :], in_=xb)))(),
                waits=[('V', vv)], inc='DSX%d' % pari, n=16)
            ds_x1[i] = ('DSX%d' % pari, v)
            if j >= 1:
                emit_stats(i - 1)
        emit_stats(g0 + GRP - 1)

    ss = sc.op('s', lambda e: e.activation(var_all, var_all, AF.Sqrt, bias=eps_c),
               waits=[('V', v_stats[NT - 1])], inc='S')
    vv = sc.op('v', lambda e: e.reciprocal(rstd_all, var_all), waits=[('S', ss)], inc='V')
    sc.op('v', lambda e: e.drain())
    vv = sc.op('v', lambda e: e.tensor_mul(bm_all, m_all, rstd_all), inc='V')
    sc.op('v', lambda e: e.drain())
    v_lnstats = sc.op('v', lambda e: e.tensor_scalar(bm_all, bm_all, -1.0, None, ALU.mult), inc='V')
    pa1_V = sc.cnt['V']; pa1_S = sc.cnt['S']

    s2ctx.close()
    s3b = contextlib.ExitStack()
    h_sb = s3b.enter_context(nc.sbuf_tensor("h_sb", [128, INNER], F32)).ap()
    g_s = kres.bitcast(F32)[:, 0:1024]
    xq2 = qres[:, 2048:3072]
    xqTb = [qres[:, 0:1024], qres[:, 1024:2048]]

    dq_x1b = [None] * NT
    s_ln2t = [0] * NT
    s_xq = [0] * NT
    v_rnd1 = [0] * NT
    t_xqA = [0] * NT
    t_E = [0] * NT
    s_silu_q = {}
    v_stt_q = {}
    s_nqh_t = [0] * NT
    s_nqh_h0 = [0] * NT
    v_nqev = [0] * NT
    ds_nq = [None] * NT

    def emit_x1load(i):
        par = i % 2
        w = [ds_x1[i]]
        if i >= 2:
            w.append(('S', s_ln2t[i - 2]))
        else:
            w += [('V', pa1_V), ('S', pa1_S), ds_x1[NT - 4], ds_x1[NT - 3]]
        v = sc.op('q', (lambda i=i, par=par: (lambda e: e.dma_start(
            out=x_st[par], in_=x1h_d[i * 128:(i + 1) * 128, :])))(), waits=w, inc='DQX%d' % par, n=16)
        dq_x1b[i] = ('DQX%d' % par, v)

    def chain_blk1(i):
        par = i % 2
        sw = [dq_x1b[i], ('S', pa1_S)]
        if i >= 1:
            sw.append(('V', v_rnd1[i - 1]))
        s_ln2t[i] = sc.op('s', (lambda par=par, i=i: (lambda e: e.activation(
            ln_scr, x_st[par], AF.Identity, scale=rstd_all[:, i:i + 1],
            bias=bm_all[:, i:i + 1])))(), waits=sw, inc='S')

    def chain_blk2(i):
        sc.op('v', lambda e: e.drain())
        vv = sc.op('v', lambda e: e.reduce_max(mx_c, ln_scr, axis=AX, apply_absolute_value=True),
                   waits=[('S', s_ln2t[i]), ('V', v_lnstats)], inc='V')
        sc.op('v', lambda e: e.drain())
        vv = sc.op('v', lambda e: e.tensor_scalar(mxc_c, mx_c, EPS, None, ALU.max), inc='V')
        sc.op('v', lambda e: e.drain())
        vv = sc.op('v', lambda e: e.reciprocal(r_c, mxc_c), inc='V')

    def chain_blk3(i):
        sc.op('v', lambda e: e.drain())
        sc.op('v', lambda e: e.tensor_scalar(s1q_c, r_c, 127.0, None, ALU.mult), inc='V')
        sc.op('v', (lambda i=i: (lambda e: e.tensor_mul(c1g_p[i % 2], mxc_c, wgdq_c)))(), inc='V')
        sc.op('v', (lambda i=i: (lambda e: e.tensor_mul(c1u_p[i % 2], mxc_c, wudq_c)))(), inc='V')
        sc.op('v', lambda e: e.drain())
        vv = sc.op('v', lambda e: e.tensor_scalar(ln_scr, ln_scr, s1q_c, cb_c, ALU.mult, ALU.add), inc='V')
        v_rnd1[i] = vv
        sw = [('V', vv), ('S', s_wq_done)]
        if i >= 1:
            sw.append(('T', t_xqA[i - 1]))
        else:
            sw += [ds_x1[NT - 2], ds_x1[NT - 1], ds_wd_done]
        s_xq[i] = sc.op('s', lambda e: e.activation(xq2, ln_scr, AF.Identity, bias=ng_c),
                        waits=sw, inc='S')
        if i + 2 < NT:
            emit_x1load(i + 2)

    emit_x1load(0)
    emit_x1load(1)
    chain_blk1(0)
    chain_blk2(0)
    chain_blk3(0)

    def emit_ychain(i):
        sc.op('v', lambda e: e.drain())
        sc.op('v', lambda e: e.tensor_max(mh4[:, 0:2], mh4[:, 0:2], mh4[:, 2:4]), inc='V')
        sc.op('v', lambda e: e.drain())
        sc.op('v', lambda e: e.tensor_max(mh_c, mh4[:, 0:1], mh4[:, 1:2]), inc='V')
        sc.op('v', lambda e: e.drain())
        sc.op('v', lambda e: e.tensor_scalar(mhc_c, mh_c, EPS, None, ALU.max), inc='V')
        sc.op('v', lambda e: e.drain())
        sc.op('v', lambda e: e.reciprocal(rh_c, mhc_c), inc='V')
        sc.op('v', lambda e: e.drain())
        sc.op('v', lambda e: e.tensor_scalar(s2q_c, rh_c, 127.0, None, ALU.mult), inc='V')
        sc.op('v', (lambda i=i: (lambda e: e.tensor_mul(c2_all[:, i:i + 1], mhc_c, wddq_c)))(), inc='V')
        sc.op('v', lambda e: e.drain())
        vv = sc.op('v', (lambda i=i: (lambda e: e.tensor_scalar(
            h_sb[:, 0:2048], h_sb[:, 0:2048], s2q_c, cb_c, ALU.mult, ALU.add)))(),
            waits=[('V', v_stt_q[(i, 3)])], inc='V')
        s_nqh_h0[i] = sc.op('s', lambda e: e.activation(
            nqh_sb[:, 0:2048], h_sb[:, 0:2048], AF.Identity, bias=ng_c),
            waits=[('V', vv)], inc='S')
        vv = sc.op('v', (lambda i=i: (lambda e: e.tensor_scalar(
            h_sb[:, 2048:4096], h_sb[:, 2048:4096], s2q_c, cb_c, ALU.mult, ALU.add)))(), inc='V')
        s_nqh_t[i] = sc.op('s', lambda e: e.activation(
            nqh_sb[:, 2048:4096], h_sb[:, 2048:4096], AF.Identity, bias=ng_c),
            waits=[('V', vv)], inc='S')

    pair_free = {0: (0, v_x1add[NT - 1]), 1: (0, v_x1add[NT - 1])}
    v_xqt_arr = [0] * NT
    gbank = {0: pb[0], 1: pb[2]}
    ubank = {0: pb[1], 1: pb[3]}

    def emit_Ablock(j):
        apair = j % 2
        aps = pb[0] if apair == 0 else pb[2]
        tw = [('S', s_xq[j]), dq_const,
              ('S', pair_free[apair][0]), ('V', pair_free[apair][1])]
        if j == 0:
            tw += [('V', v_otev_g[NT // GRP - 1]), ('V', v_x1add[NT - 1])]
        tt = 0
        for c in range(HC):
            tt = sc.op('t', (lambda c=c, aps=aps: (lambda e: e.transpose(
                aps.bitcast(BF16)[:, c * 128:(c + 1) * 128],
                xq2[:, c * 128:(c + 1) * 128], ident)))(),
                waits=tw if c == 0 else (), inc='T')
        t_xqA[j] = tt
        vw = [('T', tt)]
        if j < 2:
            vw += [ds_x1[NT - 2], ds_x1[NT - 1], ds_wd_done]
        v_xqt_arr[j] = sc.op('v', (lambda j=j, aps=aps: (lambda e: e.tensor_copy(
            xqTb[j % 2], aps.bitcast(BF16)[:, 0:1024])))(), waits=vw, inc='V')

    emit_Ablock(0)
    for i in range(NT):
        par = i % 2
        qpair = [0, 1, 1, 0] if i % 2 == 0 else [1, 0, 0, 1]
        fpair = 0 if i % 2 == 0 else 1
        v_xqt = v_xqt_arr[i]
        if i >= 1:
            emit_ychain(i - 1)

        for qtr in range(4):
            pr = qpair[qtr]
            gb = gbank[pr]
            ub = ubank[pr]
            tw = [('S', pair_free[pr][0]), ('V', pair_free[pr][1])]
            if qtr == 0:
                tw += [('V', v_xqt), ('S', s_wq_done)]
                if i == 0:
                    tw += [('V', v_x1add[NT - 1])]
            first = True
            tt = 0
            for c in range(HC):
                if qtr == 3 and c == 5 and i + 1 < NT:
                    emit_Ablock(i + 1)
                for half in range(2):
                    o0 = qtr * 1024 + half * 512
                    tt = sc.op('t', (lambda c=c, o0=o0, gb=gb, half=half, par=par: (lambda e: e.matmul(
                        gb[:, half * 512:(half + 1) * 512],
                        xqTb[par][:, c * 128:(c + 1) * 128],
                        wgq[:, c * INNER + o0: c * INNER + o0 + 512],
                        start=(c == 0), stop=(c == HC - 1))))(),
                        waits=tw if first else (), inc='T')
                    first = False
                    tt = sc.op('t', (lambda c=c, o0=o0, ub=ub, half=half, par=par: (lambda e: e.matmul(
                        ub[:, half * 512:(half + 1) * 512],
                        xqTb[par][:, c * 128:(c + 1) * 128],
                        wuq[:, c * INNER + o0: c * INNER + o0 + 512],
                        start=(c == 0), stop=(c == HC - 1))))(), inc='T')
            if qtr == 3:
                t_E[i] = tt
            sw = [('T', tt)]
            if qtr >= 1:
                sw.append(('V', v_stt_q[(i, qtr - 1)]))
            s_silu_q[(i, qtr)] = sc.op('s', (lambda gb=gb, i=i: (lambda e: e.activation(
                g_s, gb, AF.Silu, scale=c1g_p[i % 2])))(), waits=sw, inc='S')
            v_stt_q[(i, qtr)] = sc.op('v', (lambda qtr=qtr, ub=ub, i=i: (lambda e: e.scalar_tensor_tensor(
                h_sb[:, qtr * 1024:(qtr + 1) * 1024], ub, c1u_p[i % 2], g_s, ALU.mult, ALU.mult)))(),
                waits=[('S', s_silu_q[(i, qtr)])], inc='V')
            pair_free[pr] = (s_silu_q[(i, qtr)], v_stt_q[(i, qtr)])
            sc.op('v', lambda e: e.drain())
            sc.op('v', (lambda qtr=qtr: (lambda e: e.reduce_max(
                mh4[:, qtr:qtr + 1], h_sb[:, qtr * 1024:(qtr + 1) * 1024],
                axis=AX, apply_absolute_value=True)))(), inc='V')

            if i + 1 < NT:
                if qtr == 0:
                    chain_blk1(i + 1)
                elif qtr == 2:
                    chain_blk3(i + 1)
            if qtr == 1 and i >= 1:
                fg = gbank[fpair]
                fu = ubank[fpair]
                tw2 = [('S', s_nqh_h0[i - 1]),
                       ('S', pair_free[fpair][0]), ('V', pair_free[fpair][1])]
                tt2 = 0
                for c in range(OC):
                    pbx = fg if c < 16 else fu
                    coff = (c % 16) * 128
                    cw = tw2 if c == 0 else ([('S', s_nqh_t[i - 1])] if c == 16 else ())
                    tt2 = sc.op('t', (lambda c=c, pbx=pbx, coff=coff: (lambda e: e.transpose(
                        pbx.bitcast(BF16)[:, coff:coff + 128],
                        nqh_sb[:, c * 128:(c + 1) * 128], ident)))(),
                        waits=cw, inc='T')
                sw2 = [('T', tt2)]
                if i >= 2:
                    sw2.append(ds_nq[i - 2])
                ss2 = sc.op('s', (lambda fg=fg: (lambda e: e.activation(
                    nqt_sb[:, 0:2048], fg.bitcast(BF16)[:, 0:2048], AF.Identity)))(),
                    waits=sw2, inc='S')
                ss2 = sc.op('s', (lambda fu=fu: (lambda e: e.activation(
                    nqt_sb[:, 2048:4096], fu.bitcast(BF16)[:, 0:2048], AF.Identity)))(),
                    inc='S')
                s_nqev = ss2
                v_nqev[i - 1] = 0
                pair_free[fpair] = (s_nqev, pair_free[fpair][1])
                v2 = sc.op('s', (lambda i=i: (lambda e: e.dma_start(
                    out=nqh_d[i - 1], in_=nqt_sb)))(), waits=[('S', s_nqev)], inc='DSN', n=16)
                ds_nq[i - 1] = ('DSN', v2)
            if qtr == 1 and i + 1 < NT:
                chain_blk2(i + 1)
    emit_ychain(NT - 1)

    tw2 = [('S', s_nqh_t[NT - 1]), ('S', pair_free[0][0]), ('V', pair_free[0][1])]
    tt2 = 0
    for c in range(OC):
        pbx = pb[0] if c < 16 else pb[1]
        coff = (c % 16) * 128
        tt2 = sc.op('t', (lambda c=c, pbx=pbx, coff=coff: (lambda e: e.transpose(
            pbx.bitcast(BF16)[:, coff:coff + 128],
            nqh_sb[:, c * 128:(c + 1) * 128], ident)))(),
            waits=tw2 if c == 0 else (), inc='T')
    sw2 = [('T', tt2), ds_nq[NT - 2]]
    ss2 = sc.op('s', lambda e: e.activation(nqt_sb[:, 0:2048], pb[0].bitcast(BF16)[:, 0:2048], AF.Identity),
                waits=sw2, inc='S')
    ss2 = sc.op('s', lambda e: e.activation(nqt_sb[:, 2048:4096], pb[1].bitcast(BF16)[:, 0:2048], AF.Identity),
                inc='S')
    v2 = sc.op('s', lambda e: e.dma_start(out=nqh_d[NT - 1], in_=nqt_sb),
               waits=[('S', ss2)], inc='DSN', n=16)
    ds_nq[NT - 1] = ('DSN', v2)
    pa2_V = sc.cnt['V']; pa2_S = sc.cnt['S']; pa2_G = sc.cnt['G']

    s3b.close()
    s1ctx.close()
    s4ctx = contextlib.ExitStack()
    wdq_sb = s4ctx.enter_context(nc.sbuf_tensor("wdq_sb", [128, OC * 1024], BF16)).ap()
    nqt_in = [s4ctx.enter_context(nc.sbuf_tensor(f"nqt_in{i}", [128, INNER], BF16)).ap() for i in range(2)]
    x1_in = [s4ctx.enter_context(nc.sbuf_tensor(f"x1_in{i}", [128, D], F32)).ap() for i in range(2)]
    out_sb = [s4ctx.enter_context(nc.sbuf_tensor(f"out_sb{i}", [128, D], F32)).ap() for i in range(2)]

    v = sc.op('q', lambda e: e.dma_start(
        out=wdq_sb.rearrange("p (c d) -> p c d", c=OC),
        in_=wdqh_d.rearrange("c p d -> p c d")),
        waits=[('T', t_E[NT - 1]), ds_wd_done], inc='DQWD', n=16)
    dq_wdq = ('DQWD', v)

    dq_nq = [None] * NT; dq_x1pb = [None] * NT
    v_out = [0] * NT
    ds_y = [None] * NT
    t_down = [0] * NT
    dps = [pb[0], pb[1]]

    def emit_pb_loads(i):
        par = i % 2
        w = [('T', t_E[NT - 1]) if i < 2 else ('S', pa2_S), ds_nq[i]]
        if i >= 2:
            w.append(('T', t_down[i - 2]))
        v = sc.op('q', (lambda i=i, par=par: (lambda e: e.dma_start(
            out=nqt_in[par], in_=nqh_d[i])))(), waits=w, inc='DQN%d' % par, n=16)
        dq_nq[i] = ('DQN%d' % par, v)
        w2 = [('T', t_E[NT - 1]) if i < 2 else ('S', pa2_S), ('V', pa2_V) if i >= 2 else ('T', 0), ds_x1[i]]
        if i >= 2:
            w2.append(('V', v_out[i - 2]))
        v = sc.op('q', (lambda i=i, par=par: (lambda e: e.dma_start(
            out=x1_in[par], in_=x1h_d[i * 128:(i + 1) * 128, :])))(), waits=w2, inc='DQX%d' % par, n=16)
        dq_x1pb[i] = ('DQX%d' % par, v)

    emit_pb_loads(0)
    emit_pb_loads(1)
    for i in range(NT):
        par = i % 2
        tw = [dq_nq[i], dq_wdq]
        if i < 2:
            tw += [('V', pa2_V), ('S', pa2_S)]
        else:
            tw.append(('V', v_out[i - 2]))
        tt = 0
        for c in range(OC):
            for half in range(2):
                tt = sc.op('t', (lambda c=c, half=half, par=par: (lambda e: e.matmul(
                    dps[par][:, half * 512:(half + 1) * 512],
                    nqt_in[par][:, c * 128:(c + 1) * 128],
                    wdq_sb[:, c * 1024 + half * 512: c * 1024 + (half + 1) * 512],
                    start=(c == 0), stop=(c == OC - 1))))(),
                    waits=tw if (c == 0 and half == 0) else (), inc='T')
        t_down[i] = tt
        vw = [('T', tt), dq_x1pb[i]]
        if i >= 2:
            vw.append(ds_y[i - 2])
        vv = sc.op('v', (lambda i=i, par=par: (lambda e: e.scalar_tensor_tensor(
            out_sb[par][:, 0:512], dps[par][:, 0:512], c2_all[:, i:i + 1],
            x1_in[par][:, 0:512], ALU.mult, ALU.add)))(), waits=vw, inc='V')
        vv = sc.op('v', (lambda i=i, par=par: (lambda e: e.scalar_tensor_tensor(
            out_sb[par][:, 512:1024], dps[par][:, 512:1024], c2_all[:, i:i + 1],
            x1_in[par][:, 512:1024], ALU.mult, ALU.add)))(), inc='V')
        v_out[i] = vv
        v = sc.op('s', (lambda i=i, par=par: (lambda e: e.dma_start(
            out=y_d[i * 128:(i + 1) * 128, :], in_=out_sb[par])))(),
            waits=[('V', vv)], inc='DSY%d' % par, n=16)
        ds_y[i] = ('DSY%d' % par, v)
        if i + 2 < NT:
            emit_pb_loads(i + 2)

    sc.op('q', lambda e: e.nop(), waits=[(nm, sc.cnt[nm]) for nm in sc.cnt if nm.startswith('D')])

    sems = {n: stack.enter_context(nc.semaphore(n)) for n in sc.cnt}
    blk = stack.enter_context(nc.Block())

    def emit(key):
        def body(eng):
            last = {}
            for fn, waits, inc, n in sc.st[key]:
                for s, v in waits:
                    if last.get(s, 0) < v:
                        eng.wait_ge(sems[s], v)
                        last[s] = v
                inst = fn(eng)
                if inc is not None:
                    inst.then_inc(sems[inc], n)
        return body

    blk.tensor(emit('t'))
    blk.vector(emit('v'))
    blk.scalar(emit('s'))
    blk.gpsimd(emit('g'))
    blk.sync(emit('q'))
    s4ctx.close()
    stack.close()
    return nc


def kernel(L0_attn_x, L0_attn_Wq, L0_attn_Wk, L0_attn_Wv, L0_attn_Wo,
           L0_ffn_w_gate, L0_ffn_w_up, L0_ffn_w_down):
    x = np.ascontiguousarray(np.asarray(L0_attn_x, dtype=np.float32))
    assert x.shape == (8, 4096, 1024)
    wq = np.ascontiguousarray(np.asarray(L0_attn_Wq, dtype=np.float32))
    wk = np.ascontiguousarray(np.asarray(L0_attn_Wk, dtype=np.float32))
    wv = np.ascontiguousarray(np.asarray(L0_attn_Wv, dtype=np.float32))
    wo = np.ascontiguousarray(np.asarray(L0_attn_Wo, dtype=np.float32))
    wgt = np.ascontiguousarray(np.asarray(L0_ffn_w_gate, dtype=np.float32).T)
    wut = np.ascontiguousarray(np.asarray(L0_ffn_w_up, dtype=np.float32).T)
    wdt = np.ascontiguousarray(np.asarray(L0_ffn_w_down, dtype=np.float32).T)
    ident = np.eye(128, dtype=ml_dtypes.bfloat16)
    mask = np.triu(np.ones((128, 128), dtype=ml_dtypes.bfloat16))

    nc = _build()
    in_maps = [{"x": x[i], "wq": wq, "wk": wk, "wv": wv, "wo": wo,
                "wgt": wgt, "wut": wut, "wdt": wdt,
                "ident": ident, "mask": mask} for i in range(8)]
    res = run_bass_kernel_spmd(nc, in_maps, core_ids=list(range(8)))
    out = np.stack([np.asarray(res.results[i]["y"]) for i in range(8)], axis=0)
    return out.astype(np.float32)
